# version 03
# baseline (speedup 1.0000x reference)
import numpy as np
from contextlib import ExitStack

import concourse.bass as bass
import concourse.mybir as mybir
from concourse.bass_utils import run_bass_kernel_spmd

F32 = mybir.dt.float32
BF16 = mybir.dt.bfloat16
AF = mybir.ActivationFunctionType
ALU = mybir.AluOpType
AX = mybir.AxisListType

P = 128
D = 2048
NCD = D // P
S = 4096
NTC = S // P
TOK = 2048
NQC = TOK // P
NH = 16
DH = 128
IT = 1408
NIB = IT // P
NE = 16
EPS = 1e-5
QSC = 0.08838834764831845
NEG = -1e30
RN = 1.0 / D

DBG = 0
CORES = 8
NH5 = NH
RACE = True
NE7 = NE


def tiles_of(qci):
    W = 256 * (qci + 1)
    out, k = [], 0
    while k < W:
        w = min(512, W - k)
        out.append((k, w))
        k += w
    return out


def build_program():
    nc = bass.Bass("TRN2", target_bir_lowering=False, detect_race_conditions=RACE)

    Xall = nc.dram_tensor("xall", [S, D], F32, kind="ExternalInput")
    Xown = nc.dram_tensor("xown", [TOK, D], F32, kind="ExternalInput")
    Dmk = nc.dram_tensor("dmask", [P, 256], F32, kind="ExternalInput")
    Wqh = nc.dram_tensor("wq", [D, D], F32, kind="ExternalInput")
    Wkh = nc.dram_tensor("wk", [D, D], F32, kind="ExternalInput")
    Wvh = nc.dram_tensor("wv", [D, D], F32, kind="ExternalInput")
    Woh = nc.dram_tensor("wo", [D, D], F32, kind="ExternalInput")
    bqh = nc.dram_tensor("bq", [D], F32, kind="ExternalInput")
    bkh = nc.dram_tensor("bk", [D], F32, kind="ExternalInput")
    bvh = nc.dram_tensor("bv", [D], F32, kind="ExternalInput")
    boh = nc.dram_tensor("bo", [D], F32, kind="ExternalInput")
    gh = nc.dram_tensor("lng", [D], F32, kind="ExternalInput")
    bh = nc.dram_tensor("lnb", [D], F32, kind="ExternalInput")
    gwh = nc.dram_tensor("gwt", [D, 15], F32, kind="ExternalInput")
    w1h = nc.dram_tensor("w1s", [NE, NIB, P, NCD, P], F32, kind="ExternalInput")
    w3h = nc.dram_tensor("w3s", [NE, NIB, P, NCD, P], F32, kind="ExternalInput")
    w2h = nc.dram_tensor("w2s", [NE, NIB, P, D], F32, kind="ExternalInput")
    Y = nc.dram_tensor("y", [TOK, D], F32, kind="ExternalOutput")

    xns = nc.dram_tensor("xns", [NTC, P, D], BF16)
    Ks = nc.dram_tensor("ks", [NH, P, S], BF16)
    Vs = nc.dram_tensor("vs", [NH, P, S], BF16)
    Qs = nc.dram_tensor("qs", [NH, P, TOK], BF16)
    AVs = nc.dram_tensor("avs", [NQC, P, NH, DH], BF16)
    x1sp = nc.dram_tensor("x1sp", [8, P, D], F32)
    h2sp = nc.dram_tensor("h2sp", [P, NCD, 1024], BF16)

    dbg = {}
    if DBG:
        if DBG >= 1:
            dbg["d_xns"] = nc.dram_tensor("d_xns", [NTC, P, D], BF16, kind="ExternalOutput")
        if DBG >= 2:
            dbg["d_ks"] = nc.dram_tensor("d_ks", [NH, P, S], BF16, kind="ExternalOutput")
        if DBG >= 3:
            dbg["d_vs"] = nc.dram_tensor("d_vs", [NH, P, S], BF16, kind="ExternalOutput")
        if DBG >= 4:
            dbg["d_qs"] = nc.dram_tensor("d_qs", [NH, P, TOK], BF16, kind="ExternalOutput")
        if DBG >= 5:
            dbg["d_avs"] = nc.dram_tensor("d_avs", [NQC, P, NH, DH], BF16, kind="ExternalOutput")
        if DBG == 7:
            dbg["d_x1"] = nc.dram_tensor("d_x1", [8, P, D], F32, kind="ExternalOutput")
            dbg["d_h2s"] = nc.dram_tensor("d_h2s", [P, NCD, 1024], BF16, kind="ExternalOutput")

    with ExitStack() as es:
        def sem(name):
            return es.enter_context(nc.semaphore(name))

        s_cst = sem("s_cst"); s_wld = sem("s_wld")
        s_x1 = sem("s_x1"); s_xst = sem("s_xst"); s_xg = sem("s_xg")
        s_ko = sem("s_ko"); s_vo = sem("s_vo"); s_qo = sem("s_qo"); s_x4 = sem("s_x4")
        s_kl = sem("s_kl"); s_vl = sem("s_vl"); s_ql = sem("s_ql")
        s_avo = sem("s_avo"); s_avl = sem("s_avl"); s_x6 = sem("s_x6")
        s_sp = sem("s_sp"); s_spl = sem("s_spl"); s_y = sem("s_y")
        s_w13 = sem("s_w13"); s_w2 = sem("s_w2"); s_gw = sem("s_gw")

        a1 = sem("a1"); v1 = sem("v1"); p1 = sem("p1"); g1 = sem("g1"); a1e = sem("a1e")
        pmm = sem("pmm"); vev = sem("vev")
        a4 = sem("a4"); v4 = sem("v4"); p4 = sem("p4"); g4 = sem("g4"); a4e = sem("a4e")
        p5s = sem("p5s"); p5t = sem("p5t"); p5a = sem("p5a"); p5q = sem("p5q"); v5c = sem("v5c")
        v5e = sem("v5e"); v5m = sem("v5m"); a5x = sem("a5x"); v5r = sem("v5r")
        a5e = sem("a5e"); v5pe = sem("v5pe")
        p6t = sem("p6t"); a6e = sem("a6e"); p6m = sem("p6m"); v6a = sem("v6a")
        g6b = sem("g6b"); a6s = sem("a6s"); v6s = sem("v6s"); a6h = sem("a6h")
        p7g = sem("p7g"); v7g = sem("v7g"); p7a = sem("p7a"); a7s = sem("a7s")
        v7u = sem("v7u"); p7b = sem("p7b"); v7y = sem("v7y")

        Wres = es.enter_context(nc.sbuf_tensor("wres", [P, NCD, D], BF16, side="right"))
        sbt = lambda *a: es.enter_context(nc.sbuf_tensor(*a))
        ident = sbt("ident", [P, P], BF16)
        ones1 = sbt("ones1", [1, P], F32)
        rowst = sbt("rowst", [1, D], F32)
        bkc = sbt("bkc", [P, NCD], F32)
        bvc = sbt("bvc", [P, NCD], F32)
        bqc = sbt("bqc", [P, NCD], F32)
        gwt_sb = sbt("gwt_sb", [P, NCD, 15], BF16)
        dmk_sb = sbt("dmk_sb", [P, 256], F32)

        ps = [es.enter_context(nc.psum_tensor(f"ps{i}", [P, 512], F32)) for i in range(6)]
        pst = [es.enter_context(nc.psum_tensor(f"pst{i}", [P, 1024], BF16)) for i in range(2)]

        with nc.Block("p0") as blk:
            @blk.sync
            def _(sy):
                sy.dma_start(out=bkc[:], in_=bkh.rearrange("(c p) -> p c", p=P),
                             allow_slow_non_contiguous=True).then_inc(s_cst, 16)
                sy.dma_start(out=bvc[:], in_=bvh.rearrange("(c p) -> p c", p=P),
                             allow_slow_non_contiguous=True).then_inc(s_cst, 16)
                sy.dma_start(out=bqc[:], in_=bqh.rearrange("(c p) -> p c", p=P),
                             allow_slow_non_contiguous=True).then_inc(s_cst, 16)
                sy.dma_start(out=dmk_sb[:], in_=Dmk[:, :]).then_inc(s_cst, 16)

            @blk.gpsimd
            def _(g):
                g.dma_start(out=gwt_sb[:], in_=gwh.rearrange("(c p) n -> p c n", p=P)
                            ).then_inc(s_gw, 16)
                g.memset(ident[:], 0.0)
                g.affine_select(out=ident[:], in_=ident[:], compare_op=ALU.not_equal,
                                fill=1.0, base=0, pattern=[[-1, P]], channel_multiplier=1)
                g.memset(ones1[:], 1.0).then_inc(g1, 1)

        with ExitStack() as es14:
            sb14 = lambda *a: es14.enter_context(nc.sbuf_tensor(*a))
            g_bc = sb14("g_bc", [P, D], F32)
            b_bc = sb14("b_bc", [P, D], F32)

            with nc.Block("p0b") as blk:
                @blk.sync
                def _(sy):
                    sy.dma_start(out=rowst[:], in_=gh[None, :]).then_inc(s_cst, 16)

                @blk.tensor
                def _(t):
                    t.wait_ge(s_cst, 16 * 5)
                    t.wait_ge(g1, 1)
                    for j in range(4):
                        t.matmul(ps[j][:], ones1[:], rowst[:, j * 512:(j + 1) * 512],
                                 start=True, stop=True).then_inc(p1, 1)

                @blk.scalar
                def _(s):
                    for j in range(4):
                        s.wait_ge(p1, j + 1)
                        s.copy(g_bc[:, j * 512:(j + 1) * 512], ps[j][:]).then_inc(v1, 1)

                @blk.sync
                def _(sy):
                    sy.wait_ge(p1, 4)
                    sy.dma_start(out=rowst[:], in_=bh[None, :]).then_inc(s_cst, 16)

                @blk.tensor
                def _(t):
                    t.wait_ge(s_cst, 16 * 6)
                    t.wait_ge(v1, 4)
                    for j in range(4):
                        t.matmul(ps[j][:], ones1[:], rowst[:, j * 512:(j + 1) * 512],
                                 start=True, stop=True).then_inc(p1, 1)

                @blk.scalar
                def _(s):
                    for j in range(4):
                        s.wait_ge(p1, 4 + j + 1)
                        s.copy(b_bc[:, j * 512:(j + 1) * 512], ps[j][:]).then_inc(v1, 1)

            with ExitStack() as e1:
                emit_ln_pass(nc, e1, dict(
                    tag="p1", src=Xall, ntc=NTC, xns=xns, g_bc=g_bc, b_bc=b_bc,
                    ident=ident, pst=pst,
                    s_x=s_x1, s_st=s_xst, va=v1, aa=a1, pa=p1, ga=g1, ae=a1e,
                    v_base=8, a_base=0, p_base=8, g_base=1, e_base=0, st_base=0,
                    prefetch=(Wkh, Wres, s_wld)))

            with ExitStack() as e23:
                emit_proj_pass(nc, e23, dict(
                    tag="pkv", ps=ps, Wres=Wres, scale=None,
                    segments=[
                        dict(dst=Ks, bias=bkc, s_out=s_ko, wld_need=16 * 16),
                        dict(dst=Vs, bias=bvc, s_out=s_vo, wld_need=16 * 32)],
                    xsrc=xns, xst_sem=s_xst, xst_gate=16 * 32,
                    s_xg=s_xg, pmm=pmm, vev=vev, s_wld=s_wld,
                    prefetches={0: (Wvh, Wres, s_wld, pmm, 128),
                                8: (Wqh, Wres, s_wld, pmm, 256)}))

            with ExitStack() as e4:
                emit_p4(nc, e4, dict(
                    ps=ps, Xown=Xown, Qs=Qs, g_bc=g_bc, b_bc=b_bc, ident=ident, pst=pst,
                    bqc=bqc, Wres=Wres, s_wld=s_wld, wld_need=16 * 48,
                    s_x4=s_x4, s_qo=s_qo, a4=a4, v4=v4, p4=p4, g4=g4, a4e=a4e,
                    pmm=pmm, vev=vev, mm_base=256, ev_base=256,
                    gate_sem=s_vo, gate_val=16 * 8,
                    prefetch=(Woh, Wres, s_wld, pmm, 320)))

        if DBG and DBG <= 4:
            emit_dbg(nc, dbg, [
                ("d_xns", xns, s_xst, 16 * NTC),
                ("d_ks", Ks, s_ko, 16 * 8),
                ("d_vs", Vs, s_vo, 16 * 8),
                ("d_qs", Qs, s_qo, 16 * 4)])
            return nc, dbg

        with ExitStack() as es5:
            emit_p5(nc, es5, dict(
                Ks=Ks, Vs=Vs, Qs=Qs, AVs=AVs, dmk=dmk_sb, ident=ident,
                ps=ps, pst=pst,
                s_ko=s_ko, s_vo=s_vo, s_qo=s_qo, s_kl=s_kl, s_vl=s_vl, s_ql=s_ql,
                s_avo=s_avo, s_cst=s_cst,
                p5s=p5s, p5t=p5t, p5a=p5a, p5q=p5q, v5e=v5e, v5m=v5m,
                a5x=a5x, v5r=v5r, a5e=a5e, v5pe=v5pe, v5c=v5c))

        if DBG == 5:
            emit_dbg(nc, dbg, [("d_avs", AVs, s_avo, 16 * NH5)])
            return nc, dbg

        with ExitStack() as es67:
            emit_p67(nc, es67, dict(
                AVs=AVs, Xown=Xown, Y=Y, boh=boh, ones1=ones1, rowst=rowst,
                ident=ident, ps=ps, pst=pst, Wres=Wres, gwt_sb=gwt_sb,
                w1h=w1h, w3h=w3h, w2h=w2h, x1sp=x1sp, h2sp=h2sp,
                s_avo=s_avo, s_avl=s_avl, s_x6=s_x6, s_sp=s_sp, s_spl=s_spl,
                s_y=s_y, s_w13=s_w13, s_w2=s_w2, s_wld=s_wld, s_cst=s_cst, s_gw=s_gw,
                p6t=p6t, a6e=a6e, p6m=p6m, v6a=v6a, a6s=a6s, v6s=v6s,
                g1=g1, p7g=p7g, v7g=v7g, p7a=p7a, a7s=a7s, v7u=v7u,
                p7b=p7b, v7y=v7y))

        if DBG == 7:
            emit_dbg(nc, dbg, [("d_x1", x1sp, s_sp, 16 * 9),
                               ("d_h2s", h2sp, s_sp, 16 * 9)])

    return nc, dbg


def emit_ln_pass(nc, es, c):
    sb = lambda *a: es.enter_context(nc.sbuf_tensor(*a))
    tg = c["tag"]
    xbuf = [sb(f"{tg}x{i}", [P, D], F32) for i in range(2)]
    hbuf = [sb(f"{tg}h{i}", [P, D], F32) for i in range(2)]
    tbuf = [sb(f"{tg}t{i}", [P, D], F32) for i in range(2)]
    xnb = [sb(f"{tg}n{i}", [P, D], BF16) for i in range(2)]
    xtb = [sb(f"{tg}T{i}", [P, D], BF16) for i in range(2)]
    garb = sb(f"{tg}garb", [P, D], F32)
    cols = [sb(f"{tg}c{i}", [P, 8], F32) for i in range(2)]
    ntc = c["ntc"]
    src, xns = c["src"], c["xns"]
    s_x, s_st = c["s_x"], c["s_st"]
    va, aa, pa, ga, ae = c["va"], c["aa"], c["pa"], c["ga"], c["ae"]
    vb, ab, pb, gb, eb, stb = (c["v_base"], c["a_base"], c["p_base"],
                               c["g_base"], c["e_base"], c["st_base"])
    g_bc, b_bc, ident, pst = c["g_bc"], c["b_bc"], c["ident"], c["pst"]
    srcv = src.rearrange("(tc p) d -> tc p d", p=P)

    with nc.Block(c["tag"]) as blk:
        @blk.gpsimd
        def _(g):
            wh, Wd, s_wld = c["prefetch"]
            wv = wh.rearrange("(c p) n -> c p n", p=P)
            for cc in range(NCD):
                g.dma_start(out=Wd[:, cc, :], in_=wv[cc]).then_inc(s_wld, 16)
            for tc in range(ntc):
                pbuf = tc % 2
                g.wait_ge(va, vb + 4 * tc + 4)
                if tc >= 2:
                    g.wait_ge(pa, pb + 2 * (tc - 2) + 2)
                g.tensor_tensor(xnb[pbuf][:], hbuf[pbuf][:], b_bc[:], ALU.add
                                ).then_inc(ga, 1)

        @blk.sync
        def _(sy):
            for tc in range(ntc):
                pbuf = tc % 2
                if tc >= 2:
                    sy.wait_ge(va, vb + 4 * (tc - 2) + 2)
                sy.dma_start(out=xbuf[pbuf][:], in_=srcv[tc]).then_inc(s_x, 16)
                if tc >= 1:
                    sy.wait_ge(ae, eb + 2 * tc)
                    sy.dma_start(out=xns[tc - 1], in_=xtb[(tc - 1) % 2][:]
                                 ).then_inc(s_st, 16)
            sy.wait_ge(ae, eb + 2 * ntc)
            sy.dma_start(out=xns[ntc - 1], in_=xtb[(ntc - 1) % 2][:]).then_inc(s_st, 16)

        @blk.scalar
        def _(s):
            for tc in range(ntc):
                pbuf = tc % 2
                co = cols[pbuf]
                s.wait_ge(s_x, 16 * (tc + 1))
                s.activation(garb[:], xbuf[pbuf][:], AF.Square,
                             accum_out=co[:, 1:2]).then_inc(aa, 1)
                s.wait_ge(va, vb + 4 * tc + 1)
                s.activation(co[:, 5:6], co[:, 3:4], AF.Ln)
                s.activation(co[:, 4:5], co[:, 5:6], AF.Exp, scale=-0.5).then_inc(aa, 1)
                s.wait_ge(va, vb + 4 * tc + 2)
                s.activation(garb[:], hbuf[pbuf][:], AF.Square,
                             accum_out=co[:, 1:2]).then_inc(aa, 1)
                s.wait_ge(va, vb + 4 * tc + 3)
                s.activation(co[:, 5:6], co[:, 3:4], AF.Ln)
                s.activation(co[:, 4:5], co[:, 5:6], AF.Exp, scale=-0.5).then_inc(aa, 1)
                s.wait_ge(pa, pb + 2 * tc + 1)
                if tc >= 2:
                    s.wait_ge(s_st, stb + 16 * (tc - 1))
                s.copy(xtb[pbuf][:, 0:1024], pst[0][:]).then_inc(ae, 1)
                s.wait_ge(pa, pb + 2 * tc + 2)
                s.copy(xtb[pbuf][:, 1024:2048], pst[1][:]).then_inc(ae, 1)

        @blk.vector
        def _(v):
            for tc in range(ntc):
                pbuf = tc % 2
                co = cols[pbuf]
                v.wait_ge(s_x, 16 * (tc + 1))
                v.reduce_sum(co[:, 0:1], xbuf[pbuf][:], AX.X)
                v.wait_ge(aa, ab + 4 * tc + 1)
                v.tensor_scalar_mul(co[:, 2:3], co[:, 0:1], RN)
                v.tensor_scalar(co[:, 3:4], co[:, 1:2], RN, EPS, ALU.mult, ALU.add)
                v.tensor_scalar(co[:, 5:6], co[:, 2:3], co[:, 2:3], None, ALU.mult)
                v.tensor_scalar(co[:, 3:4], co[:, 5:6], co[:, 3:4], -1.0,
                                ALU.subtract, ALU.mult).then_inc(va, 1)
                v.wait_ge(aa, ab + 4 * tc + 2)
                if tc >= 2:
                    v.wait_ge(ga, gb + tc - 1)
                v.tensor_scalar(hbuf[pbuf][:], xbuf[pbuf][:], co[:, 2:3], co[:, 4:5],
                                ALU.subtract, ALU.mult).then_inc(va, 1)
                v.reduce_sum(co[:, 0:1], hbuf[pbuf][:], AX.X)
                v.wait_ge(aa, ab + 4 * tc + 3)
                v.tensor_scalar_mul(co[:, 2:3], co[:, 0:1], RN)
                v.tensor_scalar(co[:, 3:4], co[:, 1:2], RN, EPS, ALU.mult, ALU.add)
                v.tensor_scalar(co[:, 5:6], co[:, 2:3], co[:, 2:3], None, ALU.mult)
                v.tensor_scalar(co[:, 3:4], co[:, 5:6], co[:, 3:4], -1.0,
                                ALU.subtract, ALU.mult).then_inc(va, 1)
                v.wait_ge(aa, ab + 4 * tc + 4)
                v.tensor_scalar(tbuf[pbuf][:], hbuf[pbuf][:], co[:, 2:3], co[:, 4:5],
                                ALU.subtract, ALU.mult)
                v.tensor_tensor(hbuf[pbuf][:], tbuf[pbuf][:], g_bc[:], ALU.mult
                                ).then_inc(va, 1)

        @blk.tensor
        def _(t):
            for tc in range(ntc):
                pbuf = tc % 2
                t.wait_ge(ga, gb + tc + 1)
                for half in range(2):
                    bidx = 2 * tc + half
                    if bidx >= 2:
                        t.wait_ge(ae, eb + bidx - 1)
                    for cc in range(8):
                        ccc = half * 8 + cc
                        last = t.transpose(pst[bidx % 2][:, cc * P:(cc + 1) * P],
                                           xnb[pbuf][:, ccc * P:(ccc + 1) * P], ident[:])
                    last.then_inc(pa, 1)


def emit_proj_pass(nc, es, c):
    sb = lambda *a: es.enter_context(nc.sbuf_tensor(*a))
    tg = c["tag"]
    xg = [sb(f"{tg}xg{i}", [P, NCD, 512], BF16) for i in range(2)]
    ob = [sb(f"{tg}ob{i}", [P, NCD, 512], BF16) for i in range(2)]
    ps = c["ps"]
    Wres = c["Wres"]
    pmm, vev = c["pmm"], c["vev"]
    s_xg, s_wld = c["s_xg"], c["s_wld"]
    xsv = c["xsrc"].rearrange("tc p (c j) -> tc p c j", c=NCD)
    segs = c["segments"]
    ngr = 8 * len(segs)

    def seg(gi):
        return segs[gi // 8], gi % 8

    with nc.Block(tg) as blk:
        @blk.gpsimd
        def _(g):
            for at in sorted(c["prefetches"]):
                wh, Wd, s_wldN, gate_sem, gate_val = c["prefetches"][at]
                wv = wh.rearrange("(c p) n -> c p n", p=P)
                g.wait_ge(gate_sem, gate_val)
                for cc in range(NCD):
                    g.dma_start(out=Wd[:, cc, :], in_=wv[cc]).then_inc(s_wldN, 16)

        @blk.sync
        def _(sy):
            for gi in range(ngr):
                sg, gl = seg(gi)
                pbuf = gi % 2
                if gi == 0:
                    sy.wait_ge(c["xst_sem"], c["xst_gate"])
                if gi >= 2:
                    sy.wait_ge(pmm, 16 * (gi - 1))
                for u in range(4):
                    sy.dma_start(out=xg[pbuf][:, :, u * P:(u + 1) * P],
                                 in_=xsv[4 * gl + u]).then_inc(s_xg, 16)
                if gi >= 1:
                    psg, pgl = seg(gi - 1)
                    sy.wait_ge(vev, 16 * gi)
                    sy.dma_start(out=psg["dst"].rearrange("h p (g j) -> g p h j", j=512)[pgl],
                                 in_=ob[(gi - 1) % 2][:]).then_inc(psg["s_out"], 16)
            psg, pgl = seg(ngr - 1)
            sy.wait_ge(vev, 16 * ngr)
            sy.dma_start(out=psg["dst"].rearrange("h p (g j) -> g p h j", j=512)[pgl],
                         in_=ob[(ngr - 1) % 2][:]).then_inc(psg["s_out"], 16)

        @blk.tensor
        def _(t):
            for gi in range(ngr):
                sg, gl = seg(gi)
                pbuf = gi % 2
                if gl == 0:
                    t.wait_ge(s_wld, sg["wld_need"])
                t.wait_ge(s_xg, 16 * (4 * gi + 4))
                for do in range(NCD):
                    gd = gi * 16 + do
                    bank = ps[do % 4]
                    if gd >= 4:
                        t.wait_ge(vev, gd - 3)
                    for cc in range(NCD):
                        last = t.matmul(bank[:], Wres[:, cc, do * P:(do + 1) * P],
                                        xg[pbuf][:, cc, :],
                                        start=(cc == 0), stop=(cc == NCD - 1))
                    last.then_inc(pmm, 1)

        @blk.vector
        def _(v):
            nst = {id(sg): 0 for sg in segs}
            for gi in range(ngr):
                sg, gl = seg(gi)
                pbuf = gi % 2
                for do in range(NCD):
                    gd = gi * 16 + do
                    v.wait_ge(pmm, gd + 1)
                    if gi >= 2 and do == 0:
                        psg, pgl = seg(gi - 2)
                        v.wait_ge(psg["s_out"], 16 * (pgl + 1))
                    v.tensor_scalar(ob[pbuf][:, do, :], ps[do % 4][:],
                                    sg["bias"][:, do:do + 1], None, ALU.add
                                    ).then_inc(vev, 1)


def emit_p4(nc, es, c):
    sb = lambda *a: es.enter_context(nc.sbuf_tensor(*a))
    xbuf = [sb(f"p4x{i}", [P, D], F32) for i in range(2)]
    hbuf = [sb(f"p4h{i}", [P, D], F32) for i in range(2)]
    tbuf = [sb(f"p4t{i}", [P, D], F32) for i in range(2)]
    xnb = [sb(f"p4n{i}", [P, D], BF16) for i in range(2)]
    xg = [sb("p4g0", [P, NCD, 512], BF16)]
    ob = [sb("p4o0", [P, NCD, 512], BF16)]
    garb = sb("p4garb", [P, D], F32)
    cols = [sb(f"p4c{i}", [P, 8], F32) for i in range(2)]
    ps = c["ps"]
    g_bc, b_bc, ident, pst = c["g_bc"], c["b_bc"], c["ident"], c["pst"]
    Xov = c["Xown"].rearrange("(tc p) d -> tc p d", p=P)
    Qs, bqc, Wres = c["Qs"], c["bqc"], c["Wres"]
    a4, v4, p4, g4, a4e = c["a4"], c["v4"], c["p4"], c["g4"], c["a4e"]
    pmm, vev = c["pmm"], c["vev"]
    mm_b, ev_b = c["mm_base"], c["ev_base"]
    s_x4, s_qo, s_wld = c["s_x4"], c["s_qo"], c["s_wld"]
    qsv = Qs.rearrange("h p (g j) -> g p h j", j=512)

    with nc.Block("p4") as blk:
        @blk.gpsimd
        def _(g):
            for tc in range(NQC):
                pbuf = tc % 2
                g.wait_ge(v4, 4 * tc + 4)
                if tc >= 2:
                    g.wait_ge(p4, 2 * (tc - 2) + 2)
                g.tensor_tensor(xnb[pbuf][:], hbuf[pbuf][:], b_bc[:], ALU.add
                                ).then_inc(g4, 1)
            wh, Wd, s_wldN, gate_sem, gate_val = c["prefetch"]
            wv = wh.rearrange("(c p) n -> c p n", p=P)
            g.wait_ge(gate_sem, gate_val)
            for cc in range(NCD):
                g.dma_start(out=Wd[:, cc, :], in_=wv[cc]).then_inc(s_wldN, 16)

        @blk.sync
        def _(sy):
            for tc in range(NQC):
                pbuf = tc % 2
                if tc == 0:
                    sy.wait_ge(c["gate_sem"], c["gate_val"])
                if tc >= 2:
                    sy.wait_ge(v4, 4 * (tc - 2) + 2)
                sy.dma_start(out=xbuf[pbuf][:], in_=Xov[tc]).then_inc(s_x4, 16)
                if tc % 4 == 3 and tc >= 7:
                    gi = tc // 4 - 1
                    sy.wait_ge(vev, ev_b + 16 * (gi + 1))
                    sy.dma_start(out=qsv[gi], in_=ob[0][:]).then_inc(s_qo, 16)
            sy.wait_ge(vev, ev_b + 64)
            sy.dma_start(out=qsv[3], in_=ob[0][:]).then_inc(s_qo, 16)

        @blk.scalar
        def _(s):
            for tc in range(NQC):
                pbuf = tc % 2
                co = cols[pbuf]
                s.wait_ge(s_x4, 16 * (tc + 1))
                s.activation(garb[:], xbuf[pbuf][:], AF.Square, accum_out=co[:, 1:2]
                             ).then_inc(a4, 1)
                s.wait_ge(v4, 4 * tc + 1)
                s.activation(co[:, 5:6], co[:, 3:4], AF.Ln)
                s.activation(co[:, 4:5], co[:, 5:6], AF.Exp, scale=-0.5).then_inc(a4, 1)
                s.wait_ge(v4, 4 * tc + 2)
                s.activation(garb[:], hbuf[pbuf][:], AF.Square, accum_out=co[:, 1:2]
                             ).then_inc(a4, 1)
                s.wait_ge(v4, 4 * tc + 3)
                s.activation(co[:, 5:6], co[:, 3:4], AF.Ln)
                s.activation(co[:, 4:5], co[:, 5:6], AF.Exp, scale=-0.5).then_inc(a4, 1)
                gi, u = tc // 4, tc % 4
                dxg = xg[0]
                s.wait_ge(p4, 2 * tc + 1)
                if gi >= 1 and u == 0:
                    s.wait_ge(pmm, mm_b + 16 * gi)
                s.copy(dxg[:, 0:8, u * P:(u + 1) * P],
                              pst[0][:].rearrange("p (c j) -> p c j", c=8)).then_inc(a4e, 1)
                s.wait_ge(p4, 2 * tc + 2)
                s.copy(dxg[:, 8:16, u * P:(u + 1) * P],
                              pst[1][:].rearrange("p (c j) -> p c j", c=8)).then_inc(a4e, 1)

        @blk.vector
        def _(v):
            for tc in range(NQC):
                pbuf = tc % 2
                co = cols[pbuf]
                v.wait_ge(s_x4, 16 * (tc + 1))
                v.reduce_sum(co[:, 0:1], xbuf[pbuf][:], AX.X)
                v.wait_ge(a4, 4 * tc + 1)
                v.tensor_scalar_mul(co[:, 2:3], co[:, 0:1], RN)
                v.tensor_scalar(co[:, 3:4], co[:, 1:2], RN, EPS, ALU.mult, ALU.add)
                v.tensor_scalar(co[:, 5:6], co[:, 2:3], co[:, 2:3], None, ALU.mult)
                v.tensor_scalar(co[:, 3:4], co[:, 5:6], co[:, 3:4], -1.0,
                                ALU.subtract, ALU.mult).then_inc(v4, 1)
                v.wait_ge(a4, 4 * tc + 2)
                if tc >= 2:
                    v.wait_ge(g4, tc - 1)
                v.tensor_scalar(hbuf[pbuf][:], xbuf[pbuf][:], co[:, 2:3], co[:, 4:5],
                                ALU.subtract, ALU.mult).then_inc(v4, 1)
                v.reduce_sum(co[:, 0:1], hbuf[pbuf][:], AX.X)
                v.wait_ge(a4, 4 * tc + 3)
                v.tensor_scalar_mul(co[:, 2:3], co[:, 0:1], RN)
                v.tensor_scalar(co[:, 3:4], co[:, 1:2], RN, EPS, ALU.mult, ALU.add)
                v.tensor_scalar(co[:, 5:6], co[:, 2:3], co[:, 2:3], None, ALU.mult)
                v.tensor_scalar(co[:, 3:4], co[:, 5:6], co[:, 3:4], -1.0,
                                ALU.subtract, ALU.mult).then_inc(v4, 1)
                v.wait_ge(a4, 4 * tc + 4)
                v.tensor_scalar(tbuf[pbuf][:], hbuf[pbuf][:], co[:, 2:3], co[:, 4:5],
                                ALU.subtract, ALU.mult)
                v.tensor_tensor(hbuf[pbuf][:], tbuf[pbuf][:], g_bc[:], ALU.mult
                                ).then_inc(v4, 1)
                if tc % 4 == 3:
                    gi = tc // 4
                    for do in range(NCD):
                        gd = gi * 16 + do
                        v.wait_ge(pmm, mm_b + gd + 1)
                        if gi >= 1 and do == 0:
                            v.wait_ge(s_qo, 16 * gi)
                        v.tensor_scalar(ob[0][:, do, :], ps[do % 4][:],
                                        bqc[:, do:do + 1], QSC, ALU.add, ALU.mult
                                        ).then_inc(vev, 1)

        @blk.tensor
        def _(t):
            for tc in range(NQC):
                pbuf = tc % 2
                t.wait_ge(g4, tc + 1)
                for half in range(2):
                    bidx = 2 * tc + half
                    if bidx >= 2:
                        t.wait_ge(a4e, bidx - 1)
                    for cc in range(8):
                        ccc = half * 8 + cc
                        last = t.transpose(pst[bidx % 2][:, cc * P:(cc + 1) * P],
                                           xnb[pbuf][:, ccc * P:(ccc + 1) * P], ident[:])
                    last.then_inc(p4, 1)
                if tc % 4 == 3:
                    gi = tc // 4
                    t.wait_ge(a4e, 2 * (tc + 1))
                    if gi == 0:
                        t.wait_ge(s_wld, c["wld_need"])
                    for do in range(NCD):
                        gd = gi * 16 + do
                        bank = ps[do % 4]
                        if gd >= 4:
                            t.wait_ge(vev, ev_b + gd - 3)
                        for cc in range(NCD):
                            last = t.matmul(bank[:], Wres[:, cc, do * P:(do + 1) * P],
                                            xg[0][:, cc, :],
                                            start=(cc == 0), stop=(cc == NCD - 1))
                        last.then_inc(pmm, 1)


def emit_p5(nc, es, c):
    sb = lambda *a: es.enter_context(nc.sbuf_tensor(*a))
    ktb = [sb(f"p5k{i}", [P, S], BF16) for i in range(2)]
    vtb = [sb(f"p5v{i}", [P, S], BF16) for i in range(2)]
    qb = [sb(f"p5q{i}", [P, TOK], BF16) for i in range(2)]
    vtok = sb("p5vt", [P, NTC, P], BF16)
    Ssb = [sb(f"p5s{i}", [P, S], F32) for i in range(2)]
    Psb = [sb(f"p5p{i}", [P, S], BF16) for i in range(2)]
    ptb_sb = sb("p5pt", [P, 8, P], BF16)
    avh = [sb(f"p5a{i}", [P, NQC, DH], BF16) for i in range(2)]
    mx = [sb(f"p5m{i}", [P, 8], F32) for i in range(2)]
    col = [sb(f"p5c{i}", [P, 4], F32) for i in range(2)]
    ps, pst = c["ps"], c["pst"]
    Ks, Vs, Qs, AVs = c["Ks"], c["Vs"], c["Qs"], c["AVs"]
    dmk, ident = c["dmk"], c["ident"]
    s_kl, s_vl, s_ql, s_avo = c["s_kl"], c["s_vl"], c["s_ql"], c["s_avo"]
    s_cst = c["s_cst"]
    p5s, p5t, p5a = c["p5s"], c["p5t"], c["p5a"]
    v5e, v5m, a5x, v5r, a5e = c["v5e"], c["v5m"], c["a5x"], c["v5r"], c["a5e"]
    v5c = c["v5c"]
    avv = AVs.rearrange("q p h d -> p q h d")

    tile_of_E = {}
    gt = 0
    for h in range(NH5):
        for qci in range(NQC):
            E = h * NQC + qci
            tl = []
            for (k0, w) in tiles_of(qci):
                tl.append((gt, k0, w))
                gt += 1
            tile_of_E[E] = tl

    batches = []
    bodyb_of_E = {}
    av_cnt = 0
    ptb_cnt = 0
    av_through_E = {}
    bt_through_E = {}

    def emit_body_plan(E):
        nonlocal av_cnt, ptb_cnt
        qci = E % NQC
        nkc = 2 * (qci + 1)
        bl = []
        k = 0
        while k < nkc:
            kcs = list(range(k, min(k + 4, nkc)))
            b = dict(kind="pt", E=E, h=E // NQC, gb=len(batches), ptb=ptb_cnt,
                     kcs=kcs, av_idx=av_cnt)
            batches.append(b)
            bl.append(b)
            ptb_cnt += 1
            av_cnt += 1
            k += 4
        bodyb_of_E[E] = bl
        av_through_E[E] = av_cnt
        bt_through_E[E] = len(batches)

    av_through_head = {-1: 0}
    for h in range(NH5):
        if h > 0:
            emit_body_plan(h * NQC - 1)
            av_through_head[h - 1] = av_cnt
        for b in range(8):
            batches.append(dict(kind="vt", h=h, b=b, gb=len(batches),
                                kcs=list(range(4 * b, 4 * b + 4))))
        for qci in range(1, NQC):
            emit_body_plan(h * NQC + qci - 1)
    emit_body_plan(NH5 * NQC - 1)
    av_through_head[NH5 - 1] = av_cnt

    def w(eng, semh, v):
        if v > 0:
            eng.wait_ge(semh, v)

    with nc.Block("p5") as blk:
        @blk.sync
        def _(sy):
            def loads(h):
                sy.dma_start(out=ktb[h % 2][:], in_=Ks[h]).then_inc(s_kl, 16)
                sy.dma_start(out=vtb[h % 2][:], in_=Vs[h]).then_inc(s_vl, 16)
                sy.dma_start(out=qb[h % 2][:], in_=Qs[h]).then_inc(s_ql, 16)
            sy.wait_ge(c["s_ko"], 16 * 8)
            sy.wait_ge(c["s_vo"], 16 * 8)
            sy.wait_ge(c["s_qo"], 16 * 4)
            loads(0)
            loads(1)
            for h in range(NH5):
                if h + 2 < NH5:
                    w(sy, p5a, av_through_head[h])
                    loads(h + 2)
                sy.wait_ge(v5r, NQC * (h + 1))
                sy.dma_start(out=avv[:, :, h, :], in_=avh[h % 2][:]).then_inc(s_avo, 16)

        @blk.tensor
        def _(t):
            def body(E):
                h, qci = E // NQC, E % NQC
                t.wait_ge(a5x, E + 1)
                w(t, v5r, E - 1)
                avb = ps[2 + E % 2]
                nkc = 2 * (qci + 1)
                def avmm(b):
                    t.wait_ge(a5e, b["gb"] + 1)
                    for i, kc in enumerate(b["kcs"]):
                        last = t.matmul(avb[:, 0:DH],
                                        ptb_sb[:, (b["ptb"] % 2) * 4 + i, :],
                                        vtok[:, kc, :],
                                        start=(kc == 0), stop=(kc == nkc - 1))
                    last.then_inc(p5a, 1)

                bl = bodyb_of_E[E]
                for bi, b in enumerate(bl):
                    gb = b["gb"]
                    if gb >= 2:
                        t.wait_ge(a5e, gb - 1)
                    for i, kc in enumerate(b["kcs"]):
                        last = t.transpose(
                            pst[gb % 2][:, i * P:(i + 1) * P],
                            Psb[E % 2][:, kc * P:(kc + 1) * P], ident[:])
                    last.then_inc(p5t, 1)
                    if bi >= 1:
                        avmm(bl[bi - 1])
                avmm(bl[-1])

            for h in range(NH5):
                if h > 0:
                    body(h * NQC - 1)
                t.wait_ge(s_vl, 16 * (h + 1))
                vts = [bt for bt in batches if bt["kind"] == "vt" and bt["h"] == h]
                for bt in vts:
                    gb = bt["gb"]
                    if gb >= 2:
                        t.wait_ge(a5e, gb - 1)
                    for i, kc in enumerate(bt["kcs"]):
                        last = t.transpose(
                            pst[gb % 2][:, i * P:(i + 1) * P],
                            vtb[h % 2][:, kc * P:(kc + 1) * P], ident[:])
                    last.then_inc(p5t, 1)
                t.wait_ge(s_kl, 16 * (h + 1))
                t.wait_ge(s_ql, 16 * (h + 1))
                for qci in range(NQC):
                    E = h * NQC + qci
                    for (gtx, k0, wd) in tile_of_E[E]:
                        w(t, v5e, gtx - 1)
                        t.matmul(ps[gtx % 2][:, 0:wd],
                                 qb[h % 2][:, qci * P:(qci + 1) * P],
                                 ktb[h % 2][:, k0:k0 + wd],
                                 start=True, stop=True).then_inc(p5s, 1)
                    if qci > 0:
                        body(E - 1)
            body(NH5 * NQC - 1)

        @blk.scalar
        def _(s):
            def evac(bt):
                gb = bt["gb"]
                s.wait_ge(p5t, gb + 1)
                n = len(bt["kcs"])
                if bt["kind"] == "vt":
                    w(s, p5a, av_through_head[bt["h"] - 1])
                    dst = vtok[:, bt["b"] * 4:bt["b"] * 4 + n, :]
                else:
                    w(s, p5a, bt["ptb"] - 1)
                    dst = ptb_sb[:, (bt["ptb"] % 2) * 4:(bt["ptb"] % 2) * 4 + n, :]
                s.copy(dst, pst[gb % 2][:, 0:n * P]
                       .rearrange("p (k d) -> p k d", d=P)).then_inc(a5e, 1)

            for h in range(NH5):
                if h > 0:
                    for bt in bodyb_of_E[h * NQC - 1]:
                        evac(bt)
                for bt in [bt for bt in batches if bt["kind"] == "vt" and bt["h"] == h]:
                    evac(bt)
                for qci in range(NQC):
                    E = h * NQC + qci
                    s.wait_ge(v5m, E + 1)
                    if E >= 2:
                        s.wait_ge(p5t, bt_through_E[E - 2])
                        s.wait_ge(v5r, E - 1)
                    W = 256 * (qci + 1)
                    s.activation(Psb[E % 2][:, 0:W], Ssb[E % 2][:, 0:W], AF.Exp,
                                 bias=col[E % 2][:, 0:1],
                                 accum_out=col[E % 2][:, 1:2]).then_inc(a5x, 1)
                    if qci > 0:
                        for bt in bodyb_of_E[E - 1]:
                            evac(bt)
            for bt in bodyb_of_E[NH5 * NQC - 1]:
                evac(bt)

        @blk.vector
        def _(v):
            def finish(E):
                h, qci = E // NQC, E % NQC
                v.reciprocal(col[E % 2][:, 2:3], col[E % 2][:, 1:2]).then_inc(v5c, 1)
                v.wait_ge(v5c, E + 1)
                v.wait_ge(p5a, av_through_E[E])
                if qci == 0:
                    w(v, s_avo, 16 * (h - 1))
                v.tensor_scalar(avh[h % 2][:, qci, :], ps[2 + E % 2][:, 0:DH],
                                col[E % 2][:, 2:3], None, ALU.mult).then_inc(v5r, 1)

            for E in range(NH5 * NQC):
                h, qci = E // NQC, E % NQC
                if E == 0:
                    v.wait_ge(s_cst, 16 * 6)
                nt = len(tile_of_E[E])
                for t_i, (gtx, k0, wd) in enumerate(tile_of_E[E]):
                    v.wait_ge(p5s, gtx + 1)
                    if t_i == 0:
                        w(v, a5x, E - 1)
                    v.tensor_scalar(Ssb[E % 2][:, k0:k0 + wd], ps[gtx % 2][:, 0:wd],
                                    1.0, None, ALU.mult, ALU.max,
                                    accum_out=mx[E % 2][:, t_i:t_i + 1]).then_inc(v5e, 1)
                W = 256 * (qci + 1)
                v.tensor_tensor(Ssb[E % 2][:, W - 256:W], Ssb[E % 2][:, W - 256:W],
                                dmk[:], ALU.add)
                v.reduce_max(col[E % 2][:, 0:1], mx[E % 2][:, 0:nt], AX.X,
                             negate=True).then_inc(v5m, 1)
                if E >= 1:
                    v.wait_ge(a5x, E)
                    finish(E - 1)
            v.wait_ge(a5x, NH5 * NQC)
            finish(NH5 * NQC - 1)


def emit_p67(nc, es, c):
    sb = lambda *a: es.enter_context(nc.sbuf_tensor(*a))
    oa = sb("oa", [P, 8, D], F32)
    h2T = sb("h2T", [P, NCD, 1024], BF16)
    ab = sb("p67ab", [P, 7680], BF16)
    af = sb("p67af", [P, 2048], F32)
    avb = [ab[:, 0:2048].rearrange("p (h d) -> p h d", h=NH),
           ab[:, 2048:4096].rearrange("p (h d) -> p h d", h=NH)]
    avT = ab[:, 4096:6144].rearrange("p (c d) -> p c d", c=NCD)
    x6b = af
    sa = [af[:, 0:512], af[:, 512:1024]]
    w2x = ab[:, 0:5632].rearrange("p (i j) -> p i j", i=NIB)
    h2bf = sb("p6h2", [P, D], BF16)
    bo_bc = sb("p6bo", [P, D], F32)
    co6 = sb("p6c", [P, 8], F32)
    comb = sb("p7comb", [P, 8, 16], F32)
    gl = sb("p7gl", [P, 16], F32)
    gp = sb("p7gp", [P, 16], F32)
    t2b = sb("p7t2", [P, 16], F32)
    gc = sb("p7gc", [P, 8], F32)

    ps, pst = c["ps"], c["pst"]
    ident, ones1, rowst = c["ident"], c["ones1"], c["rowst"]
    Wres, gwt_sb = c["Wres"], c["gwt_sb"]
    AVs, Xown, Y, boh = c["AVs"], c["Xown"], c["Y"], c["boh"]
    w1h, w3h, w2h = c["w1h"], c["w3h"], c["w2h"]
    x1sp, h2sp = c["x1sp"], c["h2sp"]
    s_avl, s_x6 = c["s_avl"], c["s_x6"]
    s_sp, s_spl, s_y = c["s_sp"], c["s_spl"], c["s_y"]
    s_w13, s_w2, s_wld, s_cst = c["s_w13"], c["s_w2"], c["s_wld"], c["s_cst"]
    s_gw = c["s_gw"]
    p6t, a6e, p6m, v6a = c["p6t"], c["a6e"], c["p6m"], c["v6a"]
    a6s, v6s = c["a6s"], c["v6s"]
    p7g, v7g, p7a, a7s = c["p7g"], c["v7g"], c["p7a"], c["a7s"]
    v7u, p7b, v7y = c["v7u"], c["p7b"], c["v7y"]
    Xov = Xown.rearrange("(tc p) d -> tc p d", p=P)
    Yv = Y.rearrange("(tc p) d -> tc p d", p=P)
    halves = [1, 0]

    p6t_avT, p6t_h2T = {}, {}
    cnt = 0
    for gq in range(16):
        cnt += 2
        p6t_avT[gq] = cnt
        if gq >= 1:
            cnt += 2
            p6t_h2T[gq - 1] = cnt
    cnt += 2
    p6t_h2T[15] = cnt

    with nc.Block("p6") as blk:
        @blk.sync
        def _(sy):
            sy.dma_start(out=rowst[:], in_=boh[None, :]).then_inc(s_cst, 16)
            for gq in range(16):
                hf, qcl = halves[gq // 8], gq % 8
                qc = hf * 8 + qcl
                if gq == 0:
                    sy.wait_ge(c["s_avo"], 16 * NH)
                if gq >= 2:
                    sy.wait_ge(p6t, p6t_avT[gq - 2])
                sy.dma_start(out=avb[gq % 2][:], in_=AVs[qc]).then_inc(s_avl, 16)
                if gq == 8:
                    for tc in range(8):
                        sy.wait_ge(v6a, 8 * tc + 8)
                        sy.dma_start(out=x1sp[tc], in_=oa[:, tc, :]).then_inc(s_sp, 16)
                    sy.wait_ge(a6e, 4 + p6t_h2T[7])
                    sy.dma_start(out=h2sp[:, :, :], in_=h2T[:]).then_inc(s_sp, 16)
                if gq >= 1:
                    sy.wait_ge(v6a, 8 * gq)
                sy.dma_start(out=x6b[:], in_=Xov[qc]).then_inc(s_x6, 16)

        @blk.tensor
        def _(t):
            t.wait_ge(s_cst, 16 * 7)
            t.wait_ge(c["g1"], 1)
            for j in range(4):
                if j > 0:
                    t.wait_ge(a6e, j)
                t.matmul(ps[4][:], ones1[:], rowst[:, j * 512:(j + 1) * 512],
                         start=True, stop=True).then_inc(p6m, 1)
            t.wait_ge(s_wld, 16 * 64)

            def h2t(g2, bat0):
                for half_b in range(2):
                    bat = bat0 + half_b
                    t.wait_ge(v6s, 2 * g2 + 2)
                    if bat >= 2:
                        t.wait_ge(a6e, 4 + bat - 1)
                    for i in range(8):
                        cc = half_b * 8 + i
                        last = t.transpose(pst[bat % 2][:, i * P:(i + 1) * P],
                                           h2bf[:, cc * P:(cc + 1) * P], ident[:])
                    last.then_inc(p6t, 1)

            for gq in range(16):
                t.wait_ge(s_avl, 16 * (gq + 1))
                for half_b in range(2):
                    bat = p6t_avT[gq] - 2 + half_b
                    if bat >= 2:
                        t.wait_ge(a6e, 4 + bat - 1)
                    for i in range(8):
                        cc = half_b * 8 + i
                        last = t.transpose(pst[bat % 2][:, i * P:(i + 1) * P],
                                           avb[gq % 2][:, cc, :], ident[:])
                    last.then_inc(p6t, 1)
                if gq >= 1:
                    h2t(gq - 1, p6t_h2T[gq - 1] - 2)
                t.wait_ge(a6e, 4 + p6t_avT[gq])
                for j in range(4):
                    n = gq * 4 + j
                    if n >= 2:
                        t.wait_ge(v6a, 2 * (n - 2) + 2)
                    for cc in range(NCD):
                        last = t.matmul(ps[4 + n % 2][:], avT[:, cc, :],
                                        Wres[:, cc, j * 512:(j + 1) * 512],
                                        start=(cc == 0), stop=(cc == NCD - 1))
                    last.then_inc(p6m, 1)
            h2t(15, p6t_h2T[15] - 2)

        @blk.scalar
        def _(s):
            for j in range(4):
                s.wait_ge(p6m, j + 1)
                s.copy(bo_bc[:, j * 512:(j + 1) * 512], ps[4][:]).then_inc(a6e, 1)

            def h2ev(g2):
                hf2, qcl2 = halves[g2 // 8], g2 % 8
                if g2 == 8:
                    s.wait_ge(s_sp, 16 * 9)
                for half_b in range(2):
                    bat = p6t_h2T[g2] - 2 + half_b
                    s.wait_ge(p6t, bat + 1)
                    s.copy(h2T[:, half_b * 8:half_b * 8 + 8, qcl2 * P:(qcl2 + 1) * P],
                           pst[bat % 2][:].rearrange("p (k d) -> p k d", d=P)).then_inc(a6e, 1)

            for gq in range(16):
                for half_b in range(2):
                    bat = p6t_avT[gq] - 2 + half_b
                    s.wait_ge(p6t, bat + 1)
                    if half_b == 0:
                        s.wait_ge(p6m, 4 + 4 * gq)
                    s.copy(avT[:, half_b * 8:half_b * 8 + 8, :],
                           pst[bat % 2][:].rearrange("p (k d) -> p k d", d=P)).then_inc(a6e, 1)
                if gq >= 1:
                    h2ev(gq - 1)
                s.wait_ge(v6a, 8 * gq + 8)
                if gq >= 1:
                    s.wait_ge(p6t, p6t_h2T[gq - 1])
                s.activation(h2bf[:], oa[:, gq % 8, :], AF.Square,
                             accum_out=co6[:, 1:2]).then_inc(a6s, 1)
                s.wait_ge(v6s, 2 * gq + 1)
                s.activation(co6[:, 5:6], co6[:, 3:4], AF.Ln)
                s.activation(co6[:, 4:5], co6[:, 5:6], AF.Exp, scale=-0.5).then_inc(a6s, 1)
            h2ev(15)

        @blk.vector
        def _(v):
            for gq in range(16):
                qcl = gq % 8
                for j in range(4):
                    n = gq * 4 + j
                    v.wait_ge(p6m, 4 + n + 1)
                    if gq == 8 and j == 0:
                        v.wait_ge(s_sp, 16 * 8)
                    v.scalar_tensor_tensor(oa[:, qcl, j * 512:(j + 1) * 512],
                                           ps[4 + n % 2][:], 1.0,
                                           bo_bc[:, j * 512:(j + 1) * 512],
                                           ALU.mult, ALU.add).then_inc(v6a, 1)
                    v.wait_ge(s_x6, 16 * (gq + 1))
                    v.scalar_tensor_tensor(oa[:, qcl, j * 512:(j + 1) * 512],
                                           oa[:, qcl, j * 512:(j + 1) * 512], 1.0,
                                           x6b[:, j * 512:(j + 1) * 512],
                                           ALU.mult, ALU.add).then_inc(v6a, 1)
                v.reduce_sum(co6[:, 0:1], oa[:, qcl, :], AX.X)
                v.wait_ge(a6s, 2 * gq + 1)
                v.tensor_scalar_mul(co6[:, 2:3], co6[:, 0:1], RN)
                v.tensor_scalar(co6[:, 3:4], co6[:, 1:2], RN, EPS, ALU.mult, ALU.add)
                v.tensor_scalar(co6[:, 5:6], co6[:, 2:3], co6[:, 2:3], None, ALU.mult)
                v.tensor_scalar(co6[:, 3:4], co6[:, 5:6], co6[:, 3:4], -1.0,
                                ALU.subtract, ALU.mult).then_inc(v6s, 1)
                v.wait_ge(a6s, 2 * gq + 2)
                v.tensor_scalar(h2bf[:], oa[:, qcl, :], co6[:, 2:3], co6[:, 4:5],
                                ALU.subtract, ALU.mult).then_inc(v6s, 1)

    flat = Wres[:].rearrange("p c d -> p (c d)")
    u_v = flat[:, 0:11264].rearrange("p (i t) -> p i t", i=NIB)
    w1b = [flat[:, 11264 + k * 2048:11264 + (k + 1) * 2048]
           .rearrange("p (c j) -> p c j", c=NCD) for k in range(2)]
    w3b = [flat[:, 15360 + k * 2048:15360 + (k + 1) * 2048]
           .rearrange("p (c j) -> p c j", c=NCD) for k in range(2)]
    w2b = [flat[:, 19456:25088].rearrange("p (i j) -> p i j", i=NIB),
           flat[:, 25088:30720].rearrange("p (i j) -> p i j", i=NIB),
           w2x]

    NGIB = NE7 * NIB
    M7 = 32 * NE7
    NW2 = 4 * NE7
    NST = 2 * NGIB
    A7H = 16 + NST

    with nc.Block("p7") as blk:
        @blk.gpsimd
        def _(g):
            for hx in range(2):
                if hx == 0:
                    g.wait_ge(p6m, 4 + 64)
                else:
                    g.wait_ge(p7a, NST)

                def w2dma(gg):
                    e2, j2 = gg // 4, gg % 4
                    if gg >= 3:
                        g.wait_ge(p7b, M7 * hx + 8 * (gg - 2))
                    elif hx == 1 and gg == 0:
                        g.wait_ge(p7b, M7)
                    g.dma_start(out=w2b[gg % 3],
                                in_=w2h[e2, :, :, j2 * 512:(j2 + 1) * 512]
                                .rearrange("i p j -> p i j")).then_inc(s_w2, 16)

                def w13dma(gib):
                    e2, ib2 = gib // NIB, gib % NIB
                    if gib >= 2:
                        g.wait_ge(p7a, NST * hx + 2 * (gib - 1))
                    elif hx == 1 and gib == 0:
                        g.wait_ge(p7a, NST)
                    g.dma_start(out=w1b[gib % 2][:], in_=w1h[e2, ib2]).then_inc(s_w13, 16)
                    g.dma_start(out=w3b[gib % 2][:], in_=w3h[e2, ib2]).then_inc(s_w13, 16)

                for gg in range(3):
                    w2dma(gg)
                cur = 0
                for gib in range(min(6, NGIB)):
                    w13dma(gib)
                    cur = gib + 1
                for e in range(NE7):
                    hi = min(e * NIB + 12, NGIB - 1)
                    while cur <= hi:
                        w13dma(cur)
                        cur += 1
                    for gg in range(4 * e + 3, min(4 * e + 7, NW2)):
                        w2dma(gg)
                while cur < NGIB:
                    w13dma(cur)
                    cur += 1

        @blk.sync
        def _(sy):
            for hx, hf in enumerate([0, 1]):
                for tc in range(8):
                    sy.wait_ge(v7y, M7 * hx + 32 * (NE7 - 1) + 24 + tc + 1)
                    sy.dma_start(out=Yv[hf * 8 + tc], in_=oa[:, tc, :]).then_inc(s_y, 16)
            sy.wait_ge(s_y, 16 * 16)

        @blk.tensor
        def _(t):
            for hx, hf in enumerate([0, 1]):
                for tc in range(8):
                    if hx == 0 and tc == 0:
                        t.wait_ge(a6e, 4 + p6t_h2T[15])
                        t.wait_ge(s_gw, 16)
                    if hx == 1 and tc == 0:
                        t.wait_ge(s_spl, 16 * 9)
                    if tc >= 2:
                        t.wait_ge(v7g, 16 * hx + 2 * (tc - 2) + 1)
                    for cc in range(NCD):
                        last = t.matmul(ps[4 + tc % 2][:, 0:15],
                                        h2T[:, cc, tc * P:(tc + 1) * P],
                                        gwt_sb[:, cc, :],
                                        start=(cc == 0), stop=(cc == NCD - 1))
                    last.then_inc(p7g, 1)
                for e in range(NE7):
                    for ib in range(NIB):
                        gib = e * NIB + ib
                        for u in range(2):
                            st = 2 * gib + u
                            ab, bb = ps[st % 2], ps[2 + st % 2]
                            if st >= 2:
                                t.wait_ge(v7u, NST * hx + st - 1)
                            t.wait_ge(s_w13, 32 * NGIB * hx + 32 * (gib + 1))
                            for cc in range(NCD):
                                t.matmul(ab[:], w1b[gib % 2][:, cc, :],
                                         h2T[:, cc, u * 512:(u + 1) * 512],
                                         start=(cc == 0), stop=(cc == NCD - 1))
                            for cc in range(NCD):
                                last = t.matmul(bb[:], w3b[gib % 2][:, cc, :],
                                                h2T[:, cc, u * 512:(u + 1) * 512],
                                                start=(cc == 0), stop=(cc == NCD - 1))
                            last.then_inc(p7a, 1)
                    t.wait_ge(v7u, NST * hx + 22 * (e + 1))
                    for j in range(4):
                        gj = 4 * e + j
                        t.wait_ge(s_w2, 16 * (NW2 * hx + gj + 1))
                        for tc in range(8):
                            cj = 32 * e + 8 * j + tc
                            yb = ps[4 + cj % 2]
                            if cj >= 2:
                                t.wait_ge(v7y, M7 * hx + cj - 1)
                            for ic in range(NIB):
                                last = t.matmul(yb[:], u_v[:, ic, tc * P:(tc + 1) * P],
                                                w2b[gj % 3][:, ic, :],
                                                start=(ic == 0), stop=(ic == NIB - 1))
                            last.then_inc(p7b, 1)

        @blk.scalar
        def _(s):
            for hx in range(2):
                if hx == 1:
                    s.wait_ge(p7a, NST)
                    for tc in range(8):
                        s.wait_ge(s_y, 16 * (tc + 1))
                        s.dma_start(out=oa[:, tc, :], in_=x1sp[tc]).then_inc(s_spl, 16)
                    s.dma_start(out=h2T[:], in_=h2sp[:, :, :]).then_inc(s_spl, 16)
                for tc in range(8):
                    s.wait_ge(v7g, 16 * hx + 2 * tc + 1)
                    s.activation(gp[:, 0:15], gl[:, 0:15], AF.Exp, bias=gc[:, 1:2],
                                 accum_out=gc[:, 2:3]).then_inc(a7s, 1)
                    s.wait_ge(v7g, 16 * hx + 2 * tc + 2)
                    s.activation(gp[:, 0:15], gp[:, 0:15], AF.Copy,
                                 scale=gc[:, 3:4]).then_inc(a7s, 1)
                for gib in range(NGIB):
                    for u in range(2):
                        st = 2 * gib + u
                        s.wait_ge(p7a, NST * hx + st + 1)
                        if st >= 2:
                            s.wait_ge(v7u, NST * hx + st - 1)
                        s.activation(sa[st % 2][:], ps[st % 2][:], AF.Silu
                                     ).then_inc(a7s, 1)

        @blk.vector
        def _(v):
            for hx in range(2):
                v.memset(comb[:, :, 15:16], 1.0)
                for tc in range(8):
                    v.wait_ge(p7g, 8 * hx + tc + 1)
                    if tc >= 1:
                        v.wait_ge(a7s, A7H * hx + 2 * (tc - 1) + 1)
                    v.tensor_scalar(gl[:, 0:15], ps[4 + tc % 2][:, 0:15], 1.0, None, ALU.mult)
                    v.reduce_max(gc[:, 1:2], gl[:, 0:15], AX.X,
                                 negate=True).then_inc(v7g, 1)
                    v.wait_ge(a7s, A7H * hx + 2 * tc + 1)
                    v.reciprocal(gc[:, 3:4], gc[:, 2:3]).then_inc(v7g, 1)
                    v.wait_ge(a7s, A7H * hx + 2 * tc + 2)
                    v.reduce_max(gc[:, 4:5], gp[:, 0:15], AX.X)
                    v.tensor_scalar_mul(gc[:, 5:6], gc[:, 4:5], 1.0)
                    v.scalar_tensor_tensor(t2b[:, 0:15], gp[:, 0:15], gc[:, 4:5],
                                           gp[:, 0:15], ALU.is_lt, ALU.mult)
                    v.tensor_scalar_mul(gc[:, 5:6], gc[:, 4:5], 1.0)
                    v.reduce_max(gc[:, 6:7], t2b[:, 0:15], AX.X)
                    v.tensor_scalar_mul(gc[:, 5:6], gc[:, 6:7], 1.0)
                    v.scalar_tensor_tensor(comb[:, tc, 0:15], gp[:, 0:15], gc[:, 6:7],
                                           gp[:, 0:15], ALU.is_ge, ALU.mult)
                for gib in range(NGIB):
                    ib = gib % NIB
                    for u in range(2):
                        st = 2 * gib + u
                        v.wait_ge(a7s, A7H * hx + 16 + st + 1)
                        if gib == 0 and u == 0 and hx == 1:
                            v.wait_ge(p7b, M7)
                        v.tensor_tensor(u_v[:, ib, u * 512:(u + 1) * 512],
                                        sa[st % 2][:], ps[2 + st % 2][:], ALU.mult
                                        ).then_inc(v7u, 1)
                    if ib == NIB - 1:
                        e = gib // NIB
                        for j in range(4):
                            for tc in range(8):
                                cj = 32 * e + 8 * j + tc
                                v.wait_ge(p7b, M7 * hx + cj + 1)
                                if e == 0 and j == 0 and tc == 0:
                                    v.wait_ge(v7g, 16 * hx + 16)
                                v.scalar_tensor_tensor(oa[:, tc, j * 512:(j + 1) * 512],
                                                       ps[4 + cj % 2][:],
                                                       comb[:, tc, e:e + 1],
                                                       oa[:, tc, j * 512:(j + 1) * 512],
                                                       ALU.mult, ALU.add).then_inc(v7y, 1)


def emit_dbg(nc, dbg, items):
    with nc.semaphore("sdbg") as sd, nc.Block("dbg") as blk:
        @blk.sync
        def _(sy):
            n = 0
            for name, src, semh, need in items:
                if name not in dbg:
                    continue
                sy.wait_ge(semh, need)
                if len(src.shape) == 3:
                    sy.dma_start(out=dbg[name][:, :, :], in_=src[:, :, :]).then_inc(sd, 16)
                else:
                    sy.dma_start(out=dbg[name][:, :, :, :], in_=src[:, :, :, :]).then_inc(sd, 16)
                n += 1
            sy.wait_ge(sd, 16 * n)


def kernel(**inp):
    X = np.asarray(inp["L0_attn_X"], dtype=np.float32)
    assert X.shape == (4, S, D)

    w1 = np.asarray(inp["L0_ffn_w1"], np.float32)
    w3 = np.asarray(inp["L0_ffn_w3"], np.float32)
    w2 = np.asarray(inp["L0_ffn_w2"], np.float32)
    w1a = np.concatenate([w1, np.asarray(inp["L0_ffn_sw1"], np.float32)[None]], axis=0)
    w3a = np.concatenate([w3, np.asarray(inp["L0_ffn_sw3"], np.float32)[None]], axis=0)
    w2a = np.concatenate([w2, np.asarray(inp["L0_ffn_sw2"], np.float32)[None]], axis=0)
    w1s = np.ascontiguousarray(w1a.reshape(NE, NIB, P, NCD, P).transpose(0, 1, 4, 3, 2))
    w3s = np.ascontiguousarray(w3a.reshape(NE, NIB, P, NCD, P).transpose(0, 1, 4, 3, 2))
    w2s = np.ascontiguousarray(w2a.reshape(NE, D, NIB, P).transpose(0, 2, 3, 1))
    gwt = np.ascontiguousarray(np.asarray(inp["L0_ffn_gate_w"], np.float32).T)

    tri = (np.arange(P)[None, :] > np.arange(P)[:, None])
    common = dict(
        wq=np.ascontiguousarray(np.asarray(inp["L0_attn_Wq"], np.float32)),
        wk=np.ascontiguousarray(np.asarray(inp["L0_attn_Wk"], np.float32)),
        wv=np.ascontiguousarray(np.asarray(inp["L0_attn_Wv"], np.float32)),
        wo=np.ascontiguousarray(np.asarray(inp["L0_attn_Wo"], np.float32)),
        bq=np.ascontiguousarray(np.asarray(inp["L0_attn_bq"], np.float32)),
        bk=np.ascontiguousarray(np.asarray(inp["L0_attn_bk"], np.float32)),
        bv=np.ascontiguousarray(np.asarray(inp["L0_attn_bv"], np.float32)),
        bo=np.ascontiguousarray(np.asarray(inp["L0_attn_bo"], np.float32)),
        lng=np.ascontiguousarray(np.asarray(inp["L0_attn_ln_g"], np.float32)),
        lnb=np.ascontiguousarray(np.asarray(inp["L0_attn_ln_b"], np.float32)),
        gwt=gwt, w1s=w1s, w3s=w3s, w2s=w2s)

    in_maps = []
    for cix in range(8):
        sq, par = cix // 2, cix % 2
        dm = np.zeros((P, 256), np.float32)
        if par == 1:
            dm[:, P:][tri] = NEG
        else:
            dm[:, :P][tri] = NEG
            dm[:, P:] = NEG
        m = dict(common)
        m["xall"] = np.ascontiguousarray(X[sq])
        m["xown"] = np.ascontiguousarray(X[sq].reshape(NTC, P, D)[par::2].reshape(TOK, D))
        m["dmask"] = dm
        in_maps.append(m)

    nc, dbg = build_program()
    res = run_bass_kernel_spmd(nc, in_maps[:CORES], core_ids=list(range(CORES)))

    if DBG or CORES < 8:
        return res

    Yf = np.empty((4, S, D), np.float32)
    for cix in range(8):
        sq, par = cix // 2, cix % 2
        Yf[sq].reshape(NTC, P, D)[par::2] = res.results[cix]["y"].reshape(NQC, P, D)
    return Yf


# version 07
# speedup vs baseline: 1.0116x; 1.0116x over previous
import numpy as np
from contextlib import ExitStack

import concourse.bass as bass
import concourse.mybir as mybir
from concourse.bass_utils import run_bass_kernel_spmd

F32 = mybir.dt.float32
BF16 = mybir.dt.bfloat16
AF = mybir.ActivationFunctionType
ALU = mybir.AluOpType
AX = mybir.AxisListType

P = 128
D = 2048
NCD = D // P
S = 4096
NTC = S // P
TOK = 2048
NQC = TOK // P
NH = 16
DH = 128
IT = 1408
NIB = IT // P
NE = 16
EPS = 1e-5
QSC = 0.08838834764831845
NEG = -1e30
RN = 1.0 / D

DBG = 0
CORES = 8
NH5 = NH
RACE = True
NE7 = NE


def tiles_of(qci):
    W = 256 * (qci + 1)
    out, k = [], 0
    while k < W:
        w = min(512, W - k)
        out.append((k, w))
        k += w
    return out


def build_program():
    nc = bass.Bass("TRN2", target_bir_lowering=False, detect_race_conditions=RACE)

    Xall = nc.dram_tensor("xall", [S, D], F32, kind="ExternalInput")
    Xown = nc.dram_tensor("xown", [TOK, D], F32, kind="ExternalInput")
    Dmk = nc.dram_tensor("dmask", [P, 256], F32, kind="ExternalInput")
    Wqh = nc.dram_tensor("wq", [D, D], F32, kind="ExternalInput")
    Wkh = nc.dram_tensor("wk", [D, D], F32, kind="ExternalInput")
    Wvh = nc.dram_tensor("wv", [D, D], F32, kind="ExternalInput")
    Woh = nc.dram_tensor("wo", [D, D], F32, kind="ExternalInput")
    bqh = nc.dram_tensor("bq", [D], F32, kind="ExternalInput")
    bkh = nc.dram_tensor("bk", [D], F32, kind="ExternalInput")
    bvh = nc.dram_tensor("bv", [D], F32, kind="ExternalInput")
    boh = nc.dram_tensor("bo", [D], F32, kind="ExternalInput")
    gh = nc.dram_tensor("lng", [D], F32, kind="ExternalInput")
    bh = nc.dram_tensor("lnb", [D], F32, kind="ExternalInput")
    gwh = nc.dram_tensor("gwt", [D, 15], F32, kind="ExternalInput")
    w1h = nc.dram_tensor("w1s", [NE, NIB, P, NCD, P], F32, kind="ExternalInput")
    w3h = nc.dram_tensor("w3s", [NE, NIB, P, NCD, P], F32, kind="ExternalInput")
    w2h = nc.dram_tensor("w2s", [NE, NIB, P, D], F32, kind="ExternalInput")
    Y = nc.dram_tensor("y", [TOK, D], F32, kind="ExternalOutput")

    xns = nc.dram_tensor("xns", [NTC, P, D], BF16)
    Ks = nc.dram_tensor("ks", [NH, P, S], BF16)
    Vs = nc.dram_tensor("vs", [NH, P, S], BF16)
    Qs = nc.dram_tensor("qs", [NH, P, TOK], BF16)
    AVs = nc.dram_tensor("avs", [NQC, P, NH, DH], BF16)
    x1sp = nc.dram_tensor("x1sp", [8, P, D], F32)
    h2sp = nc.dram_tensor("h2sp", [P, NCD, 1024], BF16)

    dbg = {}
    if DBG:
        if DBG >= 1:
            dbg["d_xns"] = nc.dram_tensor("d_xns", [NTC, P, D], BF16, kind="ExternalOutput")
        if DBG >= 2:
            dbg["d_ks"] = nc.dram_tensor("d_ks", [NH, P, S], BF16, kind="ExternalOutput")
        if DBG >= 3:
            dbg["d_vs"] = nc.dram_tensor("d_vs", [NH, P, S], BF16, kind="ExternalOutput")
        if DBG >= 4:
            dbg["d_qs"] = nc.dram_tensor("d_qs", [NH, P, TOK], BF16, kind="ExternalOutput")
        if DBG >= 5:
            dbg["d_avs"] = nc.dram_tensor("d_avs", [NQC, P, NH, DH], BF16, kind="ExternalOutput")
        if DBG == 7:
            dbg["d_x1"] = nc.dram_tensor("d_x1", [8, P, D], F32, kind="ExternalOutput")
            dbg["d_h2s"] = nc.dram_tensor("d_h2s", [P, NCD, 1024], BF16, kind="ExternalOutput")

    with ExitStack() as es:
        def sem(name):
            return es.enter_context(nc.semaphore(name))

        s_cst = sem("s_cst"); s_wld = sem("s_wld")
        s_x1 = sem("s_x1"); s_xst = sem("s_xst"); s_xg = sem("s_xg")
        s_ko = sem("s_ko"); s_vo = sem("s_vo"); s_qo = sem("s_qo"); s_x4 = sem("s_x4")
        s_kl = sem("s_kl"); s_vl = sem("s_vl"); s_ql = sem("s_ql")
        s_avo = sem("s_avo"); s_avl = sem("s_avl"); s_x6 = sem("s_x6")
        s_sp = sem("s_sp"); s_spl = sem("s_spl"); s_y = sem("s_y")
        s_w13 = sem("s_w13"); s_w2 = sem("s_w2"); s_gw = sem("s_gw")

        a1 = sem("a1"); v1 = sem("v1"); p1 = sem("p1"); g1 = sem("g1"); a1e = sem("a1e")
        pmm = sem("pmm"); vev = sem("vev")
        a4 = sem("a4"); v4 = sem("v4"); p4 = sem("p4"); g4 = sem("g4"); a4e = sem("a4e")
        p5s = sem("p5s"); p5t = sem("p5t"); p5a = sem("p5a"); p5q = sem("p5q"); v5c = sem("v5c")
        v5e = sem("v5e"); v5m = sem("v5m"); a5x = sem("a5x"); v5r = sem("v5r")
        a5e = sem("a5e"); v5pe = sem("v5pe")
        p6t = sem("p6t"); a6e = sem("a6e"); p6m = sem("p6m"); v6a = sem("v6a")
        g6b = sem("g6b"); a6s = sem("a6s"); v6s = sem("v6s"); a6h = sem("a6h")
        p7g = sem("p7g"); v7g = sem("v7g"); p7a = sem("p7a"); a7s = sem("a7s")
        v7u = sem("v7u"); p7b = sem("p7b"); v7y = sem("v7y")

        Wres = es.enter_context(nc.sbuf_tensor("wres", [P, NCD, D], BF16, side="right"))
        sbt = lambda *a: es.enter_context(nc.sbuf_tensor(*a))
        ident = sbt("ident", [P, P], BF16)
        ones1 = sbt("ones1", [1, P], F32)
        rowst = sbt("rowst", [1, D], F32)
        bkc = sbt("bkc", [P, NCD], F32)
        bvc = sbt("bvc", [P, NCD], F32)
        bqc = sbt("bqc", [P, NCD], F32)
        gwt_sb = sbt("gwt_sb", [P, NCD, 15], BF16)
        dmk_sb = sbt("dmk_sb", [P, 256], F32)

        ps = [es.enter_context(nc.psum_tensor(f"ps{i}", [P, 512], F32)) for i in range(6)]
        pst = [es.enter_context(nc.psum_tensor(f"pst{i}", [P, 1024], BF16)) for i in range(2)]

        with nc.Block("p0") as blk:
            @blk.sync
            def _(sy):
                sy.dma_start(out=bkc[:], in_=bkh.rearrange("(c p) -> p c", p=P),
                             allow_slow_non_contiguous=True).then_inc(s_cst, 16)
                sy.dma_start(out=bvc[:], in_=bvh.rearrange("(c p) -> p c", p=P),
                             allow_slow_non_contiguous=True).then_inc(s_cst, 16)
                sy.dma_start(out=bqc[:], in_=bqh.rearrange("(c p) -> p c", p=P),
                             allow_slow_non_contiguous=True).then_inc(s_cst, 16)
                sy.dma_start(out=dmk_sb[:], in_=Dmk[:, :]).then_inc(s_cst, 16)

            @blk.gpsimd
            def _(g):
                g.dma_start(out=gwt_sb[:], in_=gwh.rearrange("(c p) n -> p c n", p=P)
                            ).then_inc(s_gw, 16)
                g.memset(ident[:], 0.0)
                g.affine_select(out=ident[:], in_=ident[:], compare_op=ALU.not_equal,
                                fill=1.0, base=0, pattern=[[-1, P]], channel_multiplier=1)
                g.memset(ones1[:], 1.0).then_inc(g1, 1)

        with ExitStack() as es14:
            sb14 = lambda *a: es14.enter_context(nc.sbuf_tensor(*a))
            g_bc = sb14("g_bc", [P, D], F32)
            b_bc = sb14("b_bc", [P, D], F32)

            with nc.Block("p0b") as blk:
                @blk.sync
                def _(sy):
                    sy.dma_start(out=rowst[:], in_=gh[None, :]).then_inc(s_cst, 16)

                @blk.tensor
                def _(t):
                    t.wait_ge(s_cst, 16 * 5)
                    t.wait_ge(g1, 1)
                    for j in range(4):
                        t.matmul(ps[j][:], ones1[:], rowst[:, j * 512:(j + 1) * 512],
                                 start=True, stop=True).then_inc(p1, 1)

                @blk.scalar
                def _(s):
                    for j in range(4):
                        s.wait_ge(p1, j + 1)
                        s.copy(g_bc[:, j * 512:(j + 1) * 512], ps[j][:]).then_inc(v1, 1)

                @blk.sync
                def _(sy):
                    sy.wait_ge(p1, 4)
                    sy.dma_start(out=rowst[:], in_=bh[None, :]).then_inc(s_cst, 16)

                @blk.tensor
                def _(t):
                    t.wait_ge(s_cst, 16 * 6)
                    t.wait_ge(v1, 4)
                    for j in range(4):
                        t.matmul(ps[j][:], ones1[:], rowst[:, j * 512:(j + 1) * 512],
                                 start=True, stop=True).then_inc(p1, 1)

                @blk.scalar
                def _(s):
                    for j in range(4):
                        s.wait_ge(p1, 4 + j + 1)
                        s.copy(b_bc[:, j * 512:(j + 1) * 512], ps[j][:]).then_inc(v1, 1)

            with ExitStack() as e1:
                emit_ln_pass(nc, e1, dict(
                    tag="p1", src=Xall, ntc=NTC, xns=xns, g_bc=g_bc, b_bc=b_bc,
                    ident=ident, pst=pst,
                    s_x=s_x1, s_st=s_xst, va=v1, aa=a1, pa=p1, ga=g1, ae=a1e,
                    v_base=8, a_base=0, p_base=8, g_base=1, e_base=0, st_base=0,
                    prefetch=(Wkh, Wres, s_wld)))

            with ExitStack() as e23:
                emit_proj_pass(nc, e23, dict(
                    tag="pkv", ps=ps, Wres=Wres, scale=None,
                    segments=[
                        dict(dst=Ks, bias=bkc, s_out=s_ko, wld_base=0),
                        dict(dst=Vs, bias=bvc, s_out=s_vo, wld_base=4)],
                    xsrc=xns, xst_sem=s_xst, xst_gate=16 * 32,
                    s_xg=s_xg, pmm=pmm, vev=vev, s_wld=s_wld,
                    prefetches={0: (Wvh, Wres, s_wld, pmm, 128),
                                8: (Wqh, Wres, s_wld, pmm, 256)}))

            with ExitStack() as e4:
                emit_p4(nc, e4, dict(
                    ps=ps, Xown=Xown, Qs=Qs, g_bc=g_bc, b_bc=b_bc, ident=ident, pst=pst,
                    bqc=bqc, Wres=Wres, s_wld=s_wld,
                    s_x4=s_x4, s_qo=s_qo, a4=a4, v4=v4, p4=p4, g4=g4, a4e=a4e,
                    pmm=pmm, vev=vev, mm_base=256, ev_base=256,
                    gate_sem=s_vo, gate_val=16 * 8,
                    prefetch=(Woh, Wres, s_wld, pmm, 320)))

        if DBG and DBG <= 4:
            emit_dbg(nc, dbg, [
                ("d_xns", xns, s_xst, 16 * NTC),
                ("d_ks", Ks, s_ko, 16 * 8),
                ("d_vs", Vs, s_vo, 16 * 8),
                ("d_qs", Qs, s_qo, 16 * 4)])
            return nc, dbg

        with ExitStack() as es5:
            emit_p5(nc, es5, dict(
                Ks=Ks, Vs=Vs, Qs=Qs, AVs=AVs, dmk=dmk_sb, ident=ident,
                ps=ps, pst=pst,
                s_ko=s_ko, s_vo=s_vo, s_qo=s_qo, s_kl=s_kl, s_vl=s_vl, s_ql=s_ql,
                s_avo=s_avo, s_cst=s_cst,
                p5s=p5s, p5t=p5t, p5a=p5a, p5q=p5q, v5e=v5e, v5m=v5m,
                a5x=a5x, v5r=v5r, a5e=a5e, v5pe=v5pe, v5c=v5c))

        if DBG == 5:
            emit_dbg(nc, dbg, [("d_avs", AVs, s_avo, 16 * NH5)])
            return nc, dbg

        with ExitStack() as es67:
            emit_p67(nc, es67, dict(
                AVs=AVs, Xown=Xown, Y=Y, boh=boh, ones1=ones1, rowst=rowst,
                ident=ident, ps=ps, pst=pst, Wres=Wres, gwt_sb=gwt_sb,
                w1h=w1h, w3h=w3h, w2h=w2h, x1sp=x1sp, h2sp=h2sp,
                s_avo=s_avo, s_avl=s_avl, s_x6=s_x6, s_sp=s_sp, s_spl=s_spl,
                s_y=s_y, s_w13=s_w13, s_w2=s_w2, s_wld=s_wld, s_cst=s_cst, s_gw=s_gw,
                p6t=p6t, a6e=a6e, p6m=p6m, v6a=v6a, a6s=a6s, v6s=v6s, v5e=v5e,
                g1=g1, p7g=p7g, v7g=v7g, p7a=p7a, a7s=a7s, v7u=v7u,
                p7b=p7b, v7y=v7y))

        if DBG == 7:
            emit_dbg(nc, dbg, [("d_x1", x1sp, s_sp, 16 * 9),
                               ("d_h2s", h2sp, s_sp, 16 * 9)])

    return nc, dbg


def emit_ln_pass(nc, es, c):
    sb = lambda *a: es.enter_context(nc.sbuf_tensor(*a))
    tg = c["tag"]
    xbuf = [sb(f"{tg}x{i}", [P, D], F32) for i in range(2)]
    hbuf = [sb(f"{tg}h{i}", [P, D], F32) for i in range(2)]
    tbuf = [sb(f"{tg}t{i}", [P, D], F32) for i in range(2)]
    xnb = [sb(f"{tg}n{i}", [P, D], BF16) for i in range(2)]
    xtb = [sb(f"{tg}T{i}", [P, D], BF16) for i in range(2)]
    garb = sb(f"{tg}garb", [P, D], F32)
    cols = [sb(f"{tg}c{i}", [P, 8], F32) for i in range(2)]
    ntc = c["ntc"]
    src, xns = c["src"], c["xns"]
    s_x, s_st = c["s_x"], c["s_st"]
    va, aa, pa, ga, ae = c["va"], c["aa"], c["pa"], c["ga"], c["ae"]
    vb, ab, pb, gb, eb, stb = (c["v_base"], c["a_base"], c["p_base"],
                               c["g_base"], c["e_base"], c["st_base"])
    g_bc, b_bc, ident, pst = c["g_bc"], c["b_bc"], c["ident"], c["pst"]
    srcv = src.rearrange("(tc p) d -> tc p d", p=P)

    with nc.Block(c["tag"]) as blk:
        @blk.gpsimd
        def _(g):
            wh, Wd, s_wld = c["prefetch"]
            for j in range(4):
                g.dma_start(out=Wd[:, :, j * 512:(j + 1) * 512],
                            in_=wh[:, j * 512:(j + 1) * 512]
                            .rearrange("(c p) n -> p c n", p=P)).then_inc(s_wld, 16)
            for tc in range(ntc):
                pbuf = tc % 2
                g.wait_ge(va, vb + 4 * tc + 4)
                if tc >= 2:
                    g.wait_ge(pa, pb + 2 * (tc - 2) + 2)
                g.tensor_tensor(xnb[pbuf][:], hbuf[pbuf][:], b_bc[:], ALU.add
                                ).then_inc(ga, 1)

        @blk.sync
        def _(sy):
            for tc in range(ntc):
                pbuf = tc % 2
                if tc >= 2:
                    sy.wait_ge(va, vb + 4 * (tc - 2) + 2)
                sy.dma_start(out=xbuf[pbuf][:], in_=srcv[tc]).then_inc(s_x, 16)
                if tc >= 1:
                    sy.wait_ge(ae, eb + 2 * tc)
                    sy.dma_start(out=xns[tc - 1], in_=xtb[(tc - 1) % 2][:]
                                 ).then_inc(s_st, 16)
            sy.wait_ge(ae, eb + 2 * ntc)
            sy.dma_start(out=xns[ntc - 1], in_=xtb[(ntc - 1) % 2][:]).then_inc(s_st, 16)

        @blk.scalar
        def _(s):
            for tc in range(ntc):
                pbuf = tc % 2
                co = cols[pbuf]
                s.wait_ge(s_x, 16 * (tc + 1))
                s.activation(garb[:], xbuf[pbuf][:], AF.Square,
                             accum_out=co[:, 1:2]).then_inc(aa, 1)
                s.wait_ge(va, vb + 4 * tc + 1)
                s.activation(co[:, 5:6], co[:, 3:4], AF.Ln)
                s.activation(co[:, 4:5], co[:, 5:6], AF.Exp, scale=-0.5).then_inc(aa, 1)
                s.wait_ge(va, vb + 4 * tc + 2)
                s.activation(garb[:], hbuf[pbuf][:], AF.Square,
                             accum_out=co[:, 1:2]).then_inc(aa, 1)
                s.wait_ge(va, vb + 4 * tc + 3)
                s.activation(co[:, 5:6], co[:, 3:4], AF.Ln)
                s.activation(co[:, 4:5], co[:, 5:6], AF.Exp, scale=-0.5).then_inc(aa, 1)
                s.wait_ge(pa, pb + 2 * tc + 1)
                if tc >= 2:
                    s.wait_ge(s_st, stb + 16 * (tc - 1))
                s.copy(xtb[pbuf][:, 0:1024], pst[0][:]).then_inc(ae, 1)
                s.wait_ge(pa, pb + 2 * tc + 2)
                s.copy(xtb[pbuf][:, 1024:2048], pst[1][:]).then_inc(ae, 1)

        @blk.vector
        def _(v):
            for tc in range(ntc):
                pbuf = tc % 2
                co = cols[pbuf]
                v.wait_ge(s_x, 16 * (tc + 1))
                v.reduce_sum(co[:, 0:1], xbuf[pbuf][:], AX.X)
                v.wait_ge(aa, ab + 4 * tc + 1)
                v.tensor_scalar_mul(co[:, 2:3], co[:, 0:1], RN)
                v.tensor_scalar(co[:, 3:4], co[:, 1:2], RN, EPS, ALU.mult, ALU.add)
                v.tensor_scalar(co[:, 5:6], co[:, 2:3], co[:, 2:3], None, ALU.mult)
                v.tensor_scalar(co[:, 3:4], co[:, 5:6], co[:, 3:4], -1.0,
                                ALU.subtract, ALU.mult).then_inc(va, 1)
                v.wait_ge(aa, ab + 4 * tc + 2)
                if tc >= 2:
                    v.wait_ge(ga, gb + tc - 1)
                v.tensor_scalar(hbuf[pbuf][:], xbuf[pbuf][:], co[:, 2:3], co[:, 4:5],
                                ALU.subtract, ALU.mult).then_inc(va, 1)
                v.reduce_sum(co[:, 0:1], hbuf[pbuf][:], AX.X)
                v.wait_ge(aa, ab + 4 * tc + 3)
                v.tensor_scalar_mul(co[:, 2:3], co[:, 0:1], RN)
                v.tensor_scalar(co[:, 3:4], co[:, 1:2], RN, EPS, ALU.mult, ALU.add)
                v.tensor_scalar(co[:, 5:6], co[:, 2:3], co[:, 2:3], None, ALU.mult)
                v.tensor_scalar(co[:, 3:4], co[:, 5:6], co[:, 3:4], -1.0,
                                ALU.subtract, ALU.mult).then_inc(va, 1)
                v.wait_ge(aa, ab + 4 * tc + 4)
                v.tensor_scalar(tbuf[pbuf][:], hbuf[pbuf][:], co[:, 2:3], co[:, 4:5],
                                ALU.subtract, ALU.mult)
                v.tensor_tensor(hbuf[pbuf][:], tbuf[pbuf][:], g_bc[:], ALU.mult
                                ).then_inc(va, 1)

        @blk.tensor
        def _(t):
            for tc in range(ntc):
                pbuf = tc % 2
                t.wait_ge(ga, gb + tc + 1)
                for half in range(2):
                    bidx = 2 * tc + half
                    if bidx >= 2:
                        t.wait_ge(ae, eb + bidx - 1)
                    for cc in range(8):
                        ccc = half * 8 + cc
                        last = t.transpose(pst[bidx % 2][:, cc * P:(cc + 1) * P],
                                           xnb[pbuf][:, ccc * P:(ccc + 1) * P], ident[:])
                    last.then_inc(pa, 1)


def emit_proj_pass(nc, es, c):
    sb = lambda *a: es.enter_context(nc.sbuf_tensor(*a))
    tg = c["tag"]
    xg = [sb(f"{tg}xg{i}", [P, NCD, 512], BF16) for i in range(2)]
    ob = [sb(f"{tg}ob{i}", [P, NCD, 512], BF16) for i in range(2)]
    ps = c["ps"]
    Wres = c["Wres"]
    pmm, vev = c["pmm"], c["vev"]
    s_xg, s_wld = c["s_xg"], c["s_wld"]
    xsv = c["xsrc"].rearrange("tc p (c j) -> tc p c j", c=NCD)
    segs = c["segments"]
    ngr = 8 * len(segs)

    def seg(gi):
        return segs[gi // 8], gi % 8

    with nc.Block(tg) as blk:
        @blk.gpsimd
        def _(g):
            for at in sorted(c["prefetches"]):
                wh, Wd, s_wldN, gate_sem, gate_val = c["prefetches"][at]
                g.wait_ge(gate_sem, gate_val)
                for j in range(4):
                    g.dma_start(out=Wd[:, :, j * 512:(j + 1) * 512],
                                in_=wh[:, j * 512:(j + 1) * 512]
                                .rearrange("(c p) n -> p c n", p=P)).then_inc(s_wldN, 16)

        @blk.sync
        def _(sy):
            for gi in range(ngr):
                sg, gl = seg(gi)
                pbuf = gi % 2
                if gi == 0:
                    sy.wait_ge(c["xst_sem"], c["xst_gate"])
                if gi >= 2:
                    sy.wait_ge(pmm, 16 * (gi - 1))
                for u in range(4):
                    sy.dma_start(out=xg[pbuf][:, :, u * P:(u + 1) * P],
                                 in_=xsv[4 * gl + u]).then_inc(s_xg, 16)
                if gi >= 1:
                    psg, pgl = seg(gi - 1)
                    sy.wait_ge(vev, 16 * gi)
                    sy.dma_start(out=psg["dst"].rearrange("h p (g j) -> g p h j", j=512)[pgl],
                                 in_=ob[(gi - 1) % 2][:]).then_inc(psg["s_out"], 16)
            psg, pgl = seg(ngr - 1)
            sy.wait_ge(vev, 16 * ngr)
            sy.dma_start(out=psg["dst"].rearrange("h p (g j) -> g p h j", j=512)[pgl],
                         in_=ob[(ngr - 1) % 2][:]).then_inc(psg["s_out"], 16)

        @blk.tensor
        def _(t):
            for gi in range(ngr):
                sg, gl = seg(gi)
                pbuf = gi % 2
                t.wait_ge(s_xg, 16 * (4 * gi + 4))
                for do in range(NCD):
                    gd = gi * 16 + do
                    bank = ps[do % 4]
                    t.wait_ge(s_wld, 16 * (sg["wld_base"] + do // 4 + 1))
                    if gd >= 4:
                        t.wait_ge(vev, gd - 3)
                    for cc in range(NCD):
                        last = t.matmul(bank[:], Wres[:, cc, do * P:(do + 1) * P],
                                        xg[pbuf][:, cc, :],
                                        start=(cc == 0), stop=(cc == NCD - 1))
                    last.then_inc(pmm, 1)

        @blk.vector
        def _(v):
            nst = {id(sg): 0 for sg in segs}
            for gi in range(ngr):
                sg, gl = seg(gi)
                pbuf = gi % 2
                for do in range(NCD):
                    gd = gi * 16 + do
                    v.wait_ge(pmm, gd + 1)
                    if gi >= 2 and do == 0:
                        psg, pgl = seg(gi - 2)
                        v.wait_ge(psg["s_out"], 16 * (pgl + 1))
                    v.tensor_scalar(ob[pbuf][:, do, :], ps[do % 4][:],
                                    sg["bias"][:, do:do + 1], None, ALU.add
                                    ).then_inc(vev, 1)


def emit_p4(nc, es, c):
    sb = lambda *a: es.enter_context(nc.sbuf_tensor(*a))
    xbuf = [sb(f"p4x{i}", [P, D], F32) for i in range(2)]
    hbuf = [sb(f"p4h{i}", [P, D], F32) for i in range(2)]
    tbuf = [sb(f"p4t{i}", [P, D], F32) for i in range(2)]
    xnb = [sb(f"p4n{i}", [P, D], BF16) for i in range(2)]
    xg = [sb("p4g0", [P, NCD, 512], BF16)]
    ob = [sb("p4o0", [P, NCD, 512], BF16)]
    garb = sb("p4garb", [P, D], F32)
    cols = [sb(f"p4c{i}", [P, 8], F32) for i in range(2)]
    ps = c["ps"]
    g_bc, b_bc, ident, pst = c["g_bc"], c["b_bc"], c["ident"], c["pst"]
    Xov = c["Xown"].rearrange("(tc p) d -> tc p d", p=P)
    Qs, bqc, Wres = c["Qs"], c["bqc"], c["Wres"]
    a4, v4, p4, g4, a4e = c["a4"], c["v4"], c["p4"], c["g4"], c["a4e"]
    pmm, vev = c["pmm"], c["vev"]
    mm_b, ev_b = c["mm_base"], c["ev_base"]
    s_x4, s_qo, s_wld = c["s_x4"], c["s_qo"], c["s_wld"]
    qsv = Qs.rearrange("h p (g j) -> g p h j", j=512)

    with nc.Block("p4") as blk:
        @blk.gpsimd
        def _(g):
            for tc in range(NQC):
                pbuf = tc % 2
                g.wait_ge(v4, 4 * tc + 4)
                if tc >= 2:
                    g.wait_ge(p4, 2 * (tc - 2) + 2)
                g.tensor_tensor(xnb[pbuf][:], hbuf[pbuf][:], b_bc[:], ALU.add
                                ).then_inc(g4, 1)
            wh, Wd, s_wldN, gate_sem, gate_val = c["prefetch"]
            g.wait_ge(gate_sem, gate_val)
            for j in range(4):
                g.dma_start(out=Wd[:, :, j * 512:(j + 1) * 512],
                            in_=wh[:, j * 512:(j + 1) * 512]
                            .rearrange("(c p) n -> p c n", p=P)).then_inc(s_wldN, 16)

        @blk.sync
        def _(sy):
            for tc in range(NQC):
                pbuf = tc % 2
                if tc == 0:
                    sy.wait_ge(c["gate_sem"], c["gate_val"])
                if tc >= 2:
                    sy.wait_ge(v4, 4 * (tc - 2) + 2)
                sy.dma_start(out=xbuf[pbuf][:], in_=Xov[tc]).then_inc(s_x4, 16)
                if tc % 4 == 3 and tc >= 7:
                    gi = tc // 4 - 1
                    sy.wait_ge(vev, ev_b + 16 * (gi + 1))
                    sy.dma_start(out=qsv[gi], in_=ob[0][:]).then_inc(s_qo, 16)
            sy.wait_ge(vev, ev_b + 64)
            sy.dma_start(out=qsv[3], in_=ob[0][:]).then_inc(s_qo, 16)

        @blk.scalar
        def _(s):
            for tc in range(NQC):
                pbuf = tc % 2
                co = cols[pbuf]
                s.wait_ge(s_x4, 16 * (tc + 1))
                s.activation(garb[:], xbuf[pbuf][:], AF.Square, accum_out=co[:, 1:2]
                             ).then_inc(a4, 1)
                s.wait_ge(v4, 4 * tc + 1)
                s.activation(co[:, 5:6], co[:, 3:4], AF.Ln)
                s.activation(co[:, 4:5], co[:, 5:6], AF.Exp, scale=-0.5).then_inc(a4, 1)
                s.wait_ge(v4, 4 * tc + 2)
                s.activation(garb[:], hbuf[pbuf][:], AF.Square, accum_out=co[:, 1:2]
                             ).then_inc(a4, 1)
                s.wait_ge(v4, 4 * tc + 3)
                s.activation(co[:, 5:6], co[:, 3:4], AF.Ln)
                s.activation(co[:, 4:5], co[:, 5:6], AF.Exp, scale=-0.5).then_inc(a4, 1)
                gi, u = tc // 4, tc % 4
                dxg = xg[0]
                s.wait_ge(p4, 2 * tc + 1)
                if gi >= 1 and u == 0:
                    s.wait_ge(pmm, mm_b + 16 * gi)
                s.copy(dxg[:, 0:8, u * P:(u + 1) * P],
                              pst[0][:].rearrange("p (c j) -> p c j", c=8)).then_inc(a4e, 1)
                s.wait_ge(p4, 2 * tc + 2)
                s.copy(dxg[:, 8:16, u * P:(u + 1) * P],
                              pst[1][:].rearrange("p (c j) -> p c j", c=8)).then_inc(a4e, 1)

        @blk.vector
        def _(v):
            for tc in range(NQC):
                pbuf = tc % 2
                co = cols[pbuf]
                v.wait_ge(s_x4, 16 * (tc + 1))
                v.reduce_sum(co[:, 0:1], xbuf[pbuf][:], AX.X)
                v.wait_ge(a4, 4 * tc + 1)
                v.tensor_scalar_mul(co[:, 2:3], co[:, 0:1], RN)
                v.tensor_scalar(co[:, 3:4], co[:, 1:2], RN, EPS, ALU.mult, ALU.add)
                v.tensor_scalar(co[:, 5:6], co[:, 2:3], co[:, 2:3], None, ALU.mult)
                v.tensor_scalar(co[:, 3:4], co[:, 5:6], co[:, 3:4], -1.0,
                                ALU.subtract, ALU.mult).then_inc(v4, 1)
                v.wait_ge(a4, 4 * tc + 2)
                if tc >= 2:
                    v.wait_ge(g4, tc - 1)
                v.tensor_scalar(hbuf[pbuf][:], xbuf[pbuf][:], co[:, 2:3], co[:, 4:5],
                                ALU.subtract, ALU.mult).then_inc(v4, 1)
                v.reduce_sum(co[:, 0:1], hbuf[pbuf][:], AX.X)
                v.wait_ge(a4, 4 * tc + 3)
                v.tensor_scalar_mul(co[:, 2:3], co[:, 0:1], RN)
                v.tensor_scalar(co[:, 3:4], co[:, 1:2], RN, EPS, ALU.mult, ALU.add)
                v.tensor_scalar(co[:, 5:6], co[:, 2:3], co[:, 2:3], None, ALU.mult)
                v.tensor_scalar(co[:, 3:4], co[:, 5:6], co[:, 3:4], -1.0,
                                ALU.subtract, ALU.mult).then_inc(v4, 1)
                v.wait_ge(a4, 4 * tc + 4)
                v.tensor_scalar(tbuf[pbuf][:], hbuf[pbuf][:], co[:, 2:3], co[:, 4:5],
                                ALU.subtract, ALU.mult)
                v.tensor_tensor(hbuf[pbuf][:], tbuf[pbuf][:], g_bc[:], ALU.mult
                                ).then_inc(v4, 1)
                if tc % 4 == 3:
                    gi = tc // 4
                    for do in range(NCD):
                        gd = gi * 16 + do
                        v.wait_ge(pmm, mm_b + gd + 1)
                        if gi >= 1 and do == 0:
                            v.wait_ge(s_qo, 16 * gi)
                        v.tensor_scalar(ob[0][:, do, :], ps[do % 4][:],
                                        bqc[:, do:do + 1], QSC, ALU.add, ALU.mult
                                        ).then_inc(vev, 1)

        @blk.tensor
        def _(t):
            for tc in range(NQC):
                pbuf = tc % 2
                t.wait_ge(g4, tc + 1)
                for half in range(2):
                    bidx = 2 * tc + half
                    if bidx >= 2:
                        t.wait_ge(a4e, bidx - 1)
                    for cc in range(8):
                        ccc = half * 8 + cc
                        last = t.transpose(pst[bidx % 2][:, cc * P:(cc + 1) * P],
                                           xnb[pbuf][:, ccc * P:(ccc + 1) * P], ident[:])
                    last.then_inc(p4, 1)
                if tc % 4 == 3:
                    gi = tc // 4
                    t.wait_ge(a4e, 2 * (tc + 1))
                    for do in range(NCD):
                        gd = gi * 16 + do
                        bank = ps[do % 4]
                        t.wait_ge(s_wld, 16 * (8 + do // 4 + 1))
                        if gd >= 4:
                            t.wait_ge(vev, ev_b + gd - 3)
                        for cc in range(NCD):
                            last = t.matmul(bank[:], Wres[:, cc, do * P:(do + 1) * P],
                                            xg[0][:, cc, :],
                                            start=(cc == 0), stop=(cc == NCD - 1))
                        last.then_inc(pmm, 1)


def emit_p5(nc, es, c):
    sb = lambda *a: es.enter_context(nc.sbuf_tensor(*a))
    ktb = [sb(f"p5k{i}", [P, S], BF16) for i in range(2)]
    vtb = [sb(f"p5v{i}", [P, S], BF16) for i in range(2)]
    qb = [sb(f"p5q{i}", [P, TOK], BF16) for i in range(2)]
    vtok = sb("p5vt", [P, NTC, P], BF16)
    Ssb = [sb(f"p5s{i}", [P, S], F32) for i in range(2)]
    Psb = [sb(f"p5p{i}", [P, S], BF16) for i in range(2)]
    ptb_sb = sb("p5pt", [P, 8, P], BF16)
    avh = [sb(f"p5a{i}", [P, NQC, DH], BF16) for i in range(2)]
    mx = [sb(f"p5m{i}", [P, 8], F32) for i in range(2)]
    col = [sb(f"p5c{i}", [P, 4], F32) for i in range(2)]
    ps, pst = c["ps"], c["pst"]
    Ks, Vs, Qs, AVs = c["Ks"], c["Vs"], c["Qs"], c["AVs"]
    dmk, ident = c["dmk"], c["ident"]
    s_kl, s_vl, s_ql, s_avo = c["s_kl"], c["s_vl"], c["s_ql"], c["s_avo"]
    s_cst = c["s_cst"]
    p5s, p5t, p5a = c["p5s"], c["p5t"], c["p5a"]
    v5e, v5m, a5x, v5r, a5e = c["v5e"], c["v5m"], c["a5x"], c["v5r"], c["a5e"]
    v5c = c["v5c"]
    avv = AVs.rearrange("q p h d -> p q h d")

    tile_of_E = {}
    gt = 0
    for h in range(NH5):
        for qci in range(NQC):
            E = h * NQC + qci
            tl = []
            for (k0, w) in tiles_of(qci):
                tl.append((gt, k0, w))
                gt += 1
            tile_of_E[E] = tl

    batches = []
    bodyb_of_E = {}
    av_cnt = 0
    ptb_cnt = 0
    av_through_E = {}
    bt_through_E = {}

    def emit_body_plan(E):
        nonlocal av_cnt, ptb_cnt
        qci = E % NQC
        nkc = 2 * (qci + 1)
        bl = []
        k = 0
        while k < nkc:
            kcs = list(range(k, min(k + 4, nkc)))
            b = dict(kind="pt", E=E, h=E // NQC, gb=len(batches), ptb=ptb_cnt,
                     kcs=kcs, av_idx=av_cnt)
            batches.append(b)
            bl.append(b)
            ptb_cnt += 1
            av_cnt += 1
            k += 4
        bodyb_of_E[E] = bl
        av_through_E[E] = av_cnt
        bt_through_E[E] = len(batches)

    av_through_head = {-1: 0}
    for h in range(NH5):
        if h > 0:
            emit_body_plan(h * NQC - 1)
            av_through_head[h - 1] = av_cnt
        for b in range(8):
            batches.append(dict(kind="vt", h=h, b=b, gb=len(batches),
                                kcs=list(range(4 * b, 4 * b + 4))))
        for qci in range(1, NQC):
            emit_body_plan(h * NQC + qci - 1)
    emit_body_plan(NH5 * NQC - 1)
    av_through_head[NH5 - 1] = av_cnt

    def w(eng, semh, v):
        if v > 0:
            eng.wait_ge(semh, v)

    with nc.Block("p5") as blk:
        @blk.sync
        def _(sy):
            def loads(h):
                sy.dma_start(out=ktb[h % 2][:], in_=Ks[h]).then_inc(s_kl, 16)
                sy.dma_start(out=vtb[h % 2][:], in_=Vs[h]).then_inc(s_vl, 16)
                sy.dma_start(out=qb[h % 2][:], in_=Qs[h]).then_inc(s_ql, 16)
            sy.wait_ge(c["s_ko"], 16 * 8)
            sy.wait_ge(c["s_vo"], 16 * 8)
            sy.wait_ge(c["s_qo"], 16 * 4)
            loads(0)
            loads(1)
            for h in range(NH5):
                if h + 2 < NH5:
                    w(sy, p5a, av_through_head[h])
                    loads(h + 2)
                sy.wait_ge(v5r, NQC * (h + 1))
                sy.dma_start(out=avv[:, :, h, :], in_=avh[h % 2][:]).then_inc(s_avo, 16)

        @blk.tensor
        def _(t):
            def body(E):
                h, qci = E // NQC, E % NQC
                t.wait_ge(a5x, E + 1)
                w(t, v5r, E - 1)
                avb = ps[2 + E % 2]
                nkc = 2 * (qci + 1)
                def avmm(b):
                    t.wait_ge(a5e, b["gb"] + 1)
                    for i, kc in enumerate(b["kcs"]):
                        last = t.matmul(avb[:, 0:DH],
                                        ptb_sb[:, (b["ptb"] % 2) * 4 + i, :],
                                        vtok[:, kc, :],
                                        start=(kc == 0), stop=(kc == nkc - 1))
                    last.then_inc(p5a, 1)

                bl = bodyb_of_E[E]
                for bi, b in enumerate(bl):
                    gb = b["gb"]
                    if gb >= 2:
                        t.wait_ge(a5e, gb - 1)
                    for i, kc in enumerate(b["kcs"]):
                        last = t.transpose(
                            pst[gb % 2][:, i * P:(i + 1) * P],
                            Psb[E % 2][:, kc * P:(kc + 1) * P], ident[:])
                    last.then_inc(p5t, 1)
                    if bi >= 1:
                        avmm(bl[bi - 1])
                avmm(bl[-1])

            for h in range(NH5):
                if h > 0:
                    body(h * NQC - 1)
                t.wait_ge(s_vl, 16 * (h + 1))
                vts = [bt for bt in batches if bt["kind"] == "vt" and bt["h"] == h]
                for bt in vts:
                    gb = bt["gb"]
                    if gb >= 2:
                        t.wait_ge(a5e, gb - 1)
                    for i, kc in enumerate(bt["kcs"]):
                        last = t.transpose(
                            pst[gb % 2][:, i * P:(i + 1) * P],
                            vtb[h % 2][:, kc * P:(kc + 1) * P], ident[:])
                    last.then_inc(p5t, 1)
                t.wait_ge(s_kl, 16 * (h + 1))
                t.wait_ge(s_ql, 16 * (h + 1))

                sbanks = (0, 1, 4, 5)

                def scores(qci):
                    E = h * NQC + qci
                    for (gtx, k0, wd) in tile_of_E[E]:
                        w(t, v5e, gtx - 3)
                        t.matmul(ps[sbanks[gtx % 4]][:, 0:wd],
                                 qb[h % 2][:, qci * P:(qci + 1) * P],
                                 ktb[h % 2][:, k0:k0 + wd],
                                 start=True, stop=True).then_inc(p5s, 1)

                scores(0)
                scores(1)
                for qci in range(NQC):
                    if qci + 2 < NQC:
                        scores(qci + 2)
                    if qci < NQC - 1:
                        body(h * NQC + qci)
            body(NH5 * NQC - 1)

        @blk.scalar
        def _(s):
            def evac(bt):
                gb = bt["gb"]
                s.wait_ge(p5t, gb + 1)
                n = len(bt["kcs"])
                if bt["kind"] == "vt":
                    w(s, p5a, av_through_head[bt["h"] - 1])
                    dst = vtok[:, bt["b"] * 4:bt["b"] * 4 + n, :]
                else:
                    w(s, p5a, bt["ptb"] - 1)
                    dst = ptb_sb[:, (bt["ptb"] % 2) * 4:(bt["ptb"] % 2) * 4 + n, :]
                s.copy(dst, pst[gb % 2][:, 0:n * P]
                       .rearrange("p (k d) -> p k d", d=P)).then_inc(a5e, 1)

            def avevac(E):
                h2, qci2 = E // NQC, E % NQC
                s.wait_ge(v5c, E + 1)
                s.wait_ge(p5a, av_through_E[E])
                if qci2 == 0:
                    w(s, s_avo, 16 * (h2 - 1))
                s.activation(avh[h2 % 2][:, qci2, :], ps[2 + E % 2][:, 0:DH],
                             AF.Copy, scale=col[E % 2][:, 2:3]).then_inc(v5r, 1)

            for h in range(NH5):
                if h > 0:
                    for bt in bodyb_of_E[h * NQC - 1]:
                        evac(bt)
                for bt in [bt for bt in batches if bt["kind"] == "vt" and bt["h"] == h]:
                    evac(bt)
                if h > 0:
                    avevac(h * NQC - 1)
                for qci in range(NQC):
                    E = h * NQC + qci
                    s.wait_ge(v5m, E + 1)
                    if E >= 2:
                        s.wait_ge(p5t, bt_through_E[E - 2])
                        s.wait_ge(v5r, E - 1)
                    W = 256 * (qci + 1)
                    s.activation(Psb[E % 2][:, 0:W], Ssb[E % 2][:, 0:W], AF.Exp,
                                 bias=col[E % 2][:, 0:1],
                                 accum_out=col[E % 2][:, 1:2]).then_inc(a5x, 1)
                    if qci > 0:
                        for bt in bodyb_of_E[E - 1]:
                            evac(bt)
                        avevac(E - 1)
            for bt in bodyb_of_E[NH5 * NQC - 1]:
                evac(bt)
            avevac(NH5 * NQC - 1)

        @blk.vector
        def _(v):
            sbanks = (0, 1, 4, 5)
            for E in range(NH5 * NQC):
                h, qci = E // NQC, E % NQC
                if E == 0:
                    v.wait_ge(s_cst, 16 * 6)
                nt = len(tile_of_E[E])
                for t_i, (gtx, k0, wd) in enumerate(tile_of_E[E]):
                    v.wait_ge(p5s, gtx + 1)
                    if t_i == 0:
                        w(v, a5x, E - 1)
                    v.tensor_scalar(Ssb[E % 2][:, k0:k0 + wd],
                                    ps[sbanks[gtx % 4]][:, 0:wd],
                                    1.0, None, ALU.mult, ALU.max,
                                    accum_out=mx[E % 2][:, t_i:t_i + 1]).then_inc(v5e, 1)
                W = 256 * (qci + 1)
                v.tensor_tensor(Ssb[E % 2][:, W - 256:W], Ssb[E % 2][:, W - 256:W],
                                dmk[:], ALU.add)
                v.reduce_max(col[E % 2][:, 0:1], mx[E % 2][:, 0:nt], AX.X,
                             negate=True).then_inc(v5m, 1)
                if E >= 1:
                    v.wait_ge(a5x, E)
                    v.reciprocal(col[(E - 1) % 2][:, 2:3],
                                 col[(E - 1) % 2][:, 1:2]).then_inc(v5c, 1)
            v.wait_ge(a5x, NH5 * NQC)
            v.reciprocal(col[(NH5 * NQC - 1) % 2][:, 2:3],
                         col[(NH5 * NQC - 1) % 2][:, 1:2]).then_inc(v5c, 1)


def emit_p67(nc, es, c):
    sb = lambda *a: es.enter_context(nc.sbuf_tensor(*a))
    oa = sb("oa", [P, 8, D], F32)
    h2T = sb("h2T", [P, NCD, 1024], BF16)
    ab = sb("p67ab", [P, 7680], BF16)
    af = sb("p67af", [P, 2048], F32)
    avb = [ab[:, 0:2048].rearrange("p (h d) -> p h d", h=NH),
           ab[:, 2048:4096].rearrange("p (h d) -> p h d", h=NH)]
    avT = ab[:, 4096:6144].rearrange("p (c d) -> p c d", c=NCD)
    x6b = af
    sa = [af[:, 0:512], af[:, 512:1024]]
    w2x = ab[:, 0:5632].rearrange("p (i j) -> p i j", i=NIB)
    h2bf = sb("p6h2", [P, D], BF16)
    bo_bc = sb("p6bo", [P, D], F32)
    co6 = sb("p6c", [P, 8], F32)
    comb = sb("p7comb", [P, 8, 16], F32)
    gl = sb("p7gl", [P, 16], F32)
    gp = sb("p7gp", [P, 16], F32)
    t2b = sb("p7t2", [P, 16], F32)
    gc = sb("p7gc", [P, 8], F32)

    ps, pst = c["ps"], c["pst"]
    ident, ones1, rowst = c["ident"], c["ones1"], c["rowst"]
    Wres, gwt_sb = c["Wres"], c["gwt_sb"]
    AVs, Xown, Y, boh = c["AVs"], c["Xown"], c["Y"], c["boh"]
    w1h, w3h, w2h = c["w1h"], c["w3h"], c["w2h"]
    x1sp, h2sp = c["x1sp"], c["h2sp"]
    s_avl, s_x6 = c["s_avl"], c["s_x6"]
    s_sp, s_spl, s_y = c["s_sp"], c["s_spl"], c["s_y"]
    s_w13, s_w2, s_wld, s_cst = c["s_w13"], c["s_w2"], c["s_wld"], c["s_cst"]
    s_gw = c["s_gw"]
    p6t, a6e, p6m, v6a = c["p6t"], c["a6e"], c["p6m"], c["v6a"]
    v5e = c["v5e"]
    a6s, v6s = c["a6s"], c["v6s"]
    p7g, v7g, p7a, a7s = c["p7g"], c["v7g"], c["p7a"], c["a7s"]
    v7u, p7b, v7y = c["v7u"], c["p7b"], c["v7y"]
    Xov = Xown.rearrange("(tc p) d -> tc p d", p=P)
    Yv = Y.rearrange("(tc p) d -> tc p d", p=P)
    halves = [1, 0]

    p6t_avT, p6t_h2T = {}, {}
    cnt = 0
    for gq in range(16):
        cnt += 2
        p6t_avT[gq] = cnt
        if gq >= 1:
            cnt += 2
            p6t_h2T[gq - 1] = cnt
    cnt += 2
    p6t_h2T[15] = cnt

    with nc.Block("p6") as blk:
        @blk.sync
        def _(sy):
            sy.dma_start(out=rowst[:], in_=boh[None, :]).then_inc(s_cst, 16)
            for gq in range(16):
                hf, qcl = halves[gq // 8], gq % 8
                qc = hf * 8 + qcl
                if gq == 0:
                    sy.wait_ge(c["s_avo"], 16 * NH)
                if gq >= 2:
                    sy.wait_ge(p6t, p6t_avT[gq - 2])
                sy.dma_start(out=avb[gq % 2][:], in_=AVs[qc]).then_inc(s_avl, 16)
                if gq == 8:
                    for tc in range(8):
                        sy.wait_ge(v6a, 8 * tc + 8)
                        sy.dma_start(out=x1sp[tc], in_=oa[:, tc, :]).then_inc(s_sp, 16)
                    sy.wait_ge(a6e, 4 + p6t_h2T[7])
                    sy.dma_start(out=h2sp[:, :, :], in_=h2T[:]).then_inc(s_sp, 16)
                if gq >= 1:
                    sy.wait_ge(v6a, 8 * gq)
                sy.dma_start(out=x6b[:], in_=Xov[qc]).then_inc(s_x6, 16)

        @blk.tensor
        def _(t):
            t.wait_ge(s_cst, 16 * 7)
            t.wait_ge(c["g1"], 1)
            t.wait_ge(v5e, 72 * NH5)
            for j in range(4):
                if j > 0:
                    t.wait_ge(a6e, j)
                t.matmul(ps[4][:], ones1[:], rowst[:, j * 512:(j + 1) * 512],
                         start=True, stop=True).then_inc(p6m, 1)

            def h2t(g2, bat0):
                for half_b in range(2):
                    bat = bat0 + half_b
                    t.wait_ge(v6s, 2 * g2 + 2)
                    if bat >= 2:
                        t.wait_ge(a6e, 4 + bat - 1)
                    for i in range(8):
                        cc = half_b * 8 + i
                        last = t.transpose(pst[bat % 2][:, i * P:(i + 1) * P],
                                           h2bf[:, cc * P:(cc + 1) * P], ident[:])
                    last.then_inc(p6t, 1)

            for gq in range(16):
                t.wait_ge(s_avl, 16 * (gq + 1))
                for half_b in range(2):
                    bat = p6t_avT[gq] - 2 + half_b
                    if bat >= 2:
                        t.wait_ge(a6e, 4 + bat - 1)
                    for i in range(8):
                        cc = half_b * 8 + i
                        last = t.transpose(pst[bat % 2][:, i * P:(i + 1) * P],
                                           avb[gq % 2][:, cc, :], ident[:])
                    last.then_inc(p6t, 1)
                if gq >= 1:
                    h2t(gq - 1, p6t_h2T[gq - 1] - 2)
                t.wait_ge(a6e, 4 + p6t_avT[gq])
                for j in range(4):
                    n = gq * 4 + j
                    t.wait_ge(s_wld, 16 * (12 + j + 1))
                    if n >= 2:
                        t.wait_ge(v6a, 2 * (n - 2) + 2)
                    for cc in range(NCD):
                        last = t.matmul(ps[4 + n % 2][:], avT[:, cc, :],
                                        Wres[:, cc, j * 512:(j + 1) * 512],
                                        start=(cc == 0), stop=(cc == NCD - 1))
                    last.then_inc(p6m, 1)
            h2t(15, p6t_h2T[15] - 2)

        @blk.scalar
        def _(s):
            for j in range(4):
                s.wait_ge(p6m, j + 1)
                s.copy(bo_bc[:, j * 512:(j + 1) * 512], ps[4][:]).then_inc(a6e, 1)

            def h2ev(g2):
                hf2, qcl2 = halves[g2 // 8], g2 % 8
                if g2 == 8:
                    s.wait_ge(s_sp, 16 * 9)
                for half_b in range(2):
                    bat = p6t_h2T[g2] - 2 + half_b
                    s.wait_ge(p6t, bat + 1)
                    s.copy(h2T[:, half_b * 8:half_b * 8 + 8, qcl2 * P:(qcl2 + 1) * P],
                           pst[bat % 2][:].rearrange("p (k d) -> p k d", d=P)).then_inc(a6e, 1)

            for gq in range(16):
                for half_b in range(2):
                    bat = p6t_avT[gq] - 2 + half_b
                    s.wait_ge(p6t, bat + 1)
                    if half_b == 0:
                        s.wait_ge(p6m, 4 + 4 * gq)
                    s.copy(avT[:, half_b * 8:half_b * 8 + 8, :],
                           pst[bat % 2][:].rearrange("p (k d) -> p k d", d=P)).then_inc(a6e, 1)
                if gq >= 1:
                    h2ev(gq - 1)
                s.wait_ge(v6a, 8 * gq + 8)
                if gq >= 1:
                    s.wait_ge(p6t, p6t_h2T[gq - 1])
                s.activation(h2bf[:], oa[:, gq % 8, :], AF.Square,
                             accum_out=co6[:, 1:2]).then_inc(a6s, 1)
                s.wait_ge(v6s, 2 * gq + 1)
                s.activation(co6[:, 5:6], co6[:, 3:4], AF.Ln)
                s.activation(co6[:, 4:5], co6[:, 5:6], AF.Exp, scale=-0.5).then_inc(a6s, 1)
            h2ev(15)

        @blk.vector
        def _(v):
            for gq in range(16):
                qcl = gq % 8
                for j in range(4):
                    n = gq * 4 + j
                    v.wait_ge(p6m, 4 + n + 1)
                    if gq == 8 and j == 0:
                        v.wait_ge(s_sp, 16 * 8)
                    v.scalar_tensor_tensor(oa[:, qcl, j * 512:(j + 1) * 512],
                                           ps[4 + n % 2][:], 1.0,
                                           bo_bc[:, j * 512:(j + 1) * 512],
                                           ALU.mult, ALU.add).then_inc(v6a, 1)
                    v.wait_ge(s_x6, 16 * (gq + 1))
                    v.scalar_tensor_tensor(oa[:, qcl, j * 512:(j + 1) * 512],
                                           oa[:, qcl, j * 512:(j + 1) * 512], 1.0,
                                           x6b[:, j * 512:(j + 1) * 512],
                                           ALU.mult, ALU.add).then_inc(v6a, 1)
                v.reduce_sum(co6[:, 0:1], oa[:, qcl, :], AX.X)
                v.wait_ge(a6s, 2 * gq + 1)
                v.tensor_scalar_mul(co6[:, 2:3], co6[:, 0:1], RN)
                v.tensor_scalar(co6[:, 3:4], co6[:, 1:2], RN, EPS, ALU.mult, ALU.add)
                v.tensor_scalar(co6[:, 5:6], co6[:, 2:3], co6[:, 2:3], None, ALU.mult)
                v.tensor_scalar(co6[:, 3:4], co6[:, 5:6], co6[:, 3:4], -1.0,
                                ALU.subtract, ALU.mult).then_inc(v6s, 1)
                v.wait_ge(a6s, 2 * gq + 2)
                v.tensor_scalar(h2bf[:], oa[:, qcl, :], co6[:, 2:3], co6[:, 4:5],
                                ALU.subtract, ALU.mult).then_inc(v6s, 1)

    flat = Wres[:].rearrange("p c d -> p (c d)")
    u_v = flat[:, 0:11264].rearrange("p (i t) -> p i t", i=NIB)
    w1b = [flat[:, 11264 + k * 2048:11264 + (k + 1) * 2048]
           .rearrange("p (c j) -> p c j", c=NCD) for k in range(2)]
    w3b = [flat[:, 15360 + k * 2048:15360 + (k + 1) * 2048]
           .rearrange("p (c j) -> p c j", c=NCD) for k in range(2)]
    w2b = [flat[:, 19456:25088].rearrange("p (i j) -> p i j", i=NIB),
           flat[:, 25088:30720].rearrange("p (i j) -> p i j", i=NIB),
           w2x]

    NGIB = NE7 * NIB
    M7 = 32 * NE7
    NW2 = 4 * NE7
    NST = 2 * NGIB
    A7H = 16 + NST

    with nc.Block("p7") as blk:
        @blk.gpsimd
        def _(g):
            for hx in range(2):
                if hx == 0:
                    g.wait_ge(p6m, 4 + 64)
                else:
                    g.wait_ge(p7a, NST)

                def w2dma(gg):
                    e2, j2 = gg // 4, gg % 4
                    if gg >= 3:
                        g.wait_ge(p7b, M7 * hx + 8 * (gg - 2))
                    elif hx == 1 and gg == 0:
                        g.wait_ge(p7b, M7)
                    g.dma_start(out=w2b[gg % 3],
                                in_=w2h[e2, :, :, j2 * 512:(j2 + 1) * 512]
                                .rearrange("i p j -> p i j")).then_inc(s_w2, 16)

                def w13dma(gib):
                    e2, ib2 = gib // NIB, gib % NIB
                    if gib >= 2:
                        g.wait_ge(p7a, NST * hx + 2 * (gib - 1))
                    elif hx == 1 and gib == 0:
                        g.wait_ge(p7a, NST)
                    g.dma_start(out=w1b[gib % 2][:], in_=w1h[e2, ib2]).then_inc(s_w13, 16)
                    g.dma_start(out=w3b[gib % 2][:], in_=w3h[e2, ib2]).then_inc(s_w13, 16)

                for gg in range(3):
                    w2dma(gg)
                cur = 0
                for gib in range(min(6, NGIB)):
                    w13dma(gib)
                    cur = gib + 1
                for e in range(NE7):
                    hi = min(e * NIB + 12, NGIB - 1)
                    while cur <= hi:
                        w13dma(cur)
                        cur += 1
                    for gg in range(4 * e + 3, min(4 * e + 7, NW2)):
                        w2dma(gg)
                while cur < NGIB:
                    w13dma(cur)
                    cur += 1

        @blk.sync
        def _(sy):
            for hx, hf in enumerate([0, 1]):
                for tc in range(8):
                    sy.wait_ge(v7y, M7 * hx + 32 * (NE7 - 1) + 24 + tc + 1)
                    sy.dma_start(out=Yv[hf * 8 + tc], in_=oa[:, tc, :]).then_inc(s_y, 16)
            sy.wait_ge(s_y, 16 * 16)

        @blk.tensor
        def _(t):
            for hx, hf in enumerate([0, 1]):
                for tc in range(8):
                    if hx == 0 and tc == 0:
                        t.wait_ge(a6e, 4 + p6t_h2T[15])
                        t.wait_ge(s_gw, 16)
                    if hx == 1 and tc == 0:
                        t.wait_ge(s_spl, 16 * 9)
                    if tc >= 2:
                        t.wait_ge(v7g, 16 * hx + 2 * (tc - 2) + 1)
                    for cc in range(NCD):
                        last = t.matmul(ps[4 + tc % 2][:, 0:15],
                                        h2T[:, cc, tc * P:(tc + 1) * P],
                                        gwt_sb[:, cc, :],
                                        start=(cc == 0), stop=(cc == NCD - 1))
                    last.then_inc(p7g, 1)
                for e in range(NE7):
                    for ib in range(NIB):
                        gib = e * NIB + ib
                        for u in range(2):
                            st = 2 * gib + u
                            ab, bb = ps[st % 2], ps[2 + st % 2]
                            if st >= 2:
                                t.wait_ge(v7u, NST * hx + st - 1)
                            t.wait_ge(s_w13, 32 * NGIB * hx + 32 * (gib + 1))
                            for cc in range(NCD):
                                t.matmul(ab[:], w1b[gib % 2][:, cc, :],
                                         h2T[:, cc, u * 512:(u + 1) * 512],
                                         start=(cc == 0), stop=(cc == NCD - 1))
                            for cc in range(NCD):
                                last = t.matmul(bb[:], w3b[gib % 2][:, cc, :],
                                                h2T[:, cc, u * 512:(u + 1) * 512],
                                                start=(cc == 0), stop=(cc == NCD - 1))
                            last.then_inc(p7a, 1)
                    t.wait_ge(v7u, NST * hx + 22 * (e + 1))
                    for j in range(4):
                        gj = 4 * e + j
                        t.wait_ge(s_w2, 16 * (NW2 * hx + gj + 1))
                        for tc in range(8):
                            cj = 32 * e + 8 * j + tc
                            yb = ps[4 + cj % 2]
                            if cj >= 2:
                                t.wait_ge(v7y, M7 * hx + cj - 1)
                            for ic in range(NIB):
                                last = t.matmul(yb[:], u_v[:, ic, tc * P:(tc + 1) * P],
                                                w2b[gj % 3][:, ic, :],
                                                start=(ic == 0), stop=(ic == NIB - 1))
                            last.then_inc(p7b, 1)

        @blk.scalar
        def _(s):
            for hx in range(2):
                if hx == 1:
                    s.wait_ge(p7a, NST)
                    for tc in range(8):
                        s.wait_ge(s_y, 16 * (tc + 1))
                        s.dma_start(out=oa[:, tc, :], in_=x1sp[tc]).then_inc(s_spl, 16)
                    s.dma_start(out=h2T[:], in_=h2sp[:, :, :]).then_inc(s_spl, 16)
                for tc in range(8):
                    s.wait_ge(v7g, 16 * hx + 2 * tc + 1)
                    s.activation(gp[:, 0:15], gl[:, 0:15], AF.Exp, bias=gc[:, 1:2],
                                 accum_out=gc[:, 2:3]).then_inc(a7s, 1)
                    s.wait_ge(v7g, 16 * hx + 2 * tc + 2)
                    s.activation(gp[:, 0:15], gp[:, 0:15], AF.Copy,
                                 scale=gc[:, 3:4]).then_inc(a7s, 1)
                for gib in range(NGIB):
                    for u in range(2):
                        st = 2 * gib + u
                        s.wait_ge(p7a, NST * hx + st + 1)
                        if st >= 2:
                            s.wait_ge(v7u, NST * hx + st - 1)
                        s.activation(sa[st % 2][:], ps[st % 2][:], AF.Silu
                                     ).then_inc(a7s, 1)

        @blk.vector
        def _(v):
            for hx in range(2):
                v.memset(comb[:, :, 15:16], 1.0)
                for tc in range(8):
                    v.wait_ge(p7g, 8 * hx + tc + 1)
                    if tc >= 1:
                        v.wait_ge(a7s, A7H * hx + 2 * (tc - 1) + 1)
                    v.tensor_scalar(gl[:, 0:15], ps[4 + tc % 2][:, 0:15], 1.0, None, ALU.mult)
                    v.reduce_max(gc[:, 1:2], gl[:, 0:15], AX.X,
                                 negate=True).then_inc(v7g, 1)
                    v.wait_ge(a7s, A7H * hx + 2 * tc + 1)
                    v.reciprocal(gc[:, 3:4], gc[:, 2:3]).then_inc(v7g, 1)
                    v.wait_ge(a7s, A7H * hx + 2 * tc + 2)
                    v.reduce_max(gc[:, 4:5], gp[:, 0:15], AX.X)
                    v.tensor_scalar_mul(gc[:, 5:6], gc[:, 4:5], 1.0)
                    v.scalar_tensor_tensor(t2b[:, 0:15], gp[:, 0:15], gc[:, 4:5],
                                           gp[:, 0:15], ALU.is_lt, ALU.mult)
                    v.tensor_scalar_mul(gc[:, 5:6], gc[:, 4:5], 1.0)
                    v.reduce_max(gc[:, 6:7], t2b[:, 0:15], AX.X)
                    v.tensor_scalar_mul(gc[:, 5:6], gc[:, 6:7], 1.0)
                    v.scalar_tensor_tensor(comb[:, tc, 0:15], gp[:, 0:15], gc[:, 6:7],
                                           gp[:, 0:15], ALU.is_ge, ALU.mult)
                for gib in range(NGIB):
                    ib = gib % NIB
                    for u in range(2):
                        st = 2 * gib + u
                        v.wait_ge(a7s, A7H * hx + 16 + st + 1)
                        if gib == 0 and u == 0 and hx == 1:
                            v.wait_ge(p7b, M7)
                        v.tensor_tensor(u_v[:, ib, u * 512:(u + 1) * 512],
                                        sa[st % 2][:], ps[2 + st % 2][:], ALU.mult
                                        ).then_inc(v7u, 1)
                    if ib == NIB - 1:
                        e = gib // NIB
                        for j in range(4):
                            for tc in range(8):
                                cj = 32 * e + 8 * j + tc
                                v.wait_ge(p7b, M7 * hx + cj + 1)
                                if e == 0 and j == 0 and tc == 0:
                                    v.wait_ge(v7g, 16 * hx + 16)
                                v.scalar_tensor_tensor(oa[:, tc, j * 512:(j + 1) * 512],
                                                       ps[4 + cj % 2][:],
                                                       comb[:, tc, e:e + 1],
                                                       oa[:, tc, j * 512:(j + 1) * 512],
                                                       ALU.mult, ALU.add).then_inc(v7y, 1)


def emit_dbg(nc, dbg, items):
    with nc.semaphore("sdbg") as sd, nc.Block("dbg") as blk:
        @blk.sync
        def _(sy):
            n = 0
            for name, src, semh, need in items:
                if name not in dbg:
                    continue
                sy.wait_ge(semh, need)
                if len(src.shape) == 3:
                    sy.dma_start(out=dbg[name][:, :, :], in_=src[:, :, :]).then_inc(sd, 16)
                else:
                    sy.dma_start(out=dbg[name][:, :, :, :], in_=src[:, :, :, :]).then_inc(sd, 16)
                n += 1
            sy.wait_ge(sd, 16 * n)


def kernel(**inp):
    X = np.asarray(inp["L0_attn_X"], dtype=np.float32)
    assert X.shape == (4, S, D)

    w1 = np.asarray(inp["L0_ffn_w1"], np.float32)
    w3 = np.asarray(inp["L0_ffn_w3"], np.float32)
    w2 = np.asarray(inp["L0_ffn_w2"], np.float32)
    w1a = np.concatenate([w1, np.asarray(inp["L0_ffn_sw1"], np.float32)[None]], axis=0)
    w3a = np.concatenate([w3, np.asarray(inp["L0_ffn_sw3"], np.float32)[None]], axis=0)
    w2a = np.concatenate([w2, np.asarray(inp["L0_ffn_sw2"], np.float32)[None]], axis=0)
    w1s = np.ascontiguousarray(w1a.reshape(NE, NIB, P, NCD, P).transpose(0, 1, 4, 3, 2))
    w3s = np.ascontiguousarray(w3a.reshape(NE, NIB, P, NCD, P).transpose(0, 1, 4, 3, 2))
    w2s = np.ascontiguousarray(w2a.reshape(NE, D, NIB, P).transpose(0, 2, 3, 1))
    gwt = np.ascontiguousarray(np.asarray(inp["L0_ffn_gate_w"], np.float32).T)

    tri = (np.arange(P)[None, :] > np.arange(P)[:, None])
    common = dict(
        wq=np.ascontiguousarray(np.asarray(inp["L0_attn_Wq"], np.float32)),
        wk=np.ascontiguousarray(np.asarray(inp["L0_attn_Wk"], np.float32)),
        wv=np.ascontiguousarray(np.asarray(inp["L0_attn_Wv"], np.float32)),
        wo=np.ascontiguousarray(np.asarray(inp["L0_attn_Wo"], np.float32)),
        bq=np.ascontiguousarray(np.asarray(inp["L0_attn_bq"], np.float32)),
        bk=np.ascontiguousarray(np.asarray(inp["L0_attn_bk"], np.float32)),
        bv=np.ascontiguousarray(np.asarray(inp["L0_attn_bv"], np.float32)),
        bo=np.ascontiguousarray(np.asarray(inp["L0_attn_bo"], np.float32)),
        lng=np.ascontiguousarray(np.asarray(inp["L0_attn_ln_g"], np.float32)),
        lnb=np.ascontiguousarray(np.asarray(inp["L0_attn_ln_b"], np.float32)),
        gwt=gwt, w1s=w1s, w3s=w3s, w2s=w2s)

    in_maps = []
    for cix in range(8):
        sq, par = cix // 2, cix % 2
        dm = np.zeros((P, 256), np.float32)
        if par == 1:
            dm[:, P:][tri] = NEG
        else:
            dm[:, :P][tri] = NEG
            dm[:, P:] = NEG
        m = dict(common)
        m["xall"] = np.ascontiguousarray(X[sq])
        m["xown"] = np.ascontiguousarray(X[sq].reshape(NTC, P, D)[par::2].reshape(TOK, D))
        m["dmask"] = dm
        in_maps.append(m)

    nc, dbg = build_program()
    res = run_bass_kernel_spmd(nc, in_maps[:CORES], core_ids=list(range(CORES)))

    if DBG or CORES < 8:
        return res

    Yf = np.empty((4, S, D), np.float32)
    for cix in range(8):
        sq, par = cix // 2, cix % 2
        Yf[sq].reshape(NTC, P, D)[par::2] = res.results[cix]["y"].reshape(NQC, P, D)
    return Yf


# version 10
# speedup vs baseline: 1.0222x; 1.0104x over previous
import numpy as np
from contextlib import ExitStack

import concourse.bass as bass
import concourse.mybir as mybir
from concourse.bass_utils import run_bass_kernel_spmd

F32 = mybir.dt.float32
BF16 = mybir.dt.bfloat16
AF = mybir.ActivationFunctionType
ALU = mybir.AluOpType
AX = mybir.AxisListType

P = 128
D = 2048
NCD = D // P
S = 4096
NTC = S // P
TOK = 2048
NQC = TOK // P
NH = 16
DH = 128
IT = 1408
NIB = IT // P
NE = 16
EPS = 1e-5
QSC = 0.08838834764831845
NEG = -1e30
RN = 1.0 / D

DBG = 0
CORES = 8
NH5 = NH
RACE = True
NE7 = NE


def tiles_of(qci):
    W = 256 * (qci + 1)
    out, k = [], 0
    while k < W:
        w = min(512, W - k)
        out.append((k, w))
        k += w
    return out


def build_program():
    nc = bass.Bass("TRN2", target_bir_lowering=False, detect_race_conditions=RACE)

    Xall = nc.dram_tensor("xall", [S, D], F32, kind="ExternalInput")
    Xown = nc.dram_tensor("xown", [TOK, D], F32, kind="ExternalInput")
    Dmk = nc.dram_tensor("dmask", [P, 256], F32, kind="ExternalInput")
    Wqh = nc.dram_tensor("wq", [D, D], F32, kind="ExternalInput")
    Wkh = nc.dram_tensor("wk", [D, D], F32, kind="ExternalInput")
    Wvh = nc.dram_tensor("wv", [D, D], F32, kind="ExternalInput")
    Woh = nc.dram_tensor("wo", [D, D], F32, kind="ExternalInput")
    bqh = nc.dram_tensor("bq", [D], F32, kind="ExternalInput")
    bkh = nc.dram_tensor("bk", [D], F32, kind="ExternalInput")
    bvh = nc.dram_tensor("bv", [D], F32, kind="ExternalInput")
    boh = nc.dram_tensor("bo", [D], F32, kind="ExternalInput")
    gh = nc.dram_tensor("lng", [D], F32, kind="ExternalInput")
    bh = nc.dram_tensor("lnb", [D], F32, kind="ExternalInput")
    gwh = nc.dram_tensor("gwt", [D, 15], F32, kind="ExternalInput")
    w1h = nc.dram_tensor("w1s", [NE, NIB, P, NCD, P], F32, kind="ExternalInput")
    w3h = nc.dram_tensor("w3s", [NE, NIB, P, NCD, P], F32, kind="ExternalInput")
    w2h = nc.dram_tensor("w2s", [NE, NIB, P, D], F32, kind="ExternalInput")
    Y = nc.dram_tensor("y", [TOK, D], F32, kind="ExternalOutput")

    xns = nc.dram_tensor("xns", [NTC, P, D], BF16)
    Ks = nc.dram_tensor("ks", [NH, P, S], BF16)
    Vs = nc.dram_tensor("vs", [NH, P, S], BF16)
    Qs = nc.dram_tensor("qs", [NH, P, TOK], BF16)
    AVs = nc.dram_tensor("avs", [NQC, P, NH, DH], BF16)
    x1sp = nc.dram_tensor("x1sp", [8, P, D], F32)
    h2sp = nc.dram_tensor("h2sp", [P, NCD, 1024], BF16)

    dbg = {}
    if DBG:
        if DBG >= 1:
            dbg["d_xns"] = nc.dram_tensor("d_xns", [NTC, P, D], BF16, kind="ExternalOutput")
        if DBG >= 2:
            dbg["d_ks"] = nc.dram_tensor("d_ks", [NH, P, S], BF16, kind="ExternalOutput")
        if DBG >= 3:
            dbg["d_vs"] = nc.dram_tensor("d_vs", [NH, P, S], BF16, kind="ExternalOutput")
        if DBG >= 4:
            dbg["d_qs"] = nc.dram_tensor("d_qs", [NH, P, TOK], BF16, kind="ExternalOutput")
        if DBG >= 5:
            dbg["d_avs"] = nc.dram_tensor("d_avs", [NQC, P, NH, DH], BF16, kind="ExternalOutput")
        if DBG == 7:
            dbg["d_x1"] = nc.dram_tensor("d_x1", [8, P, D], F32, kind="ExternalOutput")
            dbg["d_h2s"] = nc.dram_tensor("d_h2s", [P, NCD, 1024], BF16, kind="ExternalOutput")

    with ExitStack() as es:
        def sem(name):
            return es.enter_context(nc.semaphore(name))

        s_cst = sem("s_cst"); s_wld = sem("s_wld")
        s_x1 = sem("s_x1"); s_xst = sem("s_xst"); s_xg = sem("s_xg")
        s_ko = sem("s_ko"); s_vo = sem("s_vo"); s_qo = sem("s_qo"); s_x4 = sem("s_x4")
        s_kl = sem("s_kl"); s_vl = sem("s_vl"); s_ql = sem("s_ql")
        s_avo = sem("s_avo"); s_avl = sem("s_avl"); s_x6 = sem("s_x6")
        s_sp = sem("s_sp"); s_spl = sem("s_spl"); s_y = sem("s_y")
        s_w13 = sem("s_w13"); s_w2 = sem("s_w2"); s_gw = sem("s_gw")

        a1 = sem("a1"); v1 = sem("v1"); p1 = sem("p1"); g1 = sem("g1"); a1e = sem("a1e")
        pmm = sem("pmm"); vev = sem("vev")
        a4 = sem("a4"); v4 = sem("v4"); p4 = sem("p4"); g4 = sem("g4"); a4e = sem("a4e")
        p5s = sem("p5s"); p5t = sem("p5t"); p5a = sem("p5a"); p5q = sem("p5q"); v5c = sem("v5c")
        v5e = sem("v5e"); v5m = sem("v5m"); a5x = sem("a5x"); v5r = sem("v5r")
        a5e = sem("a5e"); v5pe = sem("v5pe")
        p6t = sem("p6t"); a6e = sem("a6e"); p6m = sem("p6m"); v6a = sem("v6a")
        g6b = sem("g6b"); a6s = sem("a6s"); v6s = sem("v6s"); a6h = sem("a6h")
        p7g = sem("p7g"); v7g = sem("v7g"); p7a = sem("p7a"); a7s = sem("a7s")
        v7u = sem("v7u"); p7b = sem("p7b"); v7y = sem("v7y")

        Wres = es.enter_context(nc.sbuf_tensor("wres", [P, NCD, D], BF16, side="right"))
        sbt = lambda *a: es.enter_context(nc.sbuf_tensor(*a))
        ident = sbt("ident", [P, P], BF16)
        ones1 = sbt("ones1", [1, P], F32)
        bkc = sbt("bkc", [P, NCD], F32)
        bvc = sbt("bvc", [P, NCD], F32)
        bqc = sbt("bqc", [P, NCD], F32)
        gwt_sb = sbt("gwt_sb", [P, NCD, 15], BF16)
        dmk_sb = sbt("dmk_sb", [P, 256], F32)

        ps = [es.enter_context(nc.psum_tensor(f"ps{i}", [P, 512], F32)) for i in range(6)]
        pst = [es.enter_context(nc.psum_tensor(f"pst{i}", [P, 1024], BF16)) for i in range(2)]

        with nc.Block("p0") as blk:
            @blk.sync
            def _(sy):
                sy.dma_start(out=bkc[:], in_=bkh.rearrange("(c p) -> p c", p=P),
                             allow_slow_non_contiguous=True).then_inc(s_cst, 16)
                sy.dma_start(out=bvc[:], in_=bvh.rearrange("(c p) -> p c", p=P),
                             allow_slow_non_contiguous=True).then_inc(s_cst, 16)
                sy.dma_start(out=bqc[:], in_=bqh.rearrange("(c p) -> p c", p=P),
                             allow_slow_non_contiguous=True).then_inc(s_cst, 16)
                sy.dma_start(out=dmk_sb[:], in_=Dmk[:, :]).then_inc(s_cst, 16)

            @blk.gpsimd
            def _(g):
                g.dma_start(out=gwt_sb[:], in_=gwh.rearrange("(c p) n -> p c n", p=P)
                            ).then_inc(s_gw, 16)
                g.memset(ident[:], 0.0)
                g.affine_select(out=ident[:], in_=ident[:], compare_op=ALU.not_equal,
                                fill=1.0, base=0, pattern=[[-1, P]], channel_multiplier=1)
                g.memset(ones1[:], 1.0).then_inc(g1, 1)

        with ExitStack() as es14:
            sb14 = lambda *a: es14.enter_context(nc.sbuf_tensor(*a))
            g_bc = sb14("g_bc", [P, D], F32)
            b_bc = sb14("b_bc", [P, D], F32)

            rowst = b_bc[0:1, :]
            with nc.Block("p0b") as blk:
                @blk.sync
                def _(sy):
                    sy.dma_start(out=rowst, in_=gh[None, :]).then_inc(s_cst, 16)

                @blk.tensor
                def _(t):
                    t.wait_ge(s_cst, 16 * 5)
                    t.wait_ge(g1, 1)
                    for j in range(4):
                        t.matmul(ps[j][:], ones1[:], rowst[0:1, j * 512:(j + 1) * 512],
                                 start=True, stop=True).then_inc(p1, 1)

                @blk.scalar
                def _(s):
                    for j in range(4):
                        s.wait_ge(p1, j + 1)
                        s.copy(g_bc[:, j * 512:(j + 1) * 512], ps[j][:]).then_inc(v1, 1)

                @blk.sync
                def _(sy):
                    sy.wait_ge(p1, 4)
                    sy.dma_start(out=rowst, in_=bh[None, :]).then_inc(s_cst, 16)

                @blk.tensor
                def _(t):
                    t.wait_ge(s_cst, 16 * 6)
                    t.wait_ge(v1, 4)
                    for j in range(4):
                        t.matmul(ps[j][:], ones1[:], rowst[0:1, j * 512:(j + 1) * 512],
                                 start=True, stop=True).then_inc(p1, 1)

                @blk.scalar
                def _(s):
                    for j in range(4):
                        s.wait_ge(p1, 4 + j + 1)
                        s.copy(b_bc[:, j * 512:(j + 1) * 512], ps[j][:]).then_inc(v1, 1)

            with ExitStack() as e123:
                emit_ln_pass(nc, e123, dict(
                    tag="p1", src=Xall, ntc=NTC, xns=xns, g_bc=g_bc, b_bc=b_bc,
                    ident=ident, pst=pst,
                    s_x=s_x1, s_st=s_xst, va=v1, aa=a1, pa=p1, ga=g1, ae=a1e,
                    v_base=8, a_base=0, p_base=8, g_base=1, e_base=0, st_base=0,
                    prefetch=(Wkh, Wres, s_wld)))

                emit_proj_pass(nc, e123, dict(
                    tag="pkv", ps=ps, Wres=Wres, scale=None,
                    segments=[
                        dict(dst=Ks, bias=bkc, s_out=s_ko, wld_base=0),
                        dict(dst=Vs, bias=bvc, s_out=s_vo, wld_base=4)],
                    xsrc=xns, xst_sem=s_xst,
                    s_xg=s_xg, pmm=pmm, vev=vev, s_wld=s_wld,
                    prefetches={0: (Wvh, Wres, s_wld, pmm, 128),
                                8: (Wqh, Wres, s_wld, pmm, 256)}))

            with ExitStack() as e4:
                emit_p4(nc, e4, dict(
                    ps=ps, Xown=Xown, Qs=Qs, g_bc=g_bc, b_bc=b_bc, ident=ident, pst=pst,
                    bqc=bqc, Wres=Wres, s_wld=s_wld,
                    s_x4=s_x4, s_qo=s_qo, a4=a4, v4=v4, p4=p4, g4=g4, a4e=a4e,
                    pmm=pmm, vev=vev, mm_base=256, ev_base=256,
                    gate_sem=s_vo, gate_val=16 * 8,
                    prefetch=(Woh, Wres, s_wld, pmm, 320)))

        if DBG and DBG <= 4:
            emit_dbg(nc, dbg, [
                ("d_xns", xns, s_xst, 16 * NTC),
                ("d_ks", Ks, s_ko, 16 * 8),
                ("d_vs", Vs, s_vo, 16 * 8),
                ("d_qs", Qs, s_qo, 16 * 4)])
            return nc, dbg

        with ExitStack() as es5:
            emit_p5(nc, es5, dict(
                Ks=Ks, Vs=Vs, Qs=Qs, AVs=AVs, dmk=dmk_sb, ident=ident,
                ps=ps, pst=pst,
                s_ko=s_ko, s_vo=s_vo, s_qo=s_qo, s_kl=s_kl, s_vl=s_vl, s_ql=s_ql,
                s_avo=s_avo, s_cst=s_cst,
                p5s=p5s, p5t=p5t, p5a=p5a, p5q=p5q, v5e=v5e, v5m=v5m,
                a5x=a5x, v5r=v5r, a5e=a5e, v5pe=v5pe, v5c=v5c))

        if DBG == 5:
            emit_dbg(nc, dbg, [("d_avs", AVs, s_avo, 16 * NH5)])
            return nc, dbg

        with ExitStack() as es67:
            emit_p67(nc, es67, dict(
                AVs=AVs, Xown=Xown, Y=Y, boh=boh, ones1=ones1,
                ident=ident, ps=ps, pst=pst, Wres=Wres, gwt_sb=gwt_sb,
                w1h=w1h, w3h=w3h, w2h=w2h, x1sp=x1sp, h2sp=h2sp,
                s_avo=s_avo, s_avl=s_avl, s_x6=s_x6, s_sp=s_sp, s_spl=s_spl,
                s_y=s_y, s_w13=s_w13, s_w2=s_w2, s_wld=s_wld, s_cst=s_cst, s_gw=s_gw,
                p6t=p6t, a6e=a6e, p6m=p6m, v6a=v6a, a6s=a6s, v6s=v6s, v5e=v5e,
                g1=g1, p7g=p7g, v7g=v7g, p7a=p7a, a7s=a7s, v7u=v7u,
                p7b=p7b, v7y=v7y))

        if DBG == 7:
            emit_dbg(nc, dbg, [("d_x1", x1sp, s_sp, 16 * 9),
                               ("d_h2s", h2sp, s_sp, 16 * 9)])

    return nc, dbg


def emit_ln_pass(nc, es, c):
    sb = lambda *a: es.enter_context(nc.sbuf_tensor(*a))
    tg = c["tag"]
    xbuf = [sb(f"{tg}x{i}", [P, D], F32) for i in range(2)]
    hbuf = [sb(f"{tg}h{i}", [P, D], F32) for i in range(2)]
    tbuf = [sb(f"{tg}t", [P, D], F32)] * 2
    xnb = [sb(f"{tg}n{i}", [P, D], BF16) for i in range(2)]
    xtb = [sb(f"{tg}T{i}", [P, D], BF16) for i in range(2)]
    garb = sb(f"{tg}garb", [P, D], BF16)
    cols = [sb(f"{tg}c{i}", [P, 8], F32) for i in range(2)]
    ntc = c["ntc"]
    src, xns = c["src"], c["xns"]
    s_x, s_st = c["s_x"], c["s_st"]
    va, aa, pa, ga, ae = c["va"], c["aa"], c["pa"], c["ga"], c["ae"]
    vb, ab, pb, gb, eb, stb = (c["v_base"], c["a_base"], c["p_base"],
                               c["g_base"], c["e_base"], c["st_base"])
    g_bc, b_bc, ident, pst = c["g_bc"], c["b_bc"], c["ident"], c["pst"]
    srcv = src.rearrange("(tc p) d -> tc p d", p=P)

    with nc.Block(c["tag"]) as blk:
        @blk.gpsimd
        def _(g):
            wh, Wd, s_wld = c["prefetch"]
            for j in range(4):
                g.dma_start(out=Wd[:, :, j * 512:(j + 1) * 512],
                            in_=wh[:, j * 512:(j + 1) * 512]
                            .rearrange("(c p) n -> p c n", p=P)).then_inc(s_wld, 16)
            for tc in range(ntc):
                pbuf = tc % 2
                g.wait_ge(va, vb + 4 * tc + 4)
                if tc >= 2:
                    g.wait_ge(pa, pb + 2 * (tc - 2) + 2)
                g.tensor_tensor(xnb[pbuf][:], hbuf[pbuf][:], b_bc[:], ALU.add
                                ).then_inc(ga, 2)

        @blk.sync
        def _(sy):
            for tc in range(ntc):
                pbuf = tc % 2
                if tc >= 2:
                    sy.wait_ge(va, vb + 4 * (tc - 2) + 2)
                sy.dma_start(out=xbuf[pbuf][:], in_=srcv[tc]).then_inc(s_x, 16)
                if tc >= 1:
                    sy.wait_ge(ae, eb + 2 * tc)
                    sy.dma_start(out=xns[tc - 1], in_=xtb[(tc - 1) % 2][:]
                                 ).then_inc(s_st, 16)
            sy.wait_ge(ae, eb + 2 * ntc)
            sy.dma_start(out=xns[ntc - 1], in_=xtb[(ntc - 1) % 2][:]).then_inc(s_st, 16)

        @blk.scalar
        def _(s):
            def evacs(tc):
                pbuf = tc % 2
                s.wait_ge(pa, pb + 2 * tc + 1)
                if tc >= 2:
                    s.wait_ge(s_st, stb + 16 * (tc - 1))
                s.copy(xtb[pbuf][:, 0:1024], pst[0][:]).then_inc(ae, 1)
                s.wait_ge(pa, pb + 2 * tc + 2)
                s.copy(xtb[pbuf][:, 1024:2048], pst[1][:]).then_inc(ae, 1)

            for tc in range(ntc):
                pbuf = tc % 2
                co = cols[pbuf]
                s.wait_ge(s_x, 16 * (tc + 1))
                s.activation(garb[:], xbuf[pbuf][:], AF.Square,
                             accum_out=co[:, 1:2]).then_inc(aa, 1)
                s.wait_ge(va, vb + 4 * tc + 1)
                s.activation(co[:, 5:6], co[:, 3:4], AF.Ln)
                s.activation(co[:, 4:5], co[:, 5:6], AF.Exp, scale=-0.5).then_inc(aa, 1)
                s.wait_ge(va, vb + 4 * tc + 2)
                s.activation(garb[:], hbuf[pbuf][:], AF.Square,
                             accum_out=co[:, 1:2]).then_inc(aa, 1)
                s.wait_ge(va, vb + 4 * tc + 3)
                s.activation(co[:, 5:6], co[:, 3:4], AF.Ln)
                s.activation(co[:, 4:5], co[:, 5:6], AF.Exp, scale=-0.5).then_inc(aa, 1)
                if tc >= 1:
                    evacs(tc - 1)
            evacs(ntc - 1)

        @blk.vector
        def _(v):
            for tc in range(ntc):
                pbuf = tc % 2
                co = cols[pbuf]
                v.wait_ge(s_x, 16 * (tc + 1))
                v.reduce_sum(co[:, 0:1], xbuf[pbuf][:], AX.X)
                v.wait_ge(aa, ab + 4 * tc + 1)
                v.tensor_scalar_mul(co[:, 2:3], co[:, 0:1], RN)
                v.tensor_scalar(co[:, 3:4], co[:, 1:2], RN, EPS, ALU.mult, ALU.add)
                v.tensor_scalar(co[:, 5:6], co[:, 2:3], co[:, 2:3], None, ALU.mult)
                v.tensor_scalar(co[:, 3:4], co[:, 5:6], co[:, 3:4], -1.0,
                                ALU.subtract, ALU.mult).then_inc(va, 1)
                v.wait_ge(aa, ab + 4 * tc + 2)
                if tc >= 2:
                    v.wait_ge(ga, gb + 2 * (tc - 2) + 2)
                v.tensor_scalar(hbuf[pbuf][:], xbuf[pbuf][:], co[:, 2:3], co[:, 4:5],
                                ALU.subtract, ALU.mult).then_inc(va, 1)
                v.reduce_sum(co[:, 0:1], hbuf[pbuf][:], AX.X)
                v.wait_ge(aa, ab + 4 * tc + 3)
                v.tensor_scalar_mul(co[:, 2:3], co[:, 0:1], RN)
                v.tensor_scalar(co[:, 3:4], co[:, 1:2], RN, EPS, ALU.mult, ALU.add)
                v.tensor_scalar(co[:, 5:6], co[:, 2:3], co[:, 2:3], None, ALU.mult)
                v.tensor_scalar(co[:, 3:4], co[:, 5:6], co[:, 3:4], -1.0,
                                ALU.subtract, ALU.mult).then_inc(va, 1)
                v.wait_ge(aa, ab + 4 * tc + 4)
                v.tensor_scalar(tbuf[pbuf][:], hbuf[pbuf][:], co[:, 2:3], co[:, 4:5],
                                ALU.subtract, ALU.mult)
                v.tensor_tensor(hbuf[pbuf][:], tbuf[pbuf][:], g_bc[:], ALU.mult
                                ).then_inc(va, 1)

        @blk.tensor
        def _(t):
            for tc in range(ntc):
                pbuf = tc % 2
                t.wait_ge(ga, gb + 2 * tc + 2)
                for half in range(2):
                    bidx = 2 * tc + half
                    if bidx >= 2:
                        t.wait_ge(ae, eb + bidx - 1)
                    for cc in range(8):
                        ccc = half * 8 + cc
                        last = t.transpose(pst[bidx % 2][:, cc * P:(cc + 1) * P],
                                           xnb[pbuf][:, ccc * P:(ccc + 1) * P], ident[:])
                    last.then_inc(pa, 1)


def emit_proj_pass(nc, es, c):
    sb = lambda *a: es.enter_context(nc.sbuf_tensor(*a))
    tg = c["tag"]
    xg = [sb(f"{tg}xg{i}", [P, NCD, 512], BF16) for i in range(2)]
    ob = [sb(f"{tg}ob{i}", [P, NCD, 512], BF16) for i in range(2)]
    ps = c["ps"]
    Wres = c["Wres"]
    pmm, vev = c["pmm"], c["vev"]
    s_xg, s_wld = c["s_xg"], c["s_wld"]
    xsv = c["xsrc"].rearrange("tc p (c j) -> tc p c j", c=NCD)
    segs = c["segments"]
    ngr = 8 * len(segs)

    def seg(gi):
        return segs[gi // 8], gi % 8

    with nc.Block(tg) as blk:
        @blk.gpsimd
        def _(g):
            for at in sorted(c["prefetches"]):
                wh, Wd, s_wldN, gate_sem, gate_val = c["prefetches"][at]
                g.wait_ge(gate_sem, gate_val)
                for j in range(4):
                    g.dma_start(out=Wd[:, :, j * 512:(j + 1) * 512],
                                in_=wh[:, j * 512:(j + 1) * 512]
                                .rearrange("(c p) n -> p c n", p=P)).then_inc(s_wldN, 16)

        @blk.sync
        def _(sy):
            for gi in range(ngr):
                sg, gl = seg(gi)
                pbuf = gi % 2
                sy.wait_ge(c["xst_sem"], 16 * (4 * gl + 4))
                if gi >= 2:
                    sy.wait_ge(pmm, 16 * (gi - 1))
                for u in range(4):
                    sy.dma_start(out=xg[pbuf][:, :, u * P:(u + 1) * P],
                                 in_=xsv[4 * gl + u]).then_inc(s_xg, 16)
                if gi >= 1:
                    psg, pgl = seg(gi - 1)
                    sy.wait_ge(vev, 16 * gi)
                    sy.dma_start(out=psg["dst"].rearrange("h p (g j) -> g p h j", j=512)[pgl],
                                 in_=ob[(gi - 1) % 2][:]).then_inc(psg["s_out"], 16)
            psg, pgl = seg(ngr - 1)
            sy.wait_ge(vev, 16 * ngr)
            sy.dma_start(out=psg["dst"].rearrange("h p (g j) -> g p h j", j=512)[pgl],
                         in_=ob[(ngr - 1) % 2][:]).then_inc(psg["s_out"], 16)

        @blk.tensor
        def _(t):
            for gi in range(ngr):
                sg, gl = seg(gi)
                pbuf = gi % 2
                t.wait_ge(s_xg, 16 * (4 * gi + 4))
                for do in range(NCD):
                    gd = gi * 16 + do
                    bank = ps[do % 4]
                    t.wait_ge(s_wld, 16 * (sg["wld_base"] + do // 4 + 1))
                    if gd >= 4:
                        t.wait_ge(vev, gd - 3)
                    for cc in range(NCD):
                        last = t.matmul(bank[:], Wres[:, cc, do * P:(do + 1) * P],
                                        xg[pbuf][:, cc, :],
                                        start=(cc == 0), stop=(cc == NCD - 1))
                    last.then_inc(pmm, 1)

        @blk.vector
        def _(v):
            nst = {id(sg): 0 for sg in segs}
            for gi in range(ngr):
                sg, gl = seg(gi)
                pbuf = gi % 2
                for do in range(NCD):
                    gd = gi * 16 + do
                    v.wait_ge(pmm, gd + 1)
                    if gi >= 2 and do == 0:
                        psg, pgl = seg(gi - 2)
                        v.wait_ge(psg["s_out"], 16 * (pgl + 1))
                    v.tensor_scalar(ob[pbuf][:, do, :], ps[do % 4][:],
                                    sg["bias"][:, do:do + 1], None, ALU.add
                                    ).then_inc(vev, 1)


def emit_p4(nc, es, c):
    sb = lambda *a: es.enter_context(nc.sbuf_tensor(*a))
    xbuf = [sb(f"p4x{i}", [P, D], F32) for i in range(2)]
    hbuf = [sb(f"p4h{i}", [P, D], F32) for i in range(2)]
    tbuf = [sb(f"p4t{i}", [P, D], F32) for i in range(2)]
    xnb = [sb(f"p4n{i}", [P, D], BF16) for i in range(2)]
    xg = [sb("p4g0", [P, NCD, 512], BF16)]
    ob = [sb("p4o0", [P, NCD, 512], BF16)]
    garb = sb("p4garb", [P, D], F32)
    cols = [sb(f"p4c{i}", [P, 8], F32) for i in range(2)]
    ps = c["ps"]
    g_bc, b_bc, ident, pst = c["g_bc"], c["b_bc"], c["ident"], c["pst"]
    Xov = c["Xown"].rearrange("(tc p) d -> tc p d", p=P)
    Qs, bqc, Wres = c["Qs"], c["bqc"], c["Wres"]
    a4, v4, p4, g4, a4e = c["a4"], c["v4"], c["p4"], c["g4"], c["a4e"]
    pmm, vev = c["pmm"], c["vev"]
    mm_b, ev_b = c["mm_base"], c["ev_base"]
    s_x4, s_qo, s_wld = c["s_x4"], c["s_qo"], c["s_wld"]
    qsv = Qs.rearrange("h p (g j) -> g p h j", j=512)

    with nc.Block("p4") as blk:
        @blk.gpsimd
        def _(g):
            for tc in range(NQC):
                pbuf = tc % 2
                g.wait_ge(v4, 4 * tc + 4)
                if tc >= 2:
                    g.wait_ge(p4, 2 * (tc - 2) + 2)
                g.tensor_tensor(xnb[pbuf][:], hbuf[pbuf][:], b_bc[:], ALU.add
                                ).then_inc(g4, 1)
            wh, Wd, s_wldN, gate_sem, gate_val = c["prefetch"]
            g.wait_ge(gate_sem, gate_val)
            for j in range(4):
                g.dma_start(out=Wd[:, :, j * 512:(j + 1) * 512],
                            in_=wh[:, j * 512:(j + 1) * 512]
                            .rearrange("(c p) n -> p c n", p=P)).then_inc(s_wldN, 16)

        @blk.sync
        def _(sy):
            for tc in range(NQC):
                pbuf = tc % 2
                if tc == 0:
                    sy.wait_ge(c["gate_sem"], c["gate_val"])
                if tc >= 2:
                    sy.wait_ge(v4, 4 * (tc - 2) + 2)
                sy.dma_start(out=xbuf[pbuf][:], in_=Xov[tc]).then_inc(s_x4, 16)
                if tc % 4 == 3 and tc >= 7:
                    gi = tc // 4 - 1
                    sy.wait_ge(vev, ev_b + 16 * (gi + 1))
                    sy.dma_start(out=qsv[gi], in_=ob[0][:]).then_inc(s_qo, 16)
            sy.wait_ge(vev, ev_b + 64)
            sy.dma_start(out=qsv[3], in_=ob[0][:]).then_inc(s_qo, 16)

        @blk.scalar
        def _(s):
            for tc in range(NQC):
                pbuf = tc % 2
                co = cols[pbuf]
                s.wait_ge(s_x4, 16 * (tc + 1))
                s.activation(garb[:], xbuf[pbuf][:], AF.Square, accum_out=co[:, 1:2]
                             ).then_inc(a4, 1)
                s.wait_ge(v4, 4 * tc + 1)
                s.activation(co[:, 5:6], co[:, 3:4], AF.Ln)
                s.activation(co[:, 4:5], co[:, 5:6], AF.Exp, scale=-0.5).then_inc(a4, 1)
                s.wait_ge(v4, 4 * tc + 2)
                s.activation(garb[:], hbuf[pbuf][:], AF.Square, accum_out=co[:, 1:2]
                             ).then_inc(a4, 1)
                s.wait_ge(v4, 4 * tc + 3)
                s.activation(co[:, 5:6], co[:, 3:4], AF.Ln)
                s.activation(co[:, 4:5], co[:, 5:6], AF.Exp, scale=-0.5).then_inc(a4, 1)
                gi, u = tc // 4, tc % 4
                dxg = xg[0]
                s.wait_ge(p4, 2 * tc + 1)
                if gi >= 1 and u == 0:
                    s.wait_ge(pmm, mm_b + 16 * gi)
                s.copy(dxg[:, 0:8, u * P:(u + 1) * P],
                              pst[0][:].rearrange("p (c j) -> p c j", c=8)).then_inc(a4e, 1)
                s.wait_ge(p4, 2 * tc + 2)
                s.copy(dxg[:, 8:16, u * P:(u + 1) * P],
                              pst[1][:].rearrange("p (c j) -> p c j", c=8)).then_inc(a4e, 1)

        @blk.vector
        def _(v):
            for tc in range(NQC):
                pbuf = tc % 2
                co = cols[pbuf]
                v.wait_ge(s_x4, 16 * (tc + 1))
                v.reduce_sum(co[:, 0:1], xbuf[pbuf][:], AX.X)
                v.wait_ge(a4, 4 * tc + 1)
                v.tensor_scalar_mul(co[:, 2:3], co[:, 0:1], RN)
                v.tensor_scalar(co[:, 3:4], co[:, 1:2], RN, EPS, ALU.mult, ALU.add)
                v.tensor_scalar(co[:, 5:6], co[:, 2:3], co[:, 2:3], None, ALU.mult)
                v.tensor_scalar(co[:, 3:4], co[:, 5:6], co[:, 3:4], -1.0,
                                ALU.subtract, ALU.mult).then_inc(v4, 1)
                v.wait_ge(a4, 4 * tc + 2)
                if tc >= 2:
                    v.wait_ge(g4, tc - 1)
                v.tensor_scalar(hbuf[pbuf][:], xbuf[pbuf][:], co[:, 2:3], co[:, 4:5],
                                ALU.subtract, ALU.mult).then_inc(v4, 1)
                v.reduce_sum(co[:, 0:1], hbuf[pbuf][:], AX.X)
                v.wait_ge(a4, 4 * tc + 3)
                v.tensor_scalar_mul(co[:, 2:3], co[:, 0:1], RN)
                v.tensor_scalar(co[:, 3:4], co[:, 1:2], RN, EPS, ALU.mult, ALU.add)
                v.tensor_scalar(co[:, 5:6], co[:, 2:3], co[:, 2:3], None, ALU.mult)
                v.tensor_scalar(co[:, 3:4], co[:, 5:6], co[:, 3:4], -1.0,
                                ALU.subtract, ALU.mult).then_inc(v4, 1)
                v.wait_ge(a4, 4 * tc + 4)
                v.tensor_scalar(tbuf[pbuf][:], hbuf[pbuf][:], co[:, 2:3], co[:, 4:5],
                                ALU.subtract, ALU.mult)
                v.tensor_tensor(hbuf[pbuf][:], tbuf[pbuf][:], g_bc[:], ALU.mult
                                ).then_inc(v4, 1)
                if tc % 4 == 3:
                    gi = tc // 4
                    for do in range(NCD):
                        gd = gi * 16 + do
                        v.wait_ge(pmm, mm_b + gd + 1)
                        if gi >= 1 and do == 0:
                            v.wait_ge(s_qo, 16 * gi)
                        v.tensor_scalar(ob[0][:, do, :], ps[do % 4][:],
                                        bqc[:, do:do + 1], QSC, ALU.add, ALU.mult
                                        ).then_inc(vev, 1)

        @blk.tensor
        def _(t):
            for tc in range(NQC):
                pbuf = tc % 2
                t.wait_ge(g4, tc + 1)
                for half in range(2):
                    bidx = 2 * tc + half
                    if bidx >= 2:
                        t.wait_ge(a4e, bidx - 1)
                    for cc in range(8):
                        ccc = half * 8 + cc
                        last = t.transpose(pst[bidx % 2][:, cc * P:(cc + 1) * P],
                                           xnb[pbuf][:, ccc * P:(ccc + 1) * P], ident[:])
                    last.then_inc(p4, 1)
                if tc % 4 == 3:
                    gi = tc // 4
                    t.wait_ge(a4e, 2 * (tc + 1))
                    for do in range(NCD):
                        gd = gi * 16 + do
                        bank = ps[do % 4]
                        t.wait_ge(s_wld, 16 * (8 + do // 4 + 1))
                        if gd >= 4:
                            t.wait_ge(vev, ev_b + gd - 3)
                        for cc in range(NCD):
                            last = t.matmul(bank[:], Wres[:, cc, do * P:(do + 1) * P],
                                            xg[0][:, cc, :],
                                            start=(cc == 0), stop=(cc == NCD - 1))
                        last.then_inc(pmm, 1)


def emit_p5(nc, es, c):
    sb = lambda *a: es.enter_context(nc.sbuf_tensor(*a))
    ktb = [sb(f"p5k{i}", [P, S], BF16) for i in range(2)]
    vtb = [sb(f"p5v{i}", [P, S], BF16) for i in range(2)]
    qb = [sb(f"p5q{i}", [P, TOK], BF16) for i in range(2)]
    vtok = sb("p5vt", [P, NTC, P], BF16)
    Ssb = [sb(f"p5s{i}", [P, S], F32) for i in range(2)]
    Psb = [sb(f"p5p{i}", [P, S], BF16) for i in range(2)]
    ptb_sb = sb("p5pt", [P, 8, P], BF16)
    avh = [sb(f"p5a{i}", [P, NQC, DH], BF16) for i in range(2)]
    mx = [sb(f"p5m{i}", [P, 8], F32) for i in range(2)]
    col = [sb(f"p5c{i}", [P, 4], F32) for i in range(2)]
    ps, pst = c["ps"], c["pst"]
    Ks, Vs, Qs, AVs = c["Ks"], c["Vs"], c["Qs"], c["AVs"]
    dmk, ident = c["dmk"], c["ident"]
    s_kl, s_vl, s_ql, s_avo = c["s_kl"], c["s_vl"], c["s_ql"], c["s_avo"]
    s_cst = c["s_cst"]
    p5s, p5t, p5a = c["p5s"], c["p5t"], c["p5a"]
    v5e, v5m, a5x, v5r, a5e = c["v5e"], c["v5m"], c["a5x"], c["v5r"], c["a5e"]
    v5c = c["v5c"]
    avv = AVs.rearrange("q p h d -> p q h d")

    tile_of_E = {}
    gt = 0
    for h in range(NH5):
        for qci in range(NQC):
            E = h * NQC + qci
            tl = []
            for (k0, w) in tiles_of(qci):
                tl.append((gt, k0, w))
                gt += 1
            tile_of_E[E] = tl

    batches = []
    bodyb_of_E = {}
    av_cnt = 0
    ptb_cnt = 0
    av_through_E = {}
    bt_through_E = {}

    def emit_body_plan(E):
        nonlocal av_cnt, ptb_cnt
        qci = E % NQC
        nkc = 2 * (qci + 1)
        bl = []
        k = 0
        while k < nkc:
            kcs = list(range(k, min(k + 4, nkc)))
            b = dict(kind="pt", E=E, h=E // NQC, gb=len(batches), ptb=ptb_cnt,
                     kcs=kcs, av_idx=av_cnt)
            batches.append(b)
            bl.append(b)
            ptb_cnt += 1
            av_cnt += 1
            k += 4
        bodyb_of_E[E] = bl
        av_through_E[E] = av_cnt
        bt_through_E[E] = len(batches)

    av_through_head = {-1: 0}
    for h in range(NH5):
        if h > 0:
            emit_body_plan(h * NQC - 1)
            av_through_head[h - 1] = av_cnt
        for b in range(8):
            batches.append(dict(kind="vt", h=h, b=b, gb=len(batches),
                                kcs=list(range(4 * b, 4 * b + 4))))
        for qci in range(1, NQC):
            emit_body_plan(h * NQC + qci - 1)
    emit_body_plan(NH5 * NQC - 1)
    av_through_head[NH5 - 1] = av_cnt

    def w(eng, semh, v):
        if v > 0:
            eng.wait_ge(semh, v)

    with nc.Block("p5") as blk:
        @blk.sync
        def _(sy):
            def loads(h):
                sy.dma_start(out=ktb[h % 2][:], in_=Ks[h]).then_inc(s_kl, 16)
                sy.dma_start(out=vtb[h % 2][:], in_=Vs[h]).then_inc(s_vl, 16)
                sy.dma_start(out=qb[h % 2][:], in_=Qs[h]).then_inc(s_ql, 16)
            sy.wait_ge(c["s_ko"], 16 * 8)
            sy.wait_ge(c["s_vo"], 16 * 8)
            sy.wait_ge(c["s_qo"], 16 * 4)
            loads(0)
            loads(1)
            for h in range(NH5):
                if h + 2 < NH5:
                    w(sy, p5a, av_through_head[h])
                    loads(h + 2)
                sy.wait_ge(v5r, NQC * (h + 1))
                sy.dma_start(out=avv[:, :, h, :], in_=avh[h % 2][:]).then_inc(s_avo, 16)

        @blk.tensor
        def _(t):
            def body(E):
                h, qci = E // NQC, E % NQC
                t.wait_ge(a5x, E + 1)
                w(t, v5r, E - 1)
                avb = ps[2 + E % 2]
                nkc = 2 * (qci + 1)
                def avmm(b):
                    t.wait_ge(a5e, b["gb"] + 1)
                    for i, kc in enumerate(b["kcs"]):
                        last = t.matmul(avb[:, 0:DH],
                                        ptb_sb[:, (b["ptb"] % 2) * 4 + i, :],
                                        vtok[:, kc, :],
                                        start=(kc == 0), stop=(kc == nkc - 1))
                    last.then_inc(p5a, 1)

                bl = bodyb_of_E[E]
                for bi, b in enumerate(bl):
                    gb = b["gb"]
                    if gb >= 2:
                        t.wait_ge(a5e, gb - 1)
                    for i, kc in enumerate(b["kcs"]):
                        last = t.transpose(
                            pst[gb % 2][:, i * P:(i + 1) * P],
                            Psb[E % 2][:, kc * P:(kc + 1) * P], ident[:])
                    last.then_inc(p5t, 1)
                    if bi >= 1:
                        avmm(bl[bi - 1])
                avmm(bl[-1])

            for h in range(NH5):
                if h > 0:
                    body(h * NQC - 1)
                t.wait_ge(s_vl, 16 * (h + 1))
                vts = [bt for bt in batches if bt["kind"] == "vt" and bt["h"] == h]
                for bt in vts:
                    gb = bt["gb"]
                    if gb >= 2:
                        t.wait_ge(a5e, gb - 1)
                    for i, kc in enumerate(bt["kcs"]):
                        last = t.transpose(
                            pst[gb % 2][:, i * P:(i + 1) * P],
                            vtb[h % 2][:, kc * P:(kc + 1) * P], ident[:])
                    last.then_inc(p5t, 1)
                t.wait_ge(s_kl, 16 * (h + 1))
                t.wait_ge(s_ql, 16 * (h + 1))

                sbanks = (0, 1, 4, 5)

                def scores(qci):
                    E = h * NQC + qci
                    for (gtx, k0, wd) in tile_of_E[E]:
                        w(t, v5e, gtx - 3)
                        t.matmul(ps[sbanks[gtx % 4]][:, 0:wd],
                                 qb[h % 2][:, qci * P:(qci + 1) * P],
                                 ktb[h % 2][:, k0:k0 + wd],
                                 start=True, stop=True).then_inc(p5s, 1)

                scores(0)
                scores(1)
                for qci in range(NQC):
                    if qci + 2 < NQC:
                        scores(qci + 2)
                    if qci < NQC - 1:
                        body(h * NQC + qci)
            body(NH5 * NQC - 1)

        @blk.scalar
        def _(s):
            def evac(bt):
                gb = bt["gb"]
                s.wait_ge(p5t, gb + 1)
                n = len(bt["kcs"])
                if bt["kind"] == "vt":
                    w(s, p5a, av_through_head[bt["h"] - 1])
                    dst = vtok[:, bt["b"] * 4:bt["b"] * 4 + n, :]
                else:
                    w(s, p5a, bt["ptb"] - 1)
                    dst = ptb_sb[:, (bt["ptb"] % 2) * 4:(bt["ptb"] % 2) * 4 + n, :]
                s.copy(dst, pst[gb % 2][:, 0:n * P]
                       .rearrange("p (k d) -> p k d", d=P)).then_inc(a5e, 1)

            def avevac(E):
                h2, qci2 = E // NQC, E % NQC
                s.wait_ge(v5c, E + 1)
                s.wait_ge(p5a, av_through_E[E])
                if qci2 == 0:
                    w(s, s_avo, 16 * (h2 - 1))
                s.activation(avh[h2 % 2][:, qci2, :], ps[2 + E % 2][:, 0:DH],
                             AF.Copy, scale=col[E % 2][:, 2:3]).then_inc(v5r, 1)

            for h in range(NH5):
                if h > 0:
                    for bt in bodyb_of_E[h * NQC - 1]:
                        evac(bt)
                for bt in [bt for bt in batches if bt["kind"] == "vt" and bt["h"] == h]:
                    evac(bt)
                if h > 0:
                    avevac(h * NQC - 1)
                for qci in range(NQC):
                    E = h * NQC + qci
                    s.wait_ge(v5m, E + 1)
                    if E >= 2:
                        s.wait_ge(p5t, bt_through_E[E - 2])
                        s.wait_ge(v5r, E - 1)
                    W = 256 * (qci + 1)
                    s.activation(Psb[E % 2][:, 0:W], Ssb[E % 2][:, 0:W], AF.Exp,
                                 bias=col[E % 2][:, 0:1],
                                 accum_out=col[E % 2][:, 1:2]).then_inc(a5x, 1)
                    if qci > 0:
                        for bt in bodyb_of_E[E - 1]:
                            evac(bt)
                        avevac(E - 1)
            for bt in bodyb_of_E[NH5 * NQC - 1]:
                evac(bt)
            avevac(NH5 * NQC - 1)

        @blk.vector
        def _(v):
            sbanks = (0, 1, 4, 5)
            for E in range(NH5 * NQC):
                h, qci = E // NQC, E % NQC
                if E == 0:
                    v.wait_ge(s_cst, 16 * 6)
                nt = len(tile_of_E[E])
                for t_i, (gtx, k0, wd) in enumerate(tile_of_E[E]):
                    v.wait_ge(p5s, gtx + 1)
                    if t_i == 0:
                        w(v, a5x, E - 1)
                    v.tensor_scalar(Ssb[E % 2][:, k0:k0 + wd],
                                    ps[sbanks[gtx % 4]][:, 0:wd],
                                    1.0, None, ALU.mult, ALU.max,
                                    accum_out=mx[E % 2][:, t_i:t_i + 1]).then_inc(v5e, 1)
                W = 256 * (qci + 1)
                v.tensor_tensor(Ssb[E % 2][:, W - 256:W], Ssb[E % 2][:, W - 256:W],
                                dmk[:], ALU.add)
                v.reduce_max(col[E % 2][:, 0:1], mx[E % 2][:, 0:nt], AX.X,
                             negate=True).then_inc(v5m, 1)
                if E >= 1:
                    v.wait_ge(a5x, E)
                    v.reciprocal(col[(E - 1) % 2][:, 2:3],
                                 col[(E - 1) % 2][:, 1:2]).then_inc(v5c, 1)
            v.wait_ge(a5x, NH5 * NQC)
            v.reciprocal(col[(NH5 * NQC - 1) % 2][:, 2:3],
                         col[(NH5 * NQC - 1) % 2][:, 1:2]).then_inc(v5c, 1)


def emit_p67(nc, es, c):
    sb = lambda *a: es.enter_context(nc.sbuf_tensor(*a))
    oa = sb("oa", [P, 8, D], F32)
    h2T = sb("h2T", [P, NCD, 1024], BF16)
    ab = sb("p67ab", [P, 7680], BF16)
    af = sb("p67af", [P, 2048], F32)
    avb = [ab[:, 0:2048].rearrange("p (h d) -> p h d", h=NH),
           ab[:, 2048:4096].rearrange("p (h d) -> p h d", h=NH)]
    avT = ab[:, 4096:6144].rearrange("p (c d) -> p c d", c=NCD)
    x6b = af
    sa = [af[:, 0:512], af[:, 512:1024]]
    w2x = ab[:, 0:5632].rearrange("p (i j) -> p i j", i=NIB)
    h2bf = sb("p6h2", [P, D], BF16)
    bo_bc = sb("p6bo", [P, D], F32)
    co6 = sb("p6c", [P, 8], F32)
    comb = sb("p7comb", [P, 8, 16], F32)
    gl = sb("p7gl", [P, 16], F32)
    gp = sb("p7gp", [P, 16], F32)
    t2b = sb("p7t2", [P, 16], F32)
    gc = sb("p7gc", [P, 8], F32)

    ps, pst = c["ps"], c["pst"]
    ident, ones1 = c["ident"], c["ones1"]
    rowst = af[0:1, :]
    Wres, gwt_sb = c["Wres"], c["gwt_sb"]
    AVs, Xown, Y, boh = c["AVs"], c["Xown"], c["Y"], c["boh"]
    w1h, w3h, w2h = c["w1h"], c["w3h"], c["w2h"]
    x1sp, h2sp = c["x1sp"], c["h2sp"]
    s_avl, s_x6 = c["s_avl"], c["s_x6"]
    s_sp, s_spl, s_y = c["s_sp"], c["s_spl"], c["s_y"]
    s_w13, s_w2, s_wld, s_cst = c["s_w13"], c["s_w2"], c["s_wld"], c["s_cst"]
    s_gw = c["s_gw"]
    p6t, a6e, p6m, v6a = c["p6t"], c["a6e"], c["p6m"], c["v6a"]
    v5e = c["v5e"]
    a6s, v6s = c["a6s"], c["v6s"]
    p7g, v7g, p7a, a7s = c["p7g"], c["v7g"], c["p7a"], c["a7s"]
    v7u, p7b, v7y = c["v7u"], c["p7b"], c["v7y"]
    Xov = Xown.rearrange("(tc p) d -> tc p d", p=P)
    Yv = Y.rearrange("(tc p) d -> tc p d", p=P)
    halves = [1, 0]

    p6t_avT, p6t_h2T = {}, {}
    cnt = 0
    for gq in range(16):
        cnt += 2
        p6t_avT[gq] = cnt
        if gq >= 1:
            cnt += 2
            p6t_h2T[gq - 1] = cnt
    cnt += 2
    p6t_h2T[15] = cnt

    with nc.Block("p6") as blk:
        @blk.sync
        def _(sy):
            sy.dma_start(out=rowst, in_=boh[None, :]).then_inc(s_cst, 16)
            for gq in range(16):
                hf, qcl = halves[gq // 8], gq % 8
                qc = hf * 8 + qcl
                if gq == 0:
                    sy.wait_ge(c["s_avo"], 16 * NH)
                if gq >= 2:
                    sy.wait_ge(p6t, p6t_avT[gq - 2])
                sy.dma_start(out=avb[gq % 2][:], in_=AVs[qc]).then_inc(s_avl, 16)
                if gq == 8:
                    for tc in range(8):
                        sy.wait_ge(v6a, 8 * tc + 8)
                        sy.dma_start(out=x1sp[tc], in_=oa[:, tc, :]).then_inc(s_sp, 16)
                    sy.wait_ge(a6e, 4 + p6t_h2T[7])
                    sy.dma_start(out=h2sp[:, :, :], in_=h2T[:]).then_inc(s_sp, 16)
                if gq >= 1:
                    sy.wait_ge(v6a, 8 * gq)
                else:
                    sy.wait_ge(p6m, 4)
                sy.dma_start(out=x6b[:], in_=Xov[qc]).then_inc(s_x6, 16)

        @blk.tensor
        def _(t):
            t.wait_ge(s_cst, 16 * 7)
            t.wait_ge(c["g1"], 1)
            t.wait_ge(v5e, 72 * NH5)
            for j in range(4):
                if j > 0:
                    t.wait_ge(a6e, j)
                t.matmul(ps[4][:], ones1[:], rowst[0:1, j * 512:(j + 1) * 512],
                         start=True, stop=True).then_inc(p6m, 1)

            def h2t(g2, bat0):
                for half_b in range(2):
                    bat = bat0 + half_b
                    t.wait_ge(v6s, 2 * g2 + 2)
                    if bat >= 2:
                        t.wait_ge(a6e, 4 + bat - 1)
                    for i in range(8):
                        cc = half_b * 8 + i
                        last = t.transpose(pst[bat % 2][:, i * P:(i + 1) * P],
                                           h2bf[:, cc * P:(cc + 1) * P], ident[:])
                    last.then_inc(p6t, 1)

            for gq in range(16):
                t.wait_ge(s_avl, 16 * (gq + 1))
                for half_b in range(2):
                    bat = p6t_avT[gq] - 2 + half_b
                    if bat >= 2:
                        t.wait_ge(a6e, 4 + bat - 1)
                    for i in range(8):
                        cc = half_b * 8 + i
                        last = t.transpose(pst[bat % 2][:, i * P:(i + 1) * P],
                                           avb[gq % 2][:, cc, :], ident[:])
                    last.then_inc(p6t, 1)
                if gq >= 1:
                    h2t(gq - 1, p6t_h2T[gq - 1] - 2)
                t.wait_ge(a6e, 4 + p6t_avT[gq])
                for j in range(4):
                    n = gq * 4 + j
                    t.wait_ge(s_wld, 16 * (12 + j + 1))
                    if n >= 2:
                        t.wait_ge(v6a, 2 * (n - 2) + 2)
                    for cc in range(NCD):
                        last = t.matmul(ps[4 + n % 2][:], avT[:, cc, :],
                                        Wres[:, cc, j * 512:(j + 1) * 512],
                                        start=(cc == 0), stop=(cc == NCD - 1))
                    last.then_inc(p6m, 1)
            h2t(15, p6t_h2T[15] - 2)

        @blk.scalar
        def _(s):
            for j in range(4):
                s.wait_ge(p6m, j + 1)
                s.copy(bo_bc[:, j * 512:(j + 1) * 512], ps[4][:]).then_inc(a6e, 1)

            def h2ev(g2):
                hf2, qcl2 = halves[g2 // 8], g2 % 8
                if g2 == 8:
                    s.wait_ge(s_sp, 16 * 9)
                for half_b in range(2):
                    bat = p6t_h2T[g2] - 2 + half_b
                    s.wait_ge(p6t, bat + 1)
                    s.copy(h2T[:, half_b * 8:half_b * 8 + 8, qcl2 * P:(qcl2 + 1) * P],
                           pst[bat % 2][:].rearrange("p (k d) -> p k d", d=P)).then_inc(a6e, 1)

            for gq in range(16):
                for half_b in range(2):
                    bat = p6t_avT[gq] - 2 + half_b
                    s.wait_ge(p6t, bat + 1)
                    if half_b == 0:
                        s.wait_ge(p6m, 4 + 4 * gq)
                    s.copy(avT[:, half_b * 8:half_b * 8 + 8, :],
                           pst[bat % 2][:].rearrange("p (k d) -> p k d", d=P)).then_inc(a6e, 1)
                if gq >= 1:
                    h2ev(gq - 1)
                s.wait_ge(v6a, 8 * gq + 8)
                if gq >= 1:
                    s.wait_ge(p6t, p6t_h2T[gq - 1])
                s.activation(h2bf[:], oa[:, gq % 8, :], AF.Square,
                             accum_out=co6[:, 1:2]).then_inc(a6s, 1)
                s.wait_ge(v6s, 2 * gq + 1)
                s.activation(co6[:, 5:6], co6[:, 3:4], AF.Ln)
                s.activation(co6[:, 4:5], co6[:, 5:6], AF.Exp, scale=-0.5).then_inc(a6s, 1)
            h2ev(15)

        @blk.vector
        def _(v):
            for gq in range(16):
                qcl = gq % 8
                for j in range(4):
                    n = gq * 4 + j
                    v.wait_ge(p6m, 4 + n + 1)
                    if gq == 8 and j == 0:
                        v.wait_ge(s_sp, 16 * 8)
                    v.scalar_tensor_tensor(oa[:, qcl, j * 512:(j + 1) * 512],
                                           ps[4 + n % 2][:], 1.0,
                                           bo_bc[:, j * 512:(j + 1) * 512],
                                           ALU.mult, ALU.add).then_inc(v6a, 1)
                    v.wait_ge(s_x6, 16 * (gq + 1))
                    v.scalar_tensor_tensor(oa[:, qcl, j * 512:(j + 1) * 512],
                                           oa[:, qcl, j * 512:(j + 1) * 512], 1.0,
                                           x6b[:, j * 512:(j + 1) * 512],
                                           ALU.mult, ALU.add).then_inc(v6a, 1)
                v.reduce_sum(co6[:, 0:1], oa[:, qcl, :], AX.X)
                v.wait_ge(a6s, 2 * gq + 1)
                v.tensor_scalar_mul(co6[:, 2:3], co6[:, 0:1], RN)
                v.tensor_scalar(co6[:, 3:4], co6[:, 1:2], RN, EPS, ALU.mult, ALU.add)
                v.tensor_scalar(co6[:, 5:6], co6[:, 2:3], co6[:, 2:3], None, ALU.mult)
                v.tensor_scalar(co6[:, 3:4], co6[:, 5:6], co6[:, 3:4], -1.0,
                                ALU.subtract, ALU.mult).then_inc(v6s, 1)
                v.wait_ge(a6s, 2 * gq + 2)
                v.tensor_scalar(h2bf[:], oa[:, qcl, :], co6[:, 2:3], co6[:, 4:5],
                                ALU.subtract, ALU.mult).then_inc(v6s, 1)

    flat = Wres[:].rearrange("p c d -> p (c d)")
    u_v = flat[:, 0:11264].rearrange("p (i t) -> p i t", i=NIB)
    w1b = [flat[:, 11264 + k * 2048:11264 + (k + 1) * 2048]
           .rearrange("p (c j) -> p c j", c=NCD) for k in range(2)]
    w3b = [flat[:, 15360 + k * 2048:15360 + (k + 1) * 2048]
           .rearrange("p (c j) -> p c j", c=NCD) for k in range(2)]
    w2b = [flat[:, 19456:25088].rearrange("p (i j) -> p i j", i=NIB),
           flat[:, 25088:30720].rearrange("p (i j) -> p i j", i=NIB),
           w2x]

    NGIB = NE7 * NIB
    M7 = 32 * NE7
    NW2 = 4 * NE7
    NST = 2 * NGIB
    A7H = 16 + NST

    with nc.Block("p7") as blk:
        @blk.gpsimd
        def _(g):
            for hx in range(2):
                if hx == 0:
                    g.wait_ge(p6m, 4 + 64)
                else:
                    g.wait_ge(p7a, NST)

                def w2dma(gg):
                    e2, j2 = gg // 4, gg % 4
                    if gg >= 3:
                        g.wait_ge(p7b, M7 * hx + 8 * (gg - 2))
                    elif hx == 1 and gg == 0:
                        g.wait_ge(p7b, M7)
                    g.dma_start(out=w2b[gg % 3],
                                in_=w2h[e2, :, :, j2 * 512:(j2 + 1) * 512]
                                .rearrange("i p j -> p i j")).then_inc(s_w2, 16)

                def w13dma(gib):
                    e2, ib2 = gib // NIB, gib % NIB
                    if gib >= 2:
                        g.wait_ge(p7a, NST * hx + 2 * (gib - 1))
                    elif hx == 1 and gib == 0:
                        g.wait_ge(p7a, NST)
                    g.dma_start(out=w1b[gib % 2][:], in_=w1h[e2, ib2]).then_inc(s_w13, 16)
                    g.dma_start(out=w3b[gib % 2][:], in_=w3h[e2, ib2]).then_inc(s_w13, 16)

                for gg in range(3):
                    w2dma(gg)
                cur = 0
                for gib in range(min(6, NGIB)):
                    w13dma(gib)
                    cur = gib + 1
                for e in range(NE7):
                    hi = min(e * NIB + 12, NGIB - 1)
                    while cur <= hi:
                        w13dma(cur)
                        cur += 1
                    for gg in range(4 * e + 3, min(4 * e + 7, NW2)):
                        w2dma(gg)
                while cur < NGIB:
                    w13dma(cur)
                    cur += 1

        @blk.sync
        def _(sy):
            for hx, hf in enumerate([0, 1]):
                for tc in range(8):
                    sy.wait_ge(v7y, M7 * hx + 32 * (NE7 - 1) + 24 + tc + 1)
                    sy.dma_start(out=Yv[hf * 8 + tc], in_=oa[:, tc, :]).then_inc(s_y, 16)
            sy.wait_ge(s_y, 16 * 16)

        @blk.tensor
        def _(t):
            for hx, hf in enumerate([0, 1]):
                for tc in range(8):
                    if hx == 0 and tc == 0:
                        t.wait_ge(a6e, 4 + p6t_h2T[15])
                        t.wait_ge(s_gw, 16)
                    if hx == 1 and tc == 0:
                        t.wait_ge(s_spl, 16 * 9)
                    if tc >= 2:
                        t.wait_ge(v7g, 16 * hx + 2 * (tc - 2) + 1)
                    for cc in range(NCD):
                        last = t.matmul(ps[4 + tc % 2][:, 0:15],
                                        h2T[:, cc, tc * P:(tc + 1) * P],
                                        gwt_sb[:, cc, :],
                                        start=(cc == 0), stop=(cc == NCD - 1))
                    last.then_inc(p7g, 1)
                for e in range(NE7):
                    for ib in range(NIB):
                        gib = e * NIB + ib
                        for u in range(2):
                            st = 2 * gib + u
                            ab, bb = ps[st % 2], ps[2 + st % 2]
                            if st >= 2:
                                t.wait_ge(v7u, NST * hx + st - 1)
                            t.wait_ge(s_w13, 32 * NGIB * hx + 32 * (gib + 1))
                            for cc in range(NCD):
                                t.matmul(ab[:], w1b[gib % 2][:, cc, :],
                                         h2T[:, cc, u * 512:(u + 1) * 512],
                                         start=(cc == 0), stop=(cc == NCD - 1))
                            for cc in range(NCD):
                                last = t.matmul(bb[:], w3b[gib % 2][:, cc, :],
                                                h2T[:, cc, u * 512:(u + 1) * 512],
                                                start=(cc == 0), stop=(cc == NCD - 1))
                            last.then_inc(p7a, 1)
                    t.wait_ge(v7u, NST * hx + 22 * (e + 1))
                    for j in range(4):
                        gj = 4 * e + j
                        t.wait_ge(s_w2, 16 * (NW2 * hx + gj + 1))
                        for tc in range(8):
                            cj = 32 * e + 8 * j + tc
                            yb = ps[4 + cj % 2]
                            if cj >= 2:
                                t.wait_ge(v7y, M7 * hx + cj - 1)
                            for ic in range(NIB):
                                last = t.matmul(yb[:], u_v[:, ic, tc * P:(tc + 1) * P],
                                                w2b[gj % 3][:, ic, :],
                                                start=(ic == 0), stop=(ic == NIB - 1))
                            last.then_inc(p7b, 1)

        @blk.scalar
        def _(s):
            for hx in range(2):
                if hx == 1:
                    s.wait_ge(p7a, NST)
                    for tc in range(8):
                        s.wait_ge(s_y, 16 * (tc + 1))
                        s.dma_start(out=oa[:, tc, :], in_=x1sp[tc]).then_inc(s_spl, 16)
                    s.dma_start(out=h2T[:], in_=h2sp[:, :, :]).then_inc(s_spl, 16)
                for tc in range(8):
                    s.wait_ge(v7g, 16 * hx + 2 * tc + 1)
                    s.activation(gp[:, 0:15], gl[:, 0:15], AF.Exp, bias=gc[:, 1:2],
                                 accum_out=gc[:, 2:3]).then_inc(a7s, 1)
                    s.wait_ge(v7g, 16 * hx + 2 * tc + 2)
                    s.activation(gp[:, 0:15], gp[:, 0:15], AF.Copy,
                                 scale=gc[:, 3:4]).then_inc(a7s, 1)
                for gib in range(NGIB):
                    for u in range(2):
                        st = 2 * gib + u
                        s.wait_ge(p7a, NST * hx + st + 1)
                        if st >= 2:
                            s.wait_ge(v7u, NST * hx + st - 1)
                        s.activation(sa[st % 2][:], ps[st % 2][:], AF.Silu
                                     ).then_inc(a7s, 1)

        @blk.vector
        def _(v):
            for hx in range(2):
                v.memset(comb[:, :, 15:16], 1.0)
                for tc in range(8):
                    v.wait_ge(p7g, 8 * hx + tc + 1)
                    if tc >= 1:
                        v.wait_ge(a7s, A7H * hx + 2 * (tc - 1) + 1)
                    v.tensor_scalar(gl[:, 0:15], ps[4 + tc % 2][:, 0:15], 1.0, None, ALU.mult)
                    v.reduce_max(gc[:, 1:2], gl[:, 0:15], AX.X,
                                 negate=True).then_inc(v7g, 1)
                    v.wait_ge(a7s, A7H * hx + 2 * tc + 1)
                    v.reciprocal(gc[:, 3:4], gc[:, 2:3]).then_inc(v7g, 1)
                    v.wait_ge(a7s, A7H * hx + 2 * tc + 2)
                    v.reduce_max(gc[:, 4:5], gp[:, 0:15], AX.X)
                    v.tensor_scalar_mul(gc[:, 5:6], gc[:, 4:5], 1.0)
                    v.scalar_tensor_tensor(t2b[:, 0:15], gp[:, 0:15], gc[:, 4:5],
                                           gp[:, 0:15], ALU.is_lt, ALU.mult)
                    v.tensor_scalar_mul(gc[:, 5:6], gc[:, 4:5], 1.0)
                    v.reduce_max(gc[:, 6:7], t2b[:, 0:15], AX.X)
                    v.tensor_scalar_mul(gc[:, 5:6], gc[:, 6:7], 1.0)
                    v.scalar_tensor_tensor(comb[:, tc, 0:15], gp[:, 0:15], gc[:, 6:7],
                                           gp[:, 0:15], ALU.is_ge, ALU.mult)
                for gib in range(NGIB):
                    ib = gib % NIB
                    for u in range(2):
                        st = 2 * gib + u
                        v.wait_ge(a7s, A7H * hx + 16 + st + 1)
                        if gib == 0 and u == 0 and hx == 1:
                            v.wait_ge(p7b, M7)
                        v.tensor_tensor(u_v[:, ib, u * 512:(u + 1) * 512],
                                        sa[st % 2][:], ps[2 + st % 2][:], ALU.mult
                                        ).then_inc(v7u, 1)
                    if ib == NIB - 1:
                        e = gib // NIB
                        for j in range(4):
                            for tc in range(8):
                                cj = 32 * e + 8 * j + tc
                                v.wait_ge(p7b, M7 * hx + cj + 1)
                                if e == 0 and j == 0 and tc == 0:
                                    v.wait_ge(v7g, 16 * hx + 16)
                                v.scalar_tensor_tensor(oa[:, tc, j * 512:(j + 1) * 512],
                                                       ps[4 + cj % 2][:],
                                                       comb[:, tc, e:e + 1],
                                                       oa[:, tc, j * 512:(j + 1) * 512],
                                                       ALU.mult, ALU.add).then_inc(v7y, 1)


def emit_dbg(nc, dbg, items):
    with nc.semaphore("sdbg") as sd, nc.Block("dbg") as blk:
        @blk.sync
        def _(sy):
            n = 0
            for name, src, semh, need in items:
                if name not in dbg:
                    continue
                sy.wait_ge(semh, need)
                if len(src.shape) == 3:
                    sy.dma_start(out=dbg[name][:, :, :], in_=src[:, :, :]).then_inc(sd, 16)
                else:
                    sy.dma_start(out=dbg[name][:, :, :, :], in_=src[:, :, :, :]).then_inc(sd, 16)
                n += 1
            sy.wait_ge(sd, 16 * n)


def kernel(**inp):
    X = np.asarray(inp["L0_attn_X"], dtype=np.float32)
    assert X.shape == (4, S, D)

    w1 = np.asarray(inp["L0_ffn_w1"], np.float32)
    w3 = np.asarray(inp["L0_ffn_w3"], np.float32)
    w2 = np.asarray(inp["L0_ffn_w2"], np.float32)
    w1a = np.concatenate([w1, np.asarray(inp["L0_ffn_sw1"], np.float32)[None]], axis=0)
    w3a = np.concatenate([w3, np.asarray(inp["L0_ffn_sw3"], np.float32)[None]], axis=0)
    w2a = np.concatenate([w2, np.asarray(inp["L0_ffn_sw2"], np.float32)[None]], axis=0)
    w1s = np.ascontiguousarray(w1a.reshape(NE, NIB, P, NCD, P).transpose(0, 1, 4, 3, 2))
    w3s = np.ascontiguousarray(w3a.reshape(NE, NIB, P, NCD, P).transpose(0, 1, 4, 3, 2))
    w2s = np.ascontiguousarray(w2a.reshape(NE, D, NIB, P).transpose(0, 2, 3, 1))
    gwt = np.ascontiguousarray(np.asarray(inp["L0_ffn_gate_w"], np.float32).T)

    tri = (np.arange(P)[None, :] > np.arange(P)[:, None])
    common = dict(
        wq=np.ascontiguousarray(np.asarray(inp["L0_attn_Wq"], np.float32)),
        wk=np.ascontiguousarray(np.asarray(inp["L0_attn_Wk"], np.float32)),
        wv=np.ascontiguousarray(np.asarray(inp["L0_attn_Wv"], np.float32)),
        wo=np.ascontiguousarray(np.asarray(inp["L0_attn_Wo"], np.float32)),
        bq=np.ascontiguousarray(np.asarray(inp["L0_attn_bq"], np.float32)),
        bk=np.ascontiguousarray(np.asarray(inp["L0_attn_bk"], np.float32)),
        bv=np.ascontiguousarray(np.asarray(inp["L0_attn_bv"], np.float32)),
        bo=np.ascontiguousarray(np.asarray(inp["L0_attn_bo"], np.float32)),
        lng=np.ascontiguousarray(np.asarray(inp["L0_attn_ln_g"], np.float32)),
        lnb=np.ascontiguousarray(np.asarray(inp["L0_attn_ln_b"], np.float32)),
        gwt=gwt, w1s=w1s, w3s=w3s, w2s=w2s)

    in_maps = []
    for cix in range(8):
        sq, par = cix // 2, cix % 2
        dm = np.zeros((P, 256), np.float32)
        if par == 1:
            dm[:, P:][tri] = NEG
        else:
            dm[:, :P][tri] = NEG
            dm[:, P:] = NEG
        m = dict(common)
        m["xall"] = np.ascontiguousarray(X[sq])
        m["xown"] = np.ascontiguousarray(X[sq].reshape(NTC, P, D)[par::2].reshape(TOK, D))
        m["dmask"] = dm
        in_maps.append(m)

    nc, dbg = build_program()
    res = run_bass_kernel_spmd(nc, in_maps[:CORES], core_ids=list(range(CORES)))

    if DBG or CORES < 8:
        return res

    Yf = np.empty((4, S, D), np.float32)
    for cix in range(8):
        sq, par = cix // 2, cix % 2
        Yf[sq].reshape(NTC, P, D)[par::2] = res.results[cix]["y"].reshape(NQC, P, D)
    return Yf


# version 13
# speedup vs baseline: 1.1020x; 1.0781x over previous
import numpy as np
from contextlib import ExitStack

import concourse.bass as bass
import concourse.mybir as mybir
from concourse.bass_utils import run_bass_kernel_spmd

F32 = mybir.dt.float32
BF16 = mybir.dt.bfloat16
AF = mybir.ActivationFunctionType
ALU = mybir.AluOpType
AX = mybir.AxisListType

P = 128
D = 2048
NCD = D // P
S = 4096
NTC = S // P
TOK = 2048
NQC = TOK // P
NH = 16
DH = 128
IT = 1408
NIB = IT // P
NE = 16
EPS = 1e-5
QSC = 0.08838834764831845
NEG = -1e30
RN = 1.0 / D

DBG = 0
CORES = 8
NH5 = NH
RACE = True
NE7 = NE


def tiles_of(qci):
    W = 256 * (qci + 1)
    out, k = [], 0
    while k < W:
        w = min(512, W - k)
        out.append((k, w))
        k += w
    return out


def build_program():
    nc = bass.Bass("TRN2", target_bir_lowering=False, detect_race_conditions=RACE)

    Xall = nc.dram_tensor("xall", [S, D], F32, kind="ExternalInput")
    Xown = nc.dram_tensor("xown", [TOK, D], F32, kind="ExternalInput")
    Dmk = nc.dram_tensor("dmask", [P, 256], F32, kind="ExternalInput")
    Wqh = nc.dram_tensor("wq", [D, D], F32, kind="ExternalInput")
    Wkh = nc.dram_tensor("wk", [D, D], F32, kind="ExternalInput")
    Wvh = nc.dram_tensor("wv", [D, D], F32, kind="ExternalInput")
    Woh = nc.dram_tensor("wo", [D, D], F32, kind="ExternalInput")
    bqh = nc.dram_tensor("bq", [D], F32, kind="ExternalInput")
    bkh = nc.dram_tensor("bk", [D], F32, kind="ExternalInput")
    bvh = nc.dram_tensor("bv", [D], F32, kind="ExternalInput")
    boh = nc.dram_tensor("bo", [D], F32, kind="ExternalInput")
    gh = nc.dram_tensor("lng", [D], F32, kind="ExternalInput")
    bh = nc.dram_tensor("lnb", [D], F32, kind="ExternalInput")
    gwh = nc.dram_tensor("gwt", [D, 15], F32, kind="ExternalInput")
    w1h = nc.dram_tensor("w1s", [NE, NIB, P, NCD, P], F32, kind="ExternalInput")
    w3h = nc.dram_tensor("w3s", [NE, NIB, P, NCD, P], F32, kind="ExternalInput")
    w2h = nc.dram_tensor("w2s", [NE, NIB, P, D], F32, kind="ExternalInput")
    Y = nc.dram_tensor("y", [TOK, D], F32, kind="ExternalOutput")

    xns = nc.dram_tensor("xns", [NTC, P, D], BF16)
    Ks = nc.dram_tensor("ks", [NH, P, S], BF16)
    Vs = nc.dram_tensor("vs", [NH, P, S], BF16)
    Qs = nc.dram_tensor("qs", [NH, P, TOK], BF16)
    AVs = nc.dram_tensor("avs", [NQC, P, NH, DH], BF16)
    x1sp = nc.dram_tensor("x1sp", [8, P, D], F32)
    h2sp = nc.dram_tensor("h2sp", [P, NCD, 1024], BF16)

    dbg = {}
    if DBG:
        if DBG >= 1:
            dbg["d_xns"] = nc.dram_tensor("d_xns", [NTC, P, D], BF16, kind="ExternalOutput")
        if DBG >= 2:
            dbg["d_ks"] = nc.dram_tensor("d_ks", [NH, P, S], BF16, kind="ExternalOutput")
        if DBG >= 3:
            dbg["d_vs"] = nc.dram_tensor("d_vs", [NH, P, S], BF16, kind="ExternalOutput")
        if DBG >= 4:
            dbg["d_qs"] = nc.dram_tensor("d_qs", [NH, P, TOK], BF16, kind="ExternalOutput")
        if DBG >= 5:
            dbg["d_avs"] = nc.dram_tensor("d_avs", [NQC, P, NH, DH], BF16, kind="ExternalOutput")
        if DBG == 7:
            dbg["d_x1"] = nc.dram_tensor("d_x1", [8, P, D], F32, kind="ExternalOutput")
            dbg["d_h2s"] = nc.dram_tensor("d_h2s", [P, NCD, 1024], BF16, kind="ExternalOutput")

    with ExitStack() as es:
        def sem(name):
            return es.enter_context(nc.semaphore(name))

        s_cst = sem("s_cst"); s_wld = sem("s_wld")
        s_x1 = sem("s_x1"); s_xst = sem("s_xst"); s_xg = sem("s_xg")
        s_ko = sem("s_ko"); s_vo = sem("s_vo"); s_qo = sem("s_qo"); s_x4 = sem("s_x4")
        s_kl = sem("s_kl"); s_vl = sem("s_vl"); s_ql = sem("s_ql")
        s_avo = sem("s_avo"); s_avl = sem("s_avl"); s_x6 = sem("s_x6")
        s_sp = sem("s_sp"); s_spl = sem("s_spl"); s_y = sem("s_y")
        s_w13 = sem("s_w13"); s_w2 = sem("s_w2"); s_gw = sem("s_gw")

        a1 = sem("a1"); v1 = sem("v1"); p1 = sem("p1"); g1 = sem("g1"); a1e = sem("a1e")
        pmm = sem("pmm"); vev = sem("vev")
        a4 = sem("a4"); v4 = sem("v4"); p4 = sem("p4"); g4 = sem("g4"); a4e = sem("a4e")
        p5s = sem("p5s"); p5t = sem("p5t"); p5a = sem("p5a"); p5q = sem("p5q"); v5c = sem("v5c")
        v5e = sem("v5e"); v5m = sem("v5m"); a5x = sem("a5x"); v5r = sem("v5r")
        a5e = sem("a5e"); v5pe = sem("v5pe")
        p6t = sem("p6t"); a6e = sem("a6e"); p6m = sem("p6m"); v6a = sem("v6a")
        g6b = sem("g6b"); a6s = sem("a6s"); v6s = sem("v6s"); a6h = sem("a6h")
        p7g = sem("p7g"); v7g = sem("v7g"); p7a = sem("p7a"); a7s = sem("a7s")
        v7u = sem("v7u"); p7b = sem("p7b"); v7y = sem("v7y")

        Wres = es.enter_context(nc.sbuf_tensor("wres", [P, NCD, D], BF16, side="right"))
        sbt = lambda *a: es.enter_context(nc.sbuf_tensor(*a))
        ident = sbt("ident", [P, P], BF16)
        ones1 = sbt("ones1", [1, P], F32)
        bkc = sbt("bkc", [P, NCD], F32)
        bvc = sbt("bvc", [P, NCD], F32)
        bqc = sbt("bqc", [P, NCD], F32)
        gwt_sb = sbt("gwt_sb", [P, NCD, 15], BF16)
        dmk_sb = sbt("dmk_sb", [P, 256], F32)

        ps = [es.enter_context(nc.psum_tensor(f"ps{i}", [P, 512], F32)) for i in range(6)]
        pst = [es.enter_context(nc.psum_tensor(f"pst{i}", [P, 1024], BF16)) for i in range(2)]

        with nc.Block("p0") as blk:
            @blk.sync
            def _(sy):
                sy.dma_start(out=bkc[:], in_=bkh.rearrange("(c p) -> p c", p=P),
                             allow_slow_non_contiguous=True).then_inc(s_cst, 16)
                sy.dma_start(out=bvc[:], in_=bvh.rearrange("(c p) -> p c", p=P),
                             allow_slow_non_contiguous=True).then_inc(s_cst, 16)
                sy.dma_start(out=bqc[:], in_=bqh.rearrange("(c p) -> p c", p=P),
                             allow_slow_non_contiguous=True).then_inc(s_cst, 16)
                sy.dma_start(out=dmk_sb[:], in_=Dmk[:, :]).then_inc(s_cst, 16)

            @blk.gpsimd
            def _(g):
                g.dma_start(out=gwt_sb[:], in_=gwh.rearrange("(c p) n -> p c n", p=P)
                            ).then_inc(s_gw, 16)
                g.memset(ident[:], 0.0)
                g.affine_select(out=ident[:], in_=ident[:], compare_op=ALU.not_equal,
                                fill=1.0, base=0, pattern=[[-1, P]], channel_multiplier=1)
                g.memset(ones1[:], 1.0).then_inc(g1, 1)

        with ExitStack() as es14:
            sb14 = lambda *a: es14.enter_context(nc.sbuf_tensor(*a))
            g_bc = sb14("g_bc", [P, D], F32)
            b_bc = sb14("b_bc", [P, D], F32)

            rowst = b_bc[0:1, :]
            with nc.Block("p0b") as blk:
                @blk.sync
                def _(sy):
                    sy.dma_start(out=rowst, in_=gh[None, :]).then_inc(s_cst, 16)

                @blk.tensor
                def _(t):
                    t.wait_ge(s_cst, 16 * 5)
                    t.wait_ge(g1, 1)
                    for j in range(4):
                        t.matmul(ps[j][:], ones1[:], rowst[0:1, j * 512:(j + 1) * 512],
                                 start=True, stop=True).then_inc(p1, 1)

                @blk.scalar
                def _(s):
                    for j in range(4):
                        s.wait_ge(p1, j + 1)
                        s.copy(g_bc[:, j * 512:(j + 1) * 512], ps[j][:]).then_inc(v1, 1)

                @blk.sync
                def _(sy):
                    sy.wait_ge(p1, 4)
                    sy.dma_start(out=rowst, in_=bh[None, :]).then_inc(s_cst, 16)

                @blk.tensor
                def _(t):
                    t.wait_ge(s_cst, 16 * 6)
                    t.wait_ge(v1, 4)
                    for j in range(4):
                        t.matmul(ps[j][:], ones1[:], rowst[0:1, j * 512:(j + 1) * 512],
                                 start=True, stop=True).then_inc(p1, 1)

                @blk.scalar
                def _(s):
                    for j in range(4):
                        s.wait_ge(p1, 4 + j + 1)
                        s.copy(b_bc[:, j * 512:(j + 1) * 512], ps[j][:]).then_inc(v1, 1)

            with ExitStack() as e123:
                emit_ln_pass(nc, e123, dict(
                    tag="p1", src=Xall, ntc=NTC, xns=xns, g_bc=g_bc, b_bc=b_bc,
                    ident=ident, pst=pst,
                    s_x=s_x1, s_st=s_xst, va=v1, aa=a1, pa=p1, ga=g1, ae=a1e,
                    v_base=8, a_base=0, p_base=8, g_base=1, e_base=0, st_base=0,
                    prefetch=(Wkh, Wres, s_wld)))

                emit_proj_pass(nc, e123, dict(
                    tag="pkv", ps=ps, Wres=Wres, scale=None,
                    segments=[
                        dict(dst=Ks, bias=bkc, s_out=s_ko, wld_base=0),
                        dict(dst=Vs, bias=bvc, s_out=s_vo, wld_base=4)],
                    xsrc=xns, xst_sem=s_xst,
                    s_xg=s_xg, pmm=pmm, vev=vev, s_wld=s_wld,
                    prefetches={0: (Wvh, Wres, s_wld, pmm, 116),
                                8: (Wqh, Wres, s_wld, pmm, 244)}))

            with ExitStack() as e4:
                emit_p4(nc, e4, dict(
                    ps=ps, Xown=Xown, Qs=Qs, g_bc=g_bc, b_bc=b_bc, ident=ident, pst=pst,
                    bqc=bqc, Wres=Wres, s_wld=s_wld,
                    s_x4=s_x4, s_qo=s_qo, a4=a4, v4=v4, p4=p4, g4=g4, a4e=a4e,
                    pmm=pmm, vev=vev, mm_base=256, ev_base=256,
                    gate_sem=s_vo, gate_val=16 * 8,
                    prefetch=(Woh, Wres, s_wld, pmm, 308)))

        if DBG and DBG <= 4:
            emit_dbg(nc, dbg, [
                ("d_xns", xns, s_xst, 16 * NTC),
                ("d_ks", Ks, s_ko, 16 * 8),
                ("d_vs", Vs, s_vo, 16 * 8),
                ("d_qs", Qs, s_qo, 16 * 4)])
            return nc, dbg

        with ExitStack() as es5:
            emit_p5(nc, es5, dict(
                Ks=Ks, Vs=Vs, Qs=Qs, AVs=AVs, dmk=dmk_sb, ident=ident,
                ps=ps, pst=pst,
                s_ko=s_ko, s_vo=s_vo, s_qo=s_qo, s_kl=s_kl, s_vl=s_vl, s_ql=s_ql,
                s_avo=s_avo, s_cst=s_cst,
                p5s=p5s, p5t=p5t, p5a=p5a, p5q=p5q, v5e=v5e, v5m=v5m,
                a5x=a5x, v5r=v5r, a5e=a5e, v5pe=v5pe, v5c=v5c))

        if DBG == 5:
            emit_dbg(nc, dbg, [("d_avs", AVs, s_avo, 16 * NH5)])
            return nc, dbg

        with ExitStack() as es67:
            emit_p67(nc, es67, dict(
                AVs=AVs, Xown=Xown, Y=Y, boh=boh, ones1=ones1,
                ident=ident, ps=ps, pst=pst, Wres=Wres, gwt_sb=gwt_sb,
                w1h=w1h, w3h=w3h, w2h=w2h, x1sp=x1sp, h2sp=h2sp,
                s_avo=s_avo, s_avl=s_avl, s_x6=s_x6, s_sp=s_sp, s_spl=s_spl,
                s_y=s_y, s_w13=s_w13, s_w2=s_w2, s_wld=s_wld, s_cst=s_cst, s_gw=s_gw,
                p6t=p6t, a6e=a6e, p6m=p6m, v6a=v6a, a6s=a6s, v6s=v6s, v5e=v5e,
                g1=g1, p7g=p7g, v7g=v7g, p7a=p7a, a7s=a7s, v7u=v7u,
                p7b=p7b, v7y=v7y))

        if DBG == 7:
            emit_dbg(nc, dbg, [("d_x1", x1sp, s_sp, 16 * 9),
                               ("d_h2s", h2sp, s_sp, 16 * 9)])

    return nc, dbg


def emit_ln_pass(nc, es, c):
    sb = lambda *a: es.enter_context(nc.sbuf_tensor(*a))
    tg = c["tag"]
    xbuf = [sb(f"{tg}x{i}", [P, D], F32) for i in range(2)]
    hbuf = [sb(f"{tg}h{i}", [P, D], F32) for i in range(2)]
    tbuf = [sb(f"{tg}t", [P, D], F32)] * 2
    xnb = [sb(f"{tg}n{i}", [P, D], BF16) for i in range(2)]
    xtb = [sb(f"{tg}T{i}", [P, D], BF16) for i in range(2)]
    garb = sb(f"{tg}garb", [P, D], BF16)
    cols = [sb(f"{tg}c{i}", [P, 8], F32) for i in range(2)]
    ntc = c["ntc"]
    src, xns = c["src"], c["xns"]
    s_x, s_st = c["s_x"], c["s_st"]
    va, aa, pa, ga, ae = c["va"], c["aa"], c["pa"], c["ga"], c["ae"]
    vb, ab, pb, gb, eb, stb = (c["v_base"], c["a_base"], c["p_base"],
                               c["g_base"], c["e_base"], c["st_base"])
    g_bc, b_bc, ident, pst = c["g_bc"], c["b_bc"], c["ident"], c["pst"]
    srcv = src.rearrange("(tc p) d -> tc p d", p=P)

    with nc.Block(c["tag"]) as blk:
        @blk.gpsimd
        def _(g):
            wh, Wd, s_wld = c["prefetch"]
            for j in range(4):
                g.dma_start(out=Wd[:, :, j * 512:(j + 1) * 512],
                            in_=wh[:, j * 512:(j + 1) * 512]
                            .rearrange("(c p) n -> p c n", p=P)).then_inc(s_wld, 16)
            for tc in range(ntc):
                pbuf = tc % 2
                g.wait_ge(va, vb + 4 * tc + 4)
                g.tensor_tensor(hbuf[pbuf][:], tbuf[pbuf][:], g_bc[:], ALU.mult
                                ).then_inc(ga, 1)
                if tc >= 2:
                    g.wait_ge(pa, pb + 2 * (tc - 2) + 2)
                g.tensor_tensor(xnb[pbuf][:], hbuf[pbuf][:], b_bc[:], ALU.add
                                ).then_inc(ga, 1)

        @blk.sync
        def _(sy):
            for tc in range(ntc):
                pbuf = tc % 2
                if tc >= 2:
                    sy.wait_ge(va, vb + 4 * (tc - 2) + 2)
                sy.dma_start(out=xbuf[pbuf][:], in_=srcv[tc]).then_inc(s_x, 16)
                if tc >= 1:
                    sy.wait_ge(ae, eb + 2 * tc)
                    sy.dma_start(out=xns[tc - 1], in_=xtb[(tc - 1) % 2][:]
                                 ).then_inc(s_st, 16)
            sy.wait_ge(ae, eb + 2 * ntc)
            sy.dma_start(out=xns[ntc - 1], in_=xtb[(ntc - 1) % 2][:]).then_inc(s_st, 16)

        @blk.scalar
        def _(s):
            def evacs(tc):
                pbuf = tc % 2
                s.wait_ge(pa, pb + 2 * tc + 1)
                if tc >= 2:
                    s.wait_ge(s_st, stb + 16 * (tc - 1))
                s.copy(xtb[pbuf][:, 0:1024], pst[0][:]).then_inc(ae, 1)
                s.wait_ge(pa, pb + 2 * tc + 2)
                s.copy(xtb[pbuf][:, 1024:2048], pst[1][:]).then_inc(ae, 1)

            for tc in range(ntc):
                pbuf = tc % 2
                co = cols[pbuf]
                s.wait_ge(s_x, 16 * (tc + 1))
                s.activation(garb[:], xbuf[pbuf][:], AF.Square,
                             accum_out=co[:, 1:2]).then_inc(aa, 1)
                s.wait_ge(va, vb + 4 * tc + 1)
                s.activation(co[:, 5:6], co[:, 3:4], AF.Ln)
                s.activation(co[:, 4:5], co[:, 5:6], AF.Exp, scale=-0.5).then_inc(aa, 1)
                s.wait_ge(va, vb + 4 * tc + 2)
                s.activation(garb[:], hbuf[pbuf][:], AF.Square,
                             accum_out=co[:, 1:2]).then_inc(aa, 1)
                s.wait_ge(va, vb + 4 * tc + 3)
                s.activation(co[:, 5:6], co[:, 3:4], AF.Ln)
                s.activation(co[:, 4:5], co[:, 5:6], AF.Exp, scale=-0.5).then_inc(aa, 1)
                if tc >= 1:
                    evacs(tc - 1)
            evacs(ntc - 1)

        @blk.vector
        def _(v):
            for tc in range(ntc):
                pbuf = tc % 2
                co = cols[pbuf]
                v.wait_ge(s_x, 16 * (tc + 1))
                v.reduce_sum(co[:, 0:1], xbuf[pbuf][:], AX.X)
                v.wait_ge(aa, ab + 4 * tc + 1)
                v.tensor_scalar_mul(co[:, 2:3], co[:, 0:1], RN)
                v.tensor_scalar(co[:, 3:4], co[:, 1:2], RN, EPS, ALU.mult, ALU.add)
                v.tensor_scalar(co[:, 5:6], co[:, 2:3], co[:, 2:3], None, ALU.mult)
                v.tensor_scalar(co[:, 3:4], co[:, 5:6], co[:, 3:4], -1.0,
                                ALU.subtract, ALU.mult).then_inc(va, 1)
                v.wait_ge(aa, ab + 4 * tc + 2)
                if tc >= 2:
                    v.wait_ge(ga, gb + 2 * (tc - 2) + 2)
                v.tensor_scalar(hbuf[pbuf][:], xbuf[pbuf][:], co[:, 2:3], co[:, 4:5],
                                ALU.subtract, ALU.mult).then_inc(va, 1)
                v.reduce_sum(co[:, 0:1], hbuf[pbuf][:], AX.X)
                v.wait_ge(aa, ab + 4 * tc + 3)
                v.tensor_scalar_mul(co[:, 2:3], co[:, 0:1], RN)
                v.tensor_scalar(co[:, 3:4], co[:, 1:2], RN, EPS, ALU.mult, ALU.add)
                v.tensor_scalar(co[:, 5:6], co[:, 2:3], co[:, 2:3], None, ALU.mult)
                v.tensor_scalar(co[:, 3:4], co[:, 5:6], co[:, 3:4], -1.0,
                                ALU.subtract, ALU.mult).then_inc(va, 1)
                v.wait_ge(aa, ab + 4 * tc + 4)
                if tc >= 1:
                    v.wait_ge(ga, gb + 2 * (tc - 1) + 1)
                v.tensor_scalar(tbuf[pbuf][:], hbuf[pbuf][:], co[:, 2:3], co[:, 4:5],
                                ALU.subtract, ALU.mult).then_inc(va, 1)

        @blk.tensor
        def _(t):
            for tc in range(ntc):
                pbuf = tc % 2
                t.wait_ge(ga, gb + 2 * tc + 2)
                for half in range(2):
                    bidx = 2 * tc + half
                    if bidx >= 2:
                        t.wait_ge(ae, eb + bidx - 1)
                    for cc in range(8):
                        ccc = half * 8 + cc
                        last = t.transpose(pst[bidx % 2][:, cc * P:(cc + 1) * P],
                                           xnb[pbuf][:, ccc * P:(ccc + 1) * P], ident[:])
                    last.then_inc(pa, 1)


def emit_proj_pass(nc, es, c):
    sb = lambda *a: es.enter_context(nc.sbuf_tensor(*a))
    tg = c["tag"]
    xg = [sb(f"{tg}xg{i}", [P, NCD, 512], BF16) for i in range(2)]
    ob = [sb(f"{tg}ob{i}", [P, NCD, 512], BF16) for i in range(2)]
    ps = c["ps"]
    Wres = c["Wres"]
    pmm, vev = c["pmm"], c["vev"]
    s_xg, s_wld = c["s_xg"], c["s_wld"]
    xsv = c["xsrc"].rearrange("tc p (c j) -> tc p c j", c=NCD)
    segs = c["segments"]
    ngr = 8 * len(segs)

    def seg(gi):
        return segs[gi // 8], gi % 8

    with nc.Block(tg) as blk:
        @blk.gpsimd
        def _(g):
            for at in sorted(c["prefetches"]):
                wh, Wd, s_wldN, gate_sem, gate_val = c["prefetches"][at]
                for j in range(4):
                    g.wait_ge(gate_sem, gate_val + 4 * j)
                    g.dma_start(out=Wd[:, :, j * 512:(j + 1) * 512],
                                in_=wh[:, j * 512:(j + 1) * 512]
                                .rearrange("(c p) n -> p c n", p=P)).then_inc(s_wldN, 16)

        @blk.sync
        def _(sy):
            for gi in range(ngr):
                sg, gl = seg(gi)
                pbuf = gi % 2
                sy.wait_ge(c["xst_sem"], 16 * (4 * gl + 4))
                if gi >= 2:
                    sy.wait_ge(pmm, 16 * (gi - 1))
                for u in range(4):
                    sy.dma_start(out=xg[pbuf][:, :, u * P:(u + 1) * P],
                                 in_=xsv[4 * gl + u]).then_inc(s_xg, 16)
                if gi >= 1:
                    psg, pgl = seg(gi - 1)
                    sy.wait_ge(vev, 16 * gi)
                    sy.dma_start(out=psg["dst"].rearrange("h p (g j) -> g p h j", j=512)[pgl],
                                 in_=ob[(gi - 1) % 2][:]).then_inc(psg["s_out"], 16)
            psg, pgl = seg(ngr - 1)
            sy.wait_ge(vev, 16 * ngr)
            sy.dma_start(out=psg["dst"].rearrange("h p (g j) -> g p h j", j=512)[pgl],
                         in_=ob[(ngr - 1) % 2][:]).then_inc(psg["s_out"], 16)

        @blk.tensor
        def _(t):
            for gi in range(ngr):
                sg, gl = seg(gi)
                pbuf = gi % 2
                t.wait_ge(s_xg, 16 * (4 * gi + 4))
                for do in range(NCD):
                    gd = gi * 16 + do
                    bank = ps[do % 4]
                    t.wait_ge(s_wld, 16 * (sg["wld_base"] + do // 4 + 1))
                    if gd >= 4:
                        t.wait_ge(vev, gd - 3)
                    for cc in range(NCD):
                        last = t.matmul(bank[:], Wres[:, cc, do * P:(do + 1) * P],
                                        xg[pbuf][:, cc, :],
                                        start=(cc == 0), stop=(cc == NCD - 1))
                    last.then_inc(pmm, 1)

        @blk.vector
        def _(v):
            nst = {id(sg): 0 for sg in segs}
            for gi in range(ngr):
                sg, gl = seg(gi)
                pbuf = gi % 2
                for do in range(NCD):
                    gd = gi * 16 + do
                    v.wait_ge(pmm, gd + 1)
                    if gi >= 2 and do == 0:
                        psg, pgl = seg(gi - 2)
                        v.wait_ge(psg["s_out"], 16 * (pgl + 1))
                    v.tensor_scalar(ob[pbuf][:, do, :], ps[do % 4][:],
                                    sg["bias"][:, do:do + 1], None, ALU.add
                                    ).then_inc(vev, 1)


def emit_p4(nc, es, c):
    sb = lambda *a: es.enter_context(nc.sbuf_tensor(*a))
    xbuf = [sb(f"p4x{i}", [P, D], F32) for i in range(2)]
    hbuf = [sb(f"p4h{i}", [P, D], F32) for i in range(2)]
    tbuf = [sb(f"p4t{i}", [P, D], F32) for i in range(2)]
    xnb = [sb(f"p4n{i}", [P, D], BF16) for i in range(2)]
    xg = [sb("p4g0", [P, NCD, 512], BF16)]
    ob = [sb("p4o0", [P, NCD, 512], BF16)]
    garb = sb("p4garb", [P, D], F32)
    cols = [sb(f"p4c{i}", [P, 8], F32) for i in range(2)]
    ps = c["ps"]
    g_bc, b_bc, ident, pst = c["g_bc"], c["b_bc"], c["ident"], c["pst"]
    Xov = c["Xown"].rearrange("(tc p) d -> tc p d", p=P)
    Qs, bqc, Wres = c["Qs"], c["bqc"], c["Wres"]
    a4, v4, p4, g4, a4e = c["a4"], c["v4"], c["p4"], c["g4"], c["a4e"]
    pmm, vev = c["pmm"], c["vev"]
    mm_b, ev_b = c["mm_base"], c["ev_base"]
    s_x4, s_qo, s_wld = c["s_x4"], c["s_qo"], c["s_wld"]
    qsv = Qs.rearrange("h p (g j) -> g p h j", j=512)

    with nc.Block("p4") as blk:
        @blk.gpsimd
        def _(g):
            for tc in range(NQC):
                pbuf = tc % 2
                g.wait_ge(v4, 4 * tc + 4)
                g.tensor_tensor(hbuf[pbuf][:], tbuf[pbuf][:], g_bc[:], ALU.mult
                                ).then_inc(g4, 1)
                if tc >= 2:
                    g.wait_ge(p4, 2 * (tc - 2) + 2)
                g.tensor_tensor(xnb[pbuf][:], hbuf[pbuf][:], b_bc[:], ALU.add
                                ).then_inc(g4, 1)
            wh, Wd, s_wldN, gate_sem, gate_val = c["prefetch"]
            for j in range(4):
                g.wait_ge(gate_sem, gate_val + 4 * j)
                g.dma_start(out=Wd[:, :, j * 512:(j + 1) * 512],
                            in_=wh[:, j * 512:(j + 1) * 512]
                            .rearrange("(c p) n -> p c n", p=P)).then_inc(s_wldN, 16)

        @blk.sync
        def _(sy):
            for tc in range(NQC):
                pbuf = tc % 2
                if tc == 0:
                    sy.wait_ge(c["gate_sem"], c["gate_val"])
                if tc >= 2:
                    sy.wait_ge(v4, 4 * (tc - 2) + 2)
                sy.dma_start(out=xbuf[pbuf][:], in_=Xov[tc]).then_inc(s_x4, 16)
                if tc % 4 == 3 and tc >= 7:
                    gi = tc // 4 - 1
                    sy.wait_ge(vev, ev_b + 16 * (gi + 1))
                    sy.dma_start(out=qsv[gi], in_=ob[0][:]).then_inc(s_qo, 16)
            sy.wait_ge(vev, ev_b + 64)
            sy.dma_start(out=qsv[3], in_=ob[0][:]).then_inc(s_qo, 16)

        @blk.scalar
        def _(s):
            for tc in range(NQC):
                pbuf = tc % 2
                co = cols[pbuf]
                s.wait_ge(s_x4, 16 * (tc + 1))
                s.activation(garb[:], xbuf[pbuf][:], AF.Square, accum_out=co[:, 1:2]
                             ).then_inc(a4, 1)
                s.wait_ge(v4, 4 * tc + 1)
                s.activation(co[:, 5:6], co[:, 3:4], AF.Ln)
                s.activation(co[:, 4:5], co[:, 5:6], AF.Exp, scale=-0.5).then_inc(a4, 1)
                s.wait_ge(v4, 4 * tc + 2)
                s.activation(garb[:], hbuf[pbuf][:], AF.Square, accum_out=co[:, 1:2]
                             ).then_inc(a4, 1)
                s.wait_ge(v4, 4 * tc + 3)
                s.activation(co[:, 5:6], co[:, 3:4], AF.Ln)
                s.activation(co[:, 4:5], co[:, 5:6], AF.Exp, scale=-0.5).then_inc(a4, 1)
                gi, u = tc // 4, tc % 4
                dxg = xg[0]
                s.wait_ge(p4, 2 * tc + 1)
                if gi >= 1 and u == 0:
                    s.wait_ge(pmm, mm_b + 16 * gi)
                s.copy(dxg[:, 0:8, u * P:(u + 1) * P],
                              pst[0][:].rearrange("p (c j) -> p c j", c=8)).then_inc(a4e, 1)
                s.wait_ge(p4, 2 * tc + 2)
                s.copy(dxg[:, 8:16, u * P:(u + 1) * P],
                              pst[1][:].rearrange("p (c j) -> p c j", c=8)).then_inc(a4e, 1)

        @blk.vector
        def _(v):
            for tc in range(NQC):
                pbuf = tc % 2
                co = cols[pbuf]
                v.wait_ge(s_x4, 16 * (tc + 1))
                v.reduce_sum(co[:, 0:1], xbuf[pbuf][:], AX.X)
                v.wait_ge(a4, 4 * tc + 1)
                v.tensor_scalar_mul(co[:, 2:3], co[:, 0:1], RN)
                v.tensor_scalar(co[:, 3:4], co[:, 1:2], RN, EPS, ALU.mult, ALU.add)
                v.tensor_scalar(co[:, 5:6], co[:, 2:3], co[:, 2:3], None, ALU.mult)
                v.tensor_scalar(co[:, 3:4], co[:, 5:6], co[:, 3:4], -1.0,
                                ALU.subtract, ALU.mult).then_inc(v4, 1)
                v.wait_ge(a4, 4 * tc + 2)
                if tc >= 2:
                    v.wait_ge(g4, 2 * (tc - 2) + 2)
                v.tensor_scalar(hbuf[pbuf][:], xbuf[pbuf][:], co[:, 2:3], co[:, 4:5],
                                ALU.subtract, ALU.mult).then_inc(v4, 1)
                v.reduce_sum(co[:, 0:1], hbuf[pbuf][:], AX.X)
                v.wait_ge(a4, 4 * tc + 3)
                v.tensor_scalar_mul(co[:, 2:3], co[:, 0:1], RN)
                v.tensor_scalar(co[:, 3:4], co[:, 1:2], RN, EPS, ALU.mult, ALU.add)
                v.tensor_scalar(co[:, 5:6], co[:, 2:3], co[:, 2:3], None, ALU.mult)
                v.tensor_scalar(co[:, 3:4], co[:, 5:6], co[:, 3:4], -1.0,
                                ALU.subtract, ALU.mult).then_inc(v4, 1)
                v.wait_ge(a4, 4 * tc + 4)
                if tc >= 1:
                    v.wait_ge(g4, 2 * (tc - 1) + 1)
                v.tensor_scalar(tbuf[pbuf][:], hbuf[pbuf][:], co[:, 2:3], co[:, 4:5],
                                ALU.subtract, ALU.mult).then_inc(v4, 1)
                if tc % 4 == 3:
                    gi = tc // 4
                    for do in range(NCD):
                        gd = gi * 16 + do
                        v.wait_ge(pmm, mm_b + gd + 1)
                        if gi >= 1 and do == 0:
                            v.wait_ge(s_qo, 16 * gi)
                        v.tensor_scalar(ob[0][:, do, :], ps[do % 4][:],
                                        bqc[:, do:do + 1], QSC, ALU.add, ALU.mult
                                        ).then_inc(vev, 1)

        @blk.tensor
        def _(t):
            for tc in range(NQC):
                pbuf = tc % 2
                t.wait_ge(g4, 2 * tc + 2)
                for half in range(2):
                    bidx = 2 * tc + half
                    if bidx >= 2:
                        t.wait_ge(a4e, bidx - 1)
                    for cc in range(8):
                        ccc = half * 8 + cc
                        last = t.transpose(pst[bidx % 2][:, cc * P:(cc + 1) * P],
                                           xnb[pbuf][:, ccc * P:(ccc + 1) * P], ident[:])
                    last.then_inc(p4, 1)
                if tc % 4 == 3:
                    gi = tc // 4
                    t.wait_ge(a4e, 2 * (tc + 1))
                    for do in range(NCD):
                        gd = gi * 16 + do
                        bank = ps[do % 4]
                        t.wait_ge(s_wld, 16 * (8 + do // 4 + 1))
                        if gd >= 4:
                            t.wait_ge(vev, ev_b + gd - 3)
                        for cc in range(NCD):
                            last = t.matmul(bank[:], Wres[:, cc, do * P:(do + 1) * P],
                                            xg[0][:, cc, :],
                                            start=(cc == 0), stop=(cc == NCD - 1))
                        last.then_inc(pmm, 1)


def emit_p5(nc, es, c):
    sb = lambda *a: es.enter_context(nc.sbuf_tensor(*a))
    ktb = [sb(f"p5k{i}", [P, S], BF16) for i in range(2)]
    vtb = [sb(f"p5v{i}", [P, S], BF16) for i in range(2)]
    qb = [sb(f"p5q{i}", [P, TOK], BF16) for i in range(2)]
    vtok = sb("p5vt", [P, NTC, P], BF16)
    Ssb = [sb(f"p5s{i}", [P, S], F32) for i in range(2)]
    Psb = [sb(f"p5p{i}", [P, S], BF16) for i in range(2)]
    ptb_sb = sb("p5pt", [P, 8, P], BF16)
    avh = [sb(f"p5a{i}", [P, NQC, DH], BF16) for i in range(2)]
    mx = [sb(f"p5m{i}", [P, 8], F32) for i in range(2)]
    col = [sb(f"p5c{i}", [P, 4], F32) for i in range(2)]
    ps, pst = c["ps"], c["pst"]
    Ks, Vs, Qs, AVs = c["Ks"], c["Vs"], c["Qs"], c["AVs"]
    dmk, ident = c["dmk"], c["ident"]
    s_kl, s_vl, s_ql, s_avo = c["s_kl"], c["s_vl"], c["s_ql"], c["s_avo"]
    s_cst = c["s_cst"]
    p5s, p5t, p5a = c["p5s"], c["p5t"], c["p5a"]
    v5e, v5m, a5x, v5r, a5e = c["v5e"], c["v5m"], c["a5x"], c["v5r"], c["a5e"]
    v5c = c["v5c"]
    avv = AVs.rearrange("q p h d -> p q h d")

    tile_of_E = {}
    gt = 0
    for h in range(NH5):
        for qci in range(NQC):
            E = h * NQC + qci
            tl = []
            for (k0, w) in tiles_of(qci):
                tl.append((gt, k0, w))
                gt += 1
            tile_of_E[E] = tl

    batches = []
    bodyb_of_E = {}
    av_cnt = 0
    ptb_cnt = 0
    av_through_E = {}
    bt_through_E = {}

    def emit_body_plan(E):
        nonlocal av_cnt, ptb_cnt
        qci = E % NQC
        nkc = 2 * (qci + 1)
        bl = []
        k = 0
        while k < nkc:
            kcs = list(range(k, min(k + 4, nkc)))
            b = dict(kind="pt", E=E, h=E // NQC, gb=len(batches), ptb=ptb_cnt,
                     kcs=kcs, av_idx=av_cnt)
            batches.append(b)
            bl.append(b)
            ptb_cnt += 1
            av_cnt += 1
            k += 4
        bodyb_of_E[E] = bl
        av_through_E[E] = av_cnt
        bt_through_E[E] = len(batches)

    av_through_head = {-1: 0}
    for h in range(NH5):
        if h > 0:
            emit_body_plan(h * NQC - 1)
            av_through_head[h - 1] = av_cnt
        for b in range(8):
            batches.append(dict(kind="vt", h=h, b=b, gb=len(batches),
                                kcs=list(range(4 * b, 4 * b + 4))))
        for qci in range(1, NQC):
            emit_body_plan(h * NQC + qci - 1)
    emit_body_plan(NH5 * NQC - 1)
    av_through_head[NH5 - 1] = av_cnt

    def w(eng, semh, v):
        if v > 0:
            eng.wait_ge(semh, v)

    with nc.Block("p5") as blk:
        @blk.sync
        def _(sy):
            def loads(h):
                sy.dma_start(out=ktb[h % 2][:], in_=Ks[h]).then_inc(s_kl, 16)
                sy.dma_start(out=vtb[h % 2][:], in_=Vs[h]).then_inc(s_vl, 16)
                sy.dma_start(out=qb[h % 2][:], in_=Qs[h]).then_inc(s_ql, 16)
            sy.wait_ge(c["s_ko"], 16 * 8)
            sy.wait_ge(c["s_vo"], 16 * 8)
            sy.wait_ge(c["s_qo"], 16 * 4)
            loads(0)
            loads(1)
            for h in range(NH5):
                if h + 2 < NH5:
                    w(sy, p5a, av_through_head[h])
                    loads(h + 2)
                sy.wait_ge(v5r, NQC * (h + 1))
                sy.dma_start(out=avv[:, :, h, :], in_=avh[h % 2][:]).then_inc(s_avo, 16)

        @blk.tensor
        def _(t):
            def body(E):
                h, qci = E // NQC, E % NQC
                t.wait_ge(a5x, E + 1)
                w(t, v5r, E - 1)
                avb = ps[2 + E % 2]
                nkc = 2 * (qci + 1)
                def avmm(b):
                    t.wait_ge(a5e, b["gb"] + 1)
                    for i, kc in enumerate(b["kcs"]):
                        last = t.matmul(avb[:, 0:DH],
                                        ptb_sb[:, (b["ptb"] % 2) * 4 + i, :],
                                        vtok[:, kc, :],
                                        start=(kc == 0), stop=(kc == nkc - 1))
                    last.then_inc(p5a, 1)

                bl = bodyb_of_E[E]
                for bi, b in enumerate(bl):
                    gb = b["gb"]
                    if gb >= 2:
                        t.wait_ge(a5e, gb - 1)
                    for i, kc in enumerate(b["kcs"]):
                        last = t.transpose(
                            pst[gb % 2][:, i * P:(i + 1) * P],
                            Psb[E % 2][:, kc * P:(kc + 1) * P], ident[:])
                    last.then_inc(p5t, 1)
                    if bi >= 1:
                        avmm(bl[bi - 1])
                avmm(bl[-1])

            for h in range(NH5):
                if h > 0:
                    body(h * NQC - 1)
                t.wait_ge(s_vl, 16 * (h + 1))
                vts = [bt for bt in batches if bt["kind"] == "vt" and bt["h"] == h]
                for bt in vts:
                    gb = bt["gb"]
                    if gb >= 2:
                        t.wait_ge(a5e, gb - 1)
                    for i, kc in enumerate(bt["kcs"]):
                        last = t.transpose(
                            pst[gb % 2][:, i * P:(i + 1) * P],
                            vtb[h % 2][:, kc * P:(kc + 1) * P], ident[:])
                    last.then_inc(p5t, 1)
                t.wait_ge(s_kl, 16 * (h + 1))
                t.wait_ge(s_ql, 16 * (h + 1))

                sbanks = (0, 1, 4, 5)

                def scores(qci):
                    E = h * NQC + qci
                    for (gtx, k0, wd) in tile_of_E[E]:
                        w(t, v5e, gtx - 3)
                        t.matmul(ps[sbanks[gtx % 4]][:, 0:wd],
                                 qb[h % 2][:, qci * P:(qci + 1) * P],
                                 ktb[h % 2][:, k0:k0 + wd],
                                 start=True, stop=True).then_inc(p5s, 1)

                scores(0)
                scores(1)
                for qci in range(NQC):
                    if qci + 2 < NQC:
                        scores(qci + 2)
                    if qci < NQC - 1:
                        body(h * NQC + qci)
            body(NH5 * NQC - 1)

        @blk.scalar
        def _(s):
            def evac(bt):
                gb = bt["gb"]
                s.wait_ge(p5t, gb + 1)
                n = len(bt["kcs"])
                if bt["kind"] == "vt":
                    w(s, p5a, av_through_head[bt["h"] - 1])
                    dst = vtok[:, bt["b"] * 4:bt["b"] * 4 + n, :]
                else:
                    w(s, p5a, bt["ptb"] - 1)
                    dst = ptb_sb[:, (bt["ptb"] % 2) * 4:(bt["ptb"] % 2) * 4 + n, :]
                s.copy(dst, pst[gb % 2][:, 0:n * P]
                       .rearrange("p (k d) -> p k d", d=P)).then_inc(a5e, 1)

            def avevac(E):
                h2, qci2 = E // NQC, E % NQC
                s.wait_ge(v5c, E + 1)
                s.wait_ge(p5a, av_through_E[E])
                if qci2 == 0:
                    w(s, s_avo, 16 * (h2 - 1))
                s.activation(avh[h2 % 2][:, qci2, :], ps[2 + E % 2][:, 0:DH],
                             AF.Copy, scale=col[E % 2][:, 2:3]).then_inc(v5r, 1)

            for h in range(NH5):
                if h > 0:
                    for bt in bodyb_of_E[h * NQC - 1]:
                        evac(bt)
                for bt in [bt for bt in batches if bt["kind"] == "vt" and bt["h"] == h]:
                    evac(bt)
                if h > 0:
                    avevac(h * NQC - 1)
                for qci in range(NQC):
                    E = h * NQC + qci
                    s.wait_ge(v5m, E + 1)
                    if E >= 2:
                        s.wait_ge(p5t, bt_through_E[E - 2])
                        s.wait_ge(v5r, E - 1)
                    W = 256 * (qci + 1)
                    s.activation(Psb[E % 2][:, 0:W], Ssb[E % 2][:, 0:W], AF.Exp,
                                 bias=col[E % 2][:, 0:1],
                                 accum_out=col[E % 2][:, 1:2]).then_inc(a5x, 1)
                    if qci > 0:
                        for bt in bodyb_of_E[E - 1]:
                            evac(bt)
                        avevac(E - 1)
            for bt in bodyb_of_E[NH5 * NQC - 1]:
                evac(bt)
            avevac(NH5 * NQC - 1)

        @blk.vector
        def _(v):
            sbanks = (0, 1, 4, 5)
            for E in range(NH5 * NQC):
                h, qci = E // NQC, E % NQC
                if E == 0:
                    v.wait_ge(s_cst, 16 * 6)
                nt = len(tile_of_E[E])
                for t_i, (gtx, k0, wd) in enumerate(tile_of_E[E]):
                    v.wait_ge(p5s, gtx + 1)
                    if t_i == 0:
                        w(v, a5x, E - 1)
                    v.tensor_scalar(Ssb[E % 2][:, k0:k0 + wd],
                                    ps[sbanks[gtx % 4]][:, 0:wd],
                                    1.0, None, ALU.mult, ALU.max,
                                    accum_out=mx[E % 2][:, t_i:t_i + 1]).then_inc(v5e, 1)
                W = 256 * (qci + 1)
                v.tensor_tensor(Ssb[E % 2][:, W - 256:W], Ssb[E % 2][:, W - 256:W],
                                dmk[:], ALU.add)
                v.reduce_max(col[E % 2][:, 0:1], mx[E % 2][:, 0:nt], AX.X,
                             negate=True).then_inc(v5m, 1)
                if E >= 1:
                    v.wait_ge(a5x, E)
                    v.reciprocal(col[(E - 1) % 2][:, 2:3],
                                 col[(E - 1) % 2][:, 1:2]).then_inc(v5c, 1)
            v.wait_ge(a5x, NH5 * NQC)
            v.reciprocal(col[(NH5 * NQC - 1) % 2][:, 2:3],
                         col[(NH5 * NQC - 1) % 2][:, 1:2]).then_inc(v5c, 1)


def emit_p67(nc, es, c):
    sb = lambda *a: es.enter_context(nc.sbuf_tensor(*a))
    oa = sb("oa", [P, 8, D], F32)
    h2T = sb("h2T", [P, NCD, 1024], BF16)
    ab = sb("p67ab", [P, 7680], BF16)
    af = sb("p67af", [P, 2048], F32)
    avb = [ab[:, 0:2048].rearrange("p (h d) -> p h d", h=NH),
           ab[:, 2048:4096].rearrange("p (h d) -> p h d", h=NH)]
    avT = ab[:, 4096:6144].rearrange("p (c d) -> p c d", c=NCD)
    x6b = af
    sa = [af[:, 0:512], af[:, 512:1024]]
    w2x = ab[:, 0:5632].rearrange("p (i j) -> p i j", i=NIB)
    h2bf = sb("p6h2", [P, D], BF16)
    bo_bc = sb("p6bo", [P, D], F32)
    co6 = sb("p6c", [P, 8], F32)
    comb = sb("p7comb", [P, 8, 16], F32)
    gl = sb("p7gl", [P, 16], F32)
    gp = sb("p7gp", [P, 16], F32)
    t2b = sb("p7t2", [P, 16], F32)
    gc = sb("p7gc", [P, 8], F32)

    ps, pst = c["ps"], c["pst"]
    ident, ones1 = c["ident"], c["ones1"]
    rowst = af[0:1, :]
    Wres, gwt_sb = c["Wres"], c["gwt_sb"]
    AVs, Xown, Y, boh = c["AVs"], c["Xown"], c["Y"], c["boh"]
    w1h, w3h, w2h = c["w1h"], c["w3h"], c["w2h"]
    x1sp, h2sp = c["x1sp"], c["h2sp"]
    s_avl, s_x6 = c["s_avl"], c["s_x6"]
    s_sp, s_spl, s_y = c["s_sp"], c["s_spl"], c["s_y"]
    s_w13, s_w2, s_wld, s_cst = c["s_w13"], c["s_w2"], c["s_wld"], c["s_cst"]
    s_gw = c["s_gw"]
    p6t, a6e, p6m, v6a = c["p6t"], c["a6e"], c["p6m"], c["v6a"]
    v5e = c["v5e"]
    a6s, v6s = c["a6s"], c["v6s"]
    p7g, v7g, p7a, a7s = c["p7g"], c["v7g"], c["p7a"], c["a7s"]
    v7u, p7b, v7y = c["v7u"], c["p7b"], c["v7y"]
    Xov = Xown.rearrange("(tc p) d -> tc p d", p=P)
    Yv = Y.rearrange("(tc p) d -> tc p d", p=P)
    halves = [1, 0]

    p6t_avT, p6t_h2T = {}, {}
    cnt = 0
    for gq in range(16):
        cnt += 2
        p6t_avT[gq] = cnt
        if gq >= 1:
            cnt += 2
            p6t_h2T[gq - 1] = cnt
    cnt += 2
    p6t_h2T[15] = cnt

    with nc.Block("p6") as blk:
        @blk.sync
        def _(sy):
            sy.dma_start(out=rowst, in_=boh[None, :]).then_inc(s_cst, 16)
            for gq in range(16):
                hf, qcl = halves[gq // 8], gq % 8
                qc = hf * 8 + qcl
                if gq == 0:
                    sy.wait_ge(c["s_avo"], 16 * NH)
                if gq >= 2:
                    sy.wait_ge(p6t, p6t_avT[gq - 2])
                sy.dma_start(out=avb[gq % 2][:], in_=AVs[qc]).then_inc(s_avl, 16)
                if gq == 8:
                    for tc in range(8):
                        sy.wait_ge(v6a, 8 * tc + 8)
                        sy.dma_start(out=x1sp[tc], in_=oa[:, tc, :]).then_inc(s_sp, 16)
                    sy.wait_ge(a6e, 4 + p6t_h2T[7])
                    sy.dma_start(out=h2sp[:, :, :], in_=h2T[:]).then_inc(s_sp, 16)
                if gq >= 1:
                    sy.wait_ge(v6a, 8 * gq)
                else:
                    sy.wait_ge(p6m, 4)
                sy.dma_start(out=x6b[:], in_=Xov[qc]).then_inc(s_x6, 16)

        @blk.tensor
        def _(t):
            t.wait_ge(s_cst, 16 * 7)
            t.wait_ge(c["g1"], 1)
            t.wait_ge(v5e, 72 * NH5)
            for j in range(4):
                if j > 0:
                    t.wait_ge(a6e, j)
                t.matmul(ps[4][:], ones1[:], rowst[0:1, j * 512:(j + 1) * 512],
                         start=True, stop=True).then_inc(p6m, 1)

            def h2t(g2, bat0):
                for half_b in range(2):
                    bat = bat0 + half_b
                    t.wait_ge(v6s, 2 * g2 + 2)
                    if bat >= 2:
                        t.wait_ge(a6e, 4 + bat - 1)
                    for i in range(8):
                        cc = half_b * 8 + i
                        last = t.transpose(pst[bat % 2][:, i * P:(i + 1) * P],
                                           h2bf[:, cc * P:(cc + 1) * P], ident[:])
                    last.then_inc(p6t, 1)

            for gq in range(16):
                t.wait_ge(s_avl, 16 * (gq + 1))
                for half_b in range(2):
                    bat = p6t_avT[gq] - 2 + half_b
                    if bat >= 2:
                        t.wait_ge(a6e, 4 + bat - 1)
                    for i in range(8):
                        cc = half_b * 8 + i
                        last = t.transpose(pst[bat % 2][:, i * P:(i + 1) * P],
                                           avb[gq % 2][:, cc, :], ident[:])
                    last.then_inc(p6t, 1)
                if gq >= 1:
                    h2t(gq - 1, p6t_h2T[gq - 1] - 2)
                t.wait_ge(a6e, 4 + p6t_avT[gq])
                for j in range(4):
                    n = gq * 4 + j
                    t.wait_ge(s_wld, 16 * (12 + j + 1))
                    if n >= 2:
                        t.wait_ge(v6a, 2 * (n - 2) + 2)
                    for cc in range(NCD):
                        last = t.matmul(ps[4 + n % 2][:], avT[:, cc, :],
                                        Wres[:, cc, j * 512:(j + 1) * 512],
                                        start=(cc == 0), stop=(cc == NCD - 1))
                    last.then_inc(p6m, 1)
            h2t(15, p6t_h2T[15] - 2)

        @blk.scalar
        def _(s):
            for j in range(4):
                s.wait_ge(p6m, j + 1)
                s.copy(bo_bc[:, j * 512:(j + 1) * 512], ps[4][:]).then_inc(a6e, 1)

            def h2ev(g2):
                hf2, qcl2 = halves[g2 // 8], g2 % 8
                if g2 == 8:
                    s.wait_ge(s_sp, 16 * 9)
                for half_b in range(2):
                    bat = p6t_h2T[g2] - 2 + half_b
                    s.wait_ge(p6t, bat + 1)
                    s.copy(h2T[:, half_b * 8:half_b * 8 + 8, qcl2 * P:(qcl2 + 1) * P],
                           pst[bat % 2][:].rearrange("p (k d) -> p k d", d=P)).then_inc(a6e, 1)

            for gq in range(16):
                for half_b in range(2):
                    bat = p6t_avT[gq] - 2 + half_b
                    s.wait_ge(p6t, bat + 1)
                    if half_b == 0:
                        s.wait_ge(p6m, 4 + 4 * gq)
                    s.copy(avT[:, half_b * 8:half_b * 8 + 8, :],
                           pst[bat % 2][:].rearrange("p (k d) -> p k d", d=P)).then_inc(a6e, 1)
                if gq >= 1:
                    h2ev(gq - 1)
                s.wait_ge(v6a, 8 * gq + 8)
                if gq >= 1:
                    s.wait_ge(p6t, p6t_h2T[gq - 1])
                s.activation(h2bf[:], oa[:, gq % 8, :], AF.Square,
                             accum_out=co6[:, 1:2]).then_inc(a6s, 1)
                s.wait_ge(v6s, 2 * gq + 1)
                s.activation(co6[:, 5:6], co6[:, 3:4], AF.Ln)
                s.activation(co6[:, 4:5], co6[:, 5:6], AF.Exp, scale=-0.5).then_inc(a6s, 1)
            h2ev(15)

        @blk.vector
        def _(v):
            for gq in range(16):
                qcl = gq % 8
                for j in range(4):
                    n = gq * 4 + j
                    v.wait_ge(p6m, 4 + n + 1)
                    if gq == 8 and j == 0:
                        v.wait_ge(s_sp, 16 * 8)
                    v.scalar_tensor_tensor(oa[:, qcl, j * 512:(j + 1) * 512],
                                           ps[4 + n % 2][:], 1.0,
                                           bo_bc[:, j * 512:(j + 1) * 512],
                                           ALU.mult, ALU.add).then_inc(v6a, 1)
                    v.wait_ge(s_x6, 16 * (gq + 1))
                    v.scalar_tensor_tensor(oa[:, qcl, j * 512:(j + 1) * 512],
                                           oa[:, qcl, j * 512:(j + 1) * 512], 1.0,
                                           x6b[:, j * 512:(j + 1) * 512],
                                           ALU.mult, ALU.add).then_inc(v6a, 1)
                v.reduce_sum(co6[:, 0:1], oa[:, qcl, :], AX.X)
                v.wait_ge(a6s, 2 * gq + 1)
                v.tensor_scalar_mul(co6[:, 2:3], co6[:, 0:1], RN)
                v.tensor_scalar(co6[:, 3:4], co6[:, 1:2], RN, EPS, ALU.mult, ALU.add)
                v.tensor_scalar(co6[:, 5:6], co6[:, 2:3], co6[:, 2:3], None, ALU.mult)
                v.tensor_scalar(co6[:, 3:4], co6[:, 5:6], co6[:, 3:4], -1.0,
                                ALU.subtract, ALU.mult).then_inc(v6s, 1)
                v.wait_ge(a6s, 2 * gq + 2)
                v.tensor_scalar(h2bf[:], oa[:, qcl, :], co6[:, 2:3], co6[:, 4:5],
                                ALU.subtract, ALU.mult).then_inc(v6s, 1)

    flat = Wres[:].rearrange("p c d -> p (c d)")
    u_v = flat[:, 0:11264].rearrange("p (i t) -> p i t", i=NIB)
    w1b = [flat[:, 11264 + k * 2048:11264 + (k + 1) * 2048]
           .rearrange("p (c j) -> p c j", c=NCD) for k in range(2)]
    w3b = [flat[:, 15360 + k * 2048:15360 + (k + 1) * 2048]
           .rearrange("p (c j) -> p c j", c=NCD) for k in range(2)]
    w2b = [flat[:, 19456:25088].rearrange("p (i j) -> p i j", i=NIB),
           flat[:, 25088:30720].rearrange("p (i j) -> p i j", i=NIB),
           w2x]

    NGIB = NE7 * NIB
    M7 = 32 * NE7
    NW2 = 4 * NE7
    NST = 2 * NGIB
    A7H = 16 + NST

    with nc.Block("p7") as blk:
        @blk.gpsimd
        def _(g):
            for hx in range(2):
                if hx == 0:
                    g.wait_ge(p6m, 4 + 64)
                else:
                    g.wait_ge(p7a, NST)

                def w2dma(gg):
                    e2, j2 = gg // 4, gg % 4
                    if gg >= 3:
                        g.wait_ge(p7b, M7 * hx + 8 * (gg - 2))
                    elif hx == 1 and gg == 0:
                        g.wait_ge(p7b, M7)
                    g.dma_start(out=w2b[gg % 3],
                                in_=w2h[e2, :, :, j2 * 512:(j2 + 1) * 512]
                                .rearrange("i p j -> p i j")).then_inc(s_w2, 16)

                def w13dma(gib):
                    e2, ib2 = gib // NIB, gib % NIB
                    if gib >= 2:
                        g.wait_ge(p7a, NST * hx + 2 * (gib - 1))
                    elif hx == 1 and gib == 0:
                        g.wait_ge(p7a, NST)
                    g.dma_start(out=w1b[gib % 2][:], in_=w1h[e2, ib2]).then_inc(s_w13, 16)
                    g.dma_start(out=w3b[gib % 2][:], in_=w3h[e2, ib2]).then_inc(s_w13, 16)

                for gg in range(3):
                    w2dma(gg)
                cur = 0
                for gib in range(min(6, NGIB)):
                    w13dma(gib)
                    cur = gib + 1
                for e in range(NE7):
                    hi = min(e * NIB + 12, NGIB - 1)
                    while cur <= hi:
                        w13dma(cur)
                        cur += 1
                    for gg in range(4 * e + 3, min(4 * e + 7, NW2)):
                        w2dma(gg)
                while cur < NGIB:
                    w13dma(cur)
                    cur += 1

        @blk.sync
        def _(sy):
            for hx, hf in enumerate([0, 1]):
                for tc in range(8):
                    sy.wait_ge(v7y, M7 * hx + 32 * (NE7 - 1) + 24 + tc + 1)
                    sy.dma_start(out=Yv[hf * 8 + tc], in_=oa[:, tc, :]).then_inc(s_y, 16)
            sy.wait_ge(s_y, 16 * 16)

        @blk.tensor
        def _(t):
            for hx, hf in enumerate([0, 1]):
                for tc in range(8):
                    if hx == 0 and tc == 0:
                        t.wait_ge(a6e, 4 + p6t_h2T[15])
                        t.wait_ge(s_gw, 16)
                    if hx == 1 and tc == 0:
                        t.wait_ge(s_spl, 16)
                    if tc >= 2:
                        t.wait_ge(v7g, 16 * hx + 2 * (tc - 2) + 1)
                    for cc in range(NCD):
                        last = t.matmul(ps[4 + tc % 2][:, 0:15],
                                        h2T[:, cc, tc * P:(tc + 1) * P],
                                        gwt_sb[:, cc, :],
                                        start=(cc == 0), stop=(cc == NCD - 1))
                    last.then_inc(p7g, 1)
                for e in range(NE7):
                    for ib in range(NIB):
                        gib = e * NIB + ib
                        for u in range(2):
                            st = 2 * gib + u
                            ab, bb = ps[st % 2], ps[2 + st % 2]
                            if st >= 2:
                                t.wait_ge(v7u, NST * hx + st - 1)
                            t.wait_ge(s_w13, 32 * NGIB * hx + 32 * (gib + 1))
                            for cc in range(NCD):
                                t.matmul(ab[:], w1b[gib % 2][:, cc, :],
                                         h2T[:, cc, u * 512:(u + 1) * 512],
                                         start=(cc == 0), stop=(cc == NCD - 1))
                            for cc in range(NCD):
                                last = t.matmul(bb[:], w3b[gib % 2][:, cc, :],
                                                h2T[:, cc, u * 512:(u + 1) * 512],
                                                start=(cc == 0), stop=(cc == NCD - 1))
                            last.then_inc(p7a, 1)
                    t.wait_ge(v7u, NST * hx + 22 * (e + 1))
                    for j in range(4):
                        gj = 4 * e + j
                        t.wait_ge(s_w2, 16 * (NW2 * hx + gj + 1))
                        for tc in range(8):
                            cj = 32 * e + 8 * j + tc
                            yb = ps[4 + cj % 2]
                            if cj >= 2:
                                t.wait_ge(v7y, M7 * hx + cj - 1)
                            for ic in range(NIB):
                                last = t.matmul(yb[:], u_v[:, ic, tc * P:(tc + 1) * P],
                                                w2b[gj % 3][:, ic, :],
                                                start=(ic == 0), stop=(ic == NIB - 1))
                            last.then_inc(p7b, 1)

        @blk.scalar
        def _(s):
            for hx in range(2):
                if hx == 1:
                    s.wait_ge(p7a, NST)
                    s.wait_ge(s_y, 16 * 8)
                    s.dma_start(out=h2T[:], in_=h2sp[:, :, :]).then_inc(s_spl, 16)
                    for tc in range(8):
                        s.dma_start(out=oa[:, tc, :], in_=x1sp[tc]).then_inc(s_spl, 16)
                for tc in range(8):
                    s.wait_ge(v7g, 16 * hx + 2 * tc + 1)
                    s.activation(gp[:, 0:15], gl[:, 0:15], AF.Exp, bias=gc[:, 1:2],
                                 accum_out=gc[:, 2:3]).then_inc(a7s, 1)
                    s.wait_ge(v7g, 16 * hx + 2 * tc + 2)
                    s.activation(gp[:, 0:15], gp[:, 0:15], AF.Copy,
                                 scale=gc[:, 3:4]).then_inc(a7s, 1)
                for gib in range(NGIB):
                    for u in range(2):
                        st = 2 * gib + u
                        s.wait_ge(p7a, NST * hx + st + 1)
                        if st >= 2:
                            s.wait_ge(v7u, NST * hx + st - 1)
                        s.activation(sa[st % 2][:], ps[st % 2][:], AF.Silu
                                     ).then_inc(a7s, 1)

        @blk.vector
        def _(v):
            for hx in range(2):
                v.memset(comb[:, :, 15:16], 1.0)
                for tc in range(8):
                    v.wait_ge(p7g, 8 * hx + tc + 1)
                    if tc >= 1:
                        v.wait_ge(a7s, A7H * hx + 2 * (tc - 1) + 1)
                    v.tensor_scalar(gl[:, 0:15], ps[4 + tc % 2][:, 0:15], 1.0, None, ALU.mult)
                    v.reduce_max(gc[:, 1:2], gl[:, 0:15], AX.X,
                                 negate=True).then_inc(v7g, 1)
                    v.wait_ge(a7s, A7H * hx + 2 * tc + 1)
                    v.reciprocal(gc[:, 3:4], gc[:, 2:3]).then_inc(v7g, 1)
                    v.wait_ge(a7s, A7H * hx + 2 * tc + 2)
                    v.reduce_max(gc[:, 4:5], gp[:, 0:15], AX.X)
                    v.tensor_scalar_mul(gc[:, 5:6], gc[:, 4:5], 1.0)
                    v.scalar_tensor_tensor(t2b[:, 0:15], gp[:, 0:15], gc[:, 4:5],
                                           gp[:, 0:15], ALU.is_lt, ALU.mult)
                    v.tensor_scalar_mul(gc[:, 5:6], gc[:, 4:5], 1.0)
                    v.reduce_max(gc[:, 6:7], t2b[:, 0:15], AX.X)
                    v.tensor_scalar_mul(gc[:, 5:6], gc[:, 6:7], 1.0)
                    v.scalar_tensor_tensor(comb[:, tc, 0:15], gp[:, 0:15], gc[:, 6:7],
                                           gp[:, 0:15], ALU.is_ge, ALU.mult)
                for gib in range(NGIB):
                    ib = gib % NIB
                    for u in range(2):
                        st = 2 * gib + u
                        v.wait_ge(a7s, A7H * hx + 16 + st + 1)
                        if gib == 0 and u == 0 and hx == 1:
                            v.wait_ge(p7b, M7)
                        v.tensor_tensor(u_v[:, ib, u * 512:(u + 1) * 512],
                                        sa[st % 2][:], ps[2 + st % 2][:], ALU.mult
                                        ).then_inc(v7u, 1)
                    if ib == NIB - 1:
                        e = gib // NIB
                        for j in range(4):
                            for tc in range(8):
                                cj = 32 * e + 8 * j + tc
                                v.wait_ge(p7b, M7 * hx + cj + 1)
                                if e == 0 and j == 0 and tc == 0:
                                    v.wait_ge(v7g, 16 * hx + 16)
                                    if hx == 1:
                                        v.wait_ge(s_spl, 16 * 9)
                                v.scalar_tensor_tensor(oa[:, tc, j * 512:(j + 1) * 512],
                                                       ps[4 + cj % 2][:],
                                                       comb[:, tc, e:e + 1],
                                                       oa[:, tc, j * 512:(j + 1) * 512],
                                                       ALU.mult, ALU.add).then_inc(v7y, 1)


def emit_dbg(nc, dbg, items):
    with nc.semaphore("sdbg") as sd, nc.Block("dbg") as blk:
        @blk.sync
        def _(sy):
            n = 0
            for name, src, semh, need in items:
                if name not in dbg:
                    continue
                sy.wait_ge(semh, need)
                if len(src.shape) == 3:
                    sy.dma_start(out=dbg[name][:, :, :], in_=src[:, :, :]).then_inc(sd, 16)
                else:
                    sy.dma_start(out=dbg[name][:, :, :, :], in_=src[:, :, :, :]).then_inc(sd, 16)
                n += 1
            sy.wait_ge(sd, 16 * n)


def kernel(**inp):
    X = np.asarray(inp["L0_attn_X"], dtype=np.float32)
    assert X.shape == (4, S, D)

    w1 = np.asarray(inp["L0_ffn_w1"], np.float32)
    w3 = np.asarray(inp["L0_ffn_w3"], np.float32)
    w2 = np.asarray(inp["L0_ffn_w2"], np.float32)
    w1a = np.concatenate([w1, np.asarray(inp["L0_ffn_sw1"], np.float32)[None]], axis=0)
    w3a = np.concatenate([w3, np.asarray(inp["L0_ffn_sw3"], np.float32)[None]], axis=0)
    w2a = np.concatenate([w2, np.asarray(inp["L0_ffn_sw2"], np.float32)[None]], axis=0)
    w1s = np.ascontiguousarray(w1a.reshape(NE, NIB, P, NCD, P).transpose(0, 1, 4, 3, 2))
    w3s = np.ascontiguousarray(w3a.reshape(NE, NIB, P, NCD, P).transpose(0, 1, 4, 3, 2))
    w2s = np.ascontiguousarray(w2a.reshape(NE, D, NIB, P).transpose(0, 2, 3, 1))
    gwt = np.ascontiguousarray(np.asarray(inp["L0_ffn_gate_w"], np.float32).T)

    tri = (np.arange(P)[None, :] > np.arange(P)[:, None])
    common = dict(
        wq=np.ascontiguousarray(np.asarray(inp["L0_attn_Wq"], np.float32)),
        wk=np.ascontiguousarray(np.asarray(inp["L0_attn_Wk"], np.float32)),
        wv=np.ascontiguousarray(np.asarray(inp["L0_attn_Wv"], np.float32)),
        wo=np.ascontiguousarray(np.asarray(inp["L0_attn_Wo"], np.float32)),
        bq=np.ascontiguousarray(np.asarray(inp["L0_attn_bq"], np.float32)),
        bk=np.ascontiguousarray(np.asarray(inp["L0_attn_bk"], np.float32)),
        bv=np.ascontiguousarray(np.asarray(inp["L0_attn_bv"], np.float32)),
        bo=np.ascontiguousarray(np.asarray(inp["L0_attn_bo"], np.float32)),
        lng=np.ascontiguousarray(np.asarray(inp["L0_attn_ln_g"], np.float32)),
        lnb=np.ascontiguousarray(np.asarray(inp["L0_attn_ln_b"], np.float32)),
        gwt=gwt, w1s=w1s, w3s=w3s, w2s=w2s)

    in_maps = []
    for cix in range(8):
        sq, par = cix // 2, cix % 2
        dm = np.zeros((P, 256), np.float32)
        if par == 1:
            dm[:, P:][tri] = NEG
        else:
            dm[:, :P][tri] = NEG
            dm[:, P:] = NEG
        m = dict(common)
        m["xall"] = np.ascontiguousarray(X[sq])
        m["xown"] = np.ascontiguousarray(X[sq].reshape(NTC, P, D)[par::2].reshape(TOK, D))
        m["dmask"] = dm
        in_maps.append(m)

    nc, dbg = build_program()
    res = run_bass_kernel_spmd(nc, in_maps[:CORES], core_ids=list(range(CORES)))

    if DBG or CORES < 8:
        return res

    Yf = np.empty((4, S, D), np.float32)
    for cix in range(8):
        sq, par = cix // 2, cix % 2
        Yf[sq].reshape(NTC, P, D)[par::2] = res.results[cix]["y"].reshape(NQC, P, D)
    return Yf
